# version 02
# baseline (speedup 1.0000x reference)
import math
from contextlib import ExitStack

import numpy as np
import ml_dtypes

import concourse.bass as bass
import concourse.mybir as mybir
from concourse.bass_utils import run_bass_kernel_spmd

F32 = mybir.dt.float32
BF16 = mybir.dt.bfloat16
AF = mybir.ActivationFunctionType
ALU = mybir.AluOpType
AX = mybir.AxisListType
BF = ml_dtypes.bfloat16

NT = 2048
D = 1024
NTT = NT // 128


def _host_consts():
    c = {}
    c["idf32"] = np.eye(128, dtype=np.float32)
    c["ones1x128"] = np.ones((1, 128), dtype=np.float32)
    c["sel16"] = np.kron(np.eye(16, dtype=np.float32), np.ones((1, 128), np.float32))

    H0, CH = 4, 128
    gam = 1.0 - np.exp2(-5.0 - np.arange(H0, dtype=np.float64))
    i = np.arange(CH, dtype=np.float64)
    rel = i[:, None] - i[None, :]
    Dm = np.where(rel[None] >= 0, gam[:, None, None] ** rel[None], 0.0) / 16.0
    c["l0_DT"] = np.ascontiguousarray(np.transpose(Dm, (0, 2, 1))).astype(np.float32)
    cq = (gam[:, None] ** (i[None, :] + 1) / 16.0).astype(np.float32)
    c["l0_crossq_bc"] = np.ascontiguousarray(np.broadcast_to(cq[None], (128, H0, CH)))
    c["l0_crossk"] = np.ascontiguousarray(
        (gam[:, None] ** (CH - 1 - i[None, :])).astype(np.float32).T)
    d0 = 256
    inv = 1.0 / (10000.0 ** (np.arange(0, d0, 2, dtype=np.float64) / d0))
    t = np.arange(NT, dtype=np.float64)
    fr = np.outer(inv, t)
    c["l0_cosT"] = np.cos(fr).astype(np.float32)
    c["l0_sinT"] = np.sin(fr).astype(np.float32)
    return c


DRAM_ACTS = {
    "xn0b": ((NT, D), BF16),
    "s0":   ((NT, D), F32),
    "f0":   ((NT, D), F32),
    "x1":   ((NT, D), F32),
    "xn1f": ((NT, D), F32),
    "xn1b": ((NT + 1, D), BF16),
    "s1":   ((NT, D), F32),
    "l1q":  ((NT, 2048), BF16),
    "l1k":  ((NT, 1024), BF16),
    "ymid": ((NT, D), F32),
    "y1b_0": ((NT + 1, D), BF16),
    "h2_0": ((NT, D), BF16),
    "aq_0": ((NT, 1024), BF16),
    "ak_0": ((NT, 1024), BF16),
    "ya_0": ((NT + 1, D), BF16),
    "y1f_0": ((NT, D), F32),
    "h2_1": ((NT, D), BF16),
    "aq_1": ((NT, 1024), BF16),
    "ak_1": ((NT, 1024), BF16),
    "ya_1": ((NT + 1, D), BF16),
    "y1f_1": ((NT, D), F32),
    "f1":   ((NT, D), F32),
    "yout": ((NT, D), F32),
}

STAGE_WEIGHTS = {
    "ln0": [],
    "att0": ["L0_attn_Wq", "L0_attn_Wk", "L0_attn_Wv", "L0_attn_Wg", "L0_attn_Wo",
             "l0_gn_pa", "l0_DT", "l0_crossq_bc", "l0_crossk",
             "l0_cosT", "l0_sinT", "idf32", "ones1x128"],
    "ffn0": ["L0_ffn_router_w", "L0_ffn_wg_r", "L0_ffn_wu_r", "L0_ffn_wd_r",
             "L0_ffn_wg_s", "L0_ffn_wu_s", "L0_ffn_wd_s", "idf32", "sel16"],
    "comb0": [],
    "att1": ["L1_attn_Wq", "L1_attn_Wk", "L1_attn_Wv", "l1_wo_pad",
             "L1_attn_lambda_q1", "L1_attn_lambda_k1", "L1_attn_lambda_q2",
             "L1_attn_lambda_k2", "l1_rmsw_col", "l1_cosq", "l1_sinq",
             "l1_cosk", "l1_sink", "ones1x128"],
    "ffn1a": ["L1_ffn_wqkv", "L1_ffn_q_scale", "L1_ffn_k_scale", "l1f_wo_pad_0",
              "L1_ffn_ln_g", "L1_ffn_ln_b", "w1p_0", "b1c_0", "w2p_0",
              "L1_ffn_b2", "l1f_mcols", "l1f_base2", "l1f_cosq", "l1f_sinq",
              "l1f_cosk", "l1f_sink", "idf32", "ones1x128", "l1f_cmask",
              "l1f_qs_til_0", "l1f_ks_til_0"],
    "ffn1b": ["L1_ffn_wqkv", "L1_ffn_q_scale", "L1_ffn_k_scale", "l1f_wo_pad_1",
              "L1_ffn_ln_g", "L1_ffn_ln_b", "w1p_1", "b1c_1", "w2p_1",
              "L1_ffn_b2", "l1f_mcols", "l1f_base2", "l1f_cosq", "l1f_sinq",
              "l1f_cosk", "l1f_sink", "idf32", "ones1x128", "l1f_cmask",
              "l1f_qs_til_1", "l1f_ks_til_1"],
    "comb1": [],
}
STAGE_INS = {
    "ln0": ["x"],
    "att0": ["xn0b"],
    "ffn0": ["xn0b"],
    "comb0": ["x", "s0", "f0"],
    "att1": ["xn1b"],
    "ffn1a": ["xn1f", "xn1b"],
    "ffn1b": ["ymid", "y1b_0"],
    "comb1": ["x1", "s1", "f1"],
}
STAGE_OUTS = {
    "ln0": ["xn0b"],
    "att0": ["s0"],
    "ffn0": ["f0"],
    "comb0": ["x1", "xn1f", "xn1b"],
    "att1": ["s1"],
    "ffn1a": ["ymid", "y1b_0"],
    "ffn1b": ["f1"],
    "comb1": ["yout"],
}
STAGE_INTERNAL = {
    "att1": ["l1q", "l1k"],
    "ffn1a": ["h2_0", "aq_0", "ak_0", "ya_0", "y1f_0"],
    "ffn1b": ["h2_1", "aq_1", "ak_1", "ya_1", "y1f_1"],
}


class Ctr:
    __slots__ = ("sem", "n")
    def __init__(self, sem):
        self.sem = sem
        self.n = 0
    def bump(self, k=1):
        self.n += k
        return self.n


ENGS = ("tensor", "vector", "scalar", "gpsimd", "sync")


class Em:
    def __init__(self, P):
        self.P = P
        self.qs = {e: [] for e in ENGS}

    def _push(self, eng, fn):
        self.qs[eng].append(fn)

    def t(self, fn): self._push("tensor", fn)
    def v(self, fn): self._push("vector", fn)
    def s(self, fn): self._push("scalar", fn)
    def g(self, fn): self._push("gpsimd", fn)
    def y(self, fn): self._push("sync", fn)

    def wait(self, eng, ctr, val=None):
        v = ctr.n if val is None else val
        if v > 0:
            self._push(eng, lambda e, s=ctr.sem, v=v: e.wait_ge(s, v))

    def dma(self, eng, out, in_, ctr, **kw):
        ctr.bump(16)
        self._push(eng, lambda e, o=out, i=in_, kw=kw, s=ctr.sem:
                   e.dma_start(out=o, in_=i, **kw).then_inc(s, 16))

    def dmaT(self, eng, out, in_, ctr):
        ctr.bump(16)
        self._push(eng, lambda e, o=out, i=in_, s=ctr.sem:
                   e.dma_start_transpose(o, i).then_inc(s, 16))

    def flush(self):
        for name, q in self.qs.items():
            if q:
                getattr(self.P.block, name)(lambda eng, q=tuple(q): [f(eng) for f in q])
        self.qs = {e: [] for e in ENGS}

    def _selfsync(self, eng, ctr):
        n = ctr.bump(1)
        self._push(eng, lambda e, s=ctr.sem: e.engine_nop().then_inc(s, 1))
        self._push(eng, lambda e, s=ctr.sem, n=n: e.wait_ge(s, n))

    def vbar(self): self._selfsync("vector", self.P.vself)
    def sbar(self): self._selfsync("scalar", self.P.sself)
    def gbar(self): self._selfsync("gpsimd", self.P.gself)


def inc(fn, ctr, k=1):
    ctr.bump(k)
    return lambda e, fn=fn, s=ctr.sem, n=k: fn(e).then_inc(s, n)


def barrier(P, em):
    P.bar_k += 1
    tgt = 5 * P.bar_k
    for eng in ENGS:
        em._push(eng, lambda e, s=P.bar.sem: e.sem_inc(s, 1))
        em._push(eng, lambda e, s=P.bar.sem, t=tgt: e.wait_ge(s, t))
    em.flush()


class Prog:
    def __init__(self, nc, block, stack, aps):
        self.nc = nc
        self.block = block
        self.stack = stack
        self.aps = aps
        self.bar_k = 0
        self.bar = self.ctr("bar")
        self.vself = self.ctr("vself")
        self.sself = self.ctr("sself")
        self.gself = self.ctr("gself")

    def ctr(self, name):
        sem = self.stack.enter_context(self.nc.semaphore(f"c{getattr(self, '_ctr_i', 0)}_{name}"))
        self._ctr_i = getattr(self, "_ctr_i", 0) + 1
        return Ctr(sem)

    def sb(self, name, shape, dtype):
        st = getattr(self, "stage_stack", None) or self.stack
        return st.enter_context(self.nc.sbuf_tensor(name, shape, dtype))

    def ps(self, name, shape, dtype=F32):
        st = getattr(self, "stage_stack", None) or self.stack
        return st.enter_context(self.nc.psum_tensor(name, shape, dtype))


def emit_ln(P, src_name, outb_name, outf_name=None, extra_row0=False, src_sbuf=None,
            tag="ln"):
    nc = P.nc
    em = Em(P)
    x = P.aps[src_name] if src_name else None
    ob = P.aps[outb_name]
    of = P.aps[outf_name] if outf_name else None

    xt = P.sb(f"{tag}_x", [128, 3, D], F32)
    trash = P.sb(f"{tag}_tr", [128, D], BF16)
    cols = P.sb(f"{tag}_c", [128, 3, 8], F32)
    sqc = P.sb(f"{tag}_sq", [128, 3, 4], F32)
    sqbuf = P.sb(f"{tag}_sqb", [128, D], F32)
    onb = P.sb(f"{tag}_ob", [128, 3, D], BF16)
    onf = P.sb(f"{tag}_of", [128, 3, D], F32) if of is not None else None
    zrow = P.sb(f"{tag}_z", [1, D], BF16)

    ld, vd, sd, st = P.ctr("ld"), P.ctr("vd"), P.ctr("sd"), P.ctr("st")
    row0 = 1 if extra_row0 else 0
    if extra_row0:
        em.v(lambda e: e.memset(zrow[:, :], 0.0))
        em.v(inc(lambda e: e.engine_nop(), vd))
        em.wait("sync", vd, vd.n)
        em.dma("sync", ob[0:1, :], zrow[0:1, :], st)

    snaps = []
    for tt in range(NTT):
        b = tt % 3
        if tt >= 3:
            sn = snaps[tt - 3]
            em.wait("sync", vd, sn["vd"])
            em.wait("sync", sd, sn["sd"])
            em.wait("vector", st, sn["st"])
            em.wait("scalar", vd, sn["vd"])
        em.dma("sync", xt[:, b, :], x[tt * 128:(tt + 1) * 128, :], ld)
        em.wait("vector", ld, 16 * (tt + 1))
        em.v(inc(lambda e, b=b: e.reduce_sum(cols[:, b, 0:1], xt[:, b, :], axis=AX.X), vd))
        em.v(lambda e, b=b: e.tensor_tensor(sqbuf[:, :], xt[:, b, :], xt[:, b, :], ALU.mult))
        em.vbar()
        em.v(inc(lambda e, b=b: e.reduce_sum(sqc[:, b, 0:1], sqbuf[:, :], axis=AX.X), vd))
        em.vbar()
        em.v(lambda e, b=b: e.tensor_scalar(cols[:, b, 3:4], sqc[:, b, 0:1], 1.0 / D, None, ALU.mult))
        em.v(lambda e, b=b: e.tensor_scalar(cols[:, b, 2:3], cols[:, b, 0:1], 1.0 / D, None, ALU.mult))
        em.vbar()
        em.v(lambda e, b=b: e.tensor_tensor(cols[:, b, 4:5], cols[:, b, 2:3], cols[:, b, 2:3], ALU.mult))
        em.vbar()
        em.v(inc(lambda e, b=b: e.tensor_scalar(cols[:, b, 5:6], cols[:, b, 3:4], cols[:, b, 4:5],
                                                1e-5, ALU.subtract, ALU.add), vd))
        em.wait("scalar", vd, vd.n)
        em.s(inc(lambda e, b=b: e.activation(out=cols[:, b, 6:7], in_=cols[:, b, 5:6], func=AF.Sqrt), sd))
        em.wait("vector", sd, sd.n)
        em.v(lambda e, b=b: e.reciprocal(cols[:, b, 7:8], cols[:, b, 6:7]))
        em.vbar()
        em.v(lambda e, b=b: e.tensor_scalar(onb[:, b, :], xt[:, b, :], cols[:, b, 2:3],
                                            cols[:, b, 7:8], ALU.subtract, ALU.mult))
        if onf is not None:
            em.v(lambda e, b=b: e.tensor_scalar(onf[:, b, :], xt[:, b, :], cols[:, b, 2:3],
                                                cols[:, b, 7:8], ALU.subtract, ALU.mult))
        em.v(inc(lambda e: e.engine_nop(), vd))
        em.wait("sync", vd, vd.n)
        em.dma("sync", ob[row0 + tt * 128: row0 + (tt + 1) * 128, :], onb[:, b, :], st)
        if of is not None:
            em.dma("sync", of[tt * 128:(tt + 1) * 128, :], onf[:, b, :], st)
        snaps.append({"vd": vd.n, "sd": sd.n, "st": st.n})
    em.wait("sync", st, st.n)
    em.flush()


def emit_ln0(P):
    emit_ln(P, "x", "xn0b", tag="ln0")


def emit_combine(P, x_name, s_name, f_name, out_name, lnb_name=None, lnf_name=None, tag="cmb"):
    nc = P.nc
    em = Em(P)
    xA, sA, fA = P.aps[x_name], P.aps[s_name], P.aps[f_name]
    oA = P.aps[out_name]
    lnbA = P.aps[lnb_name] if lnb_name else None
    lnfA = P.aps[lnf_name] if lnf_name else None

    NB = 3
    xt = P.sb(f"{tag}_x", [128, NB, D], F32)
    st_ = P.sb(f"{tag}_s", [128, NB, D], F32)
    ft = P.sb(f"{tag}_f", [128, NB, D], F32)
    gt = P.sb(f"{tag}_g", [128, NB, D], F32)
    x1 = P.sb(f"{tag}_o", [128, NB, D], F32)
    cols = P.sb(f"{tag}_c", [128, NB, 8], F32)
    sqc = P.sb(f"{tag}_sq", [128, NB, 4], F32)
    sqbuf = P.sb(f"{tag}_sqb", [128, D], F32)
    onb = P.sb(f"{tag}_ob", [128, NB, D], BF16) if lnbA is not None else None
    onf = P.sb(f"{tag}_of", [128, NB, D], F32) if lnfA is not None else None
    trash = P.sb(f"{tag}_tr", [128, D], BF16)
    zrow = P.sb(f"{tag}_z", [1, D], BF16)

    ld, vd, sd, st = P.ctr("ld"), P.ctr("vd"), P.ctr("sd"), P.ctr("st")
    if lnbA is not None:
        em.v(lambda e: e.memset(zrow[:, :], 0.0))
        em.v(inc(lambda e: e.engine_nop(), vd))
        em.wait("sync", vd, vd.n)
        em.dma("sync", lnbA[0:1, :], zrow[0:1, :], st)

    snaps = []
    for tt in range(NTT):
        b = tt % NB
        t0 = tt * 128
        if tt >= NB:
            sn = snaps[tt - NB]
            em.wait("sync", vd, sn["vd"])
            em.wait("sync", sd, sn["sd"])
            em.wait("vector", st, sn["st"])
            em.wait("scalar", vd, sn["vd"])
        em.dma("sync", xt[:, b, :], xA[t0:t0 + 128, :], ld)
        em.dma("sync", st_[:, b, :], sA[t0:t0 + 128, :], ld)
        em.dma("sync", ft[:, b, :], fA[t0:t0 + 128, :], ld)
        em.wait("scalar", ld, ld.n)
        em.s(inc(lambda e, b=b: e.activation(out=gt[:, b, :], in_=ft[:, b, :], func=AF.Silu), sd))
        em.wait("vector", sd, sd.n)
        em.wait("vector", ld, ld.n)
        em.v(lambda e, b=b: e.tensor_tensor(gt[:, b, :], gt[:, b, :], st_[:, b, :], ALU.mult))
        em.vbar()
        em.v(inc(lambda e, b=b: e.tensor_tensor(x1[:, b, :], xt[:, b, :], gt[:, b, :], ALU.add), vd))
        em.vbar()
        em.wait("sync", vd, vd.n)
        em.dma("sync", oA[t0:t0 + 128, :], x1[:, b, :], st)
        if lnbA is not None:
            em.v(inc(lambda e, b=b: e.reduce_sum(cols[:, b, 0:1], x1[:, b, :], axis=AX.X), vd))
            em.v(lambda e, b=b: e.tensor_tensor(sqbuf[:, :], x1[:, b, :], x1[:, b, :], ALU.mult))
            em.vbar()
            em.v(inc(lambda e, b=b: e.reduce_sum(sqc[:, b, 0:1], sqbuf[:, :], axis=AX.X), vd))
            em.vbar()
            em.v(lambda e, b=b: e.tensor_scalar(cols[:, b, 3:4], sqc[:, b, 0:1], 1.0 / D, None, ALU.mult))
            em.v(lambda e, b=b: e.tensor_scalar(cols[:, b, 2:3], cols[:, b, 0:1], 1.0 / D, None, ALU.mult))
            em.vbar()
            em.v(lambda e, b=b: e.tensor_tensor(cols[:, b, 4:5], cols[:, b, 2:3], cols[:, b, 2:3], ALU.mult))
            em.vbar()
            em.v(inc(lambda e, b=b: e.tensor_scalar(cols[:, b, 5:6], cols[:, b, 3:4], cols[:, b, 4:5],
                                                    1e-5, ALU.subtract, ALU.add), vd))
            em.wait("scalar", vd, vd.n)
            em.s(inc(lambda e, b=b: e.activation(out=cols[:, b, 6:7], in_=cols[:, b, 5:6], func=AF.Sqrt), sd))
            em.wait("vector", sd, sd.n)
            em.v(lambda e, b=b: e.reciprocal(cols[:, b, 7:8], cols[:, b, 6:7]))
            em.vbar()
            em.v(lambda e, b=b: e.tensor_scalar(onb[:, b, :], x1[:, b, :], cols[:, b, 2:3],
                                                cols[:, b, 7:8], ALU.subtract, ALU.mult))
            if onf is not None:
                em.v(lambda e, b=b: e.tensor_scalar(onf[:, b, :], x1[:, b, :], cols[:, b, 2:3],
                                                    cols[:, b, 7:8], ALU.subtract, ALU.mult))
            em.v(inc(lambda e: e.engine_nop(), vd))
            em.wait("sync", vd, vd.n)
            em.dma("sync", lnbA[1 + t0: 1 + t0 + 128, :], onb[:, b, :], st)
            if onf is not None:
                em.dma("sync", lnfA[t0:t0 + 128, :], onf[:, b, :], st)
        snaps.append({"vd": vd.n, "sd": sd.n, "st": st.n})
    em.wait("sync", st, st.n)
    em.flush()


def emit_comb0(P):
    emit_combine(P, "x", "s0", "f0", "x1", lnb_name="xn1b", lnf_name="xn1f", tag="c0")


def emit_comb1(P):
    emit_combine(P, "x1", "s1", "f1", "yout", tag="c1")


def emit_ffn0(P):
    nc = P.nc
    em = Em(P)
    xn0b = P.aps["xn0b"]
    NE, MR, MS = 16, 256, 512
    NQ, QT = 4, 512
    QTT = QT // 128

    xn0T = P.sb("f0_xnT", [128, 8, NT], BF16)
    rw = P.sb("f0_rw", [128, 8, 16], BF16)
    idf = P.sb("f0_id", [128, 128], F32)
    sel = P.sb("f0_sel", [16, NE * 128], BF16)
    combT = P.sb("f0_combT", [16, NT], BF16)
    ecol = P.sb("f0_e", [128, 16], F32)
    emask = P.sb("f0_em", [128, 16], F32)
    rcols = P.sb("f0_rc", [128, 8], F32)
    comb = P.sb("f0_comb", [128, 16], F32)

    wds = P.sb("f0_wds", [128, 8, D], BF16)
    wdr = P.sb("f0_wdr", [128, 32, D], BF16)
    wgq = [P.sb(f"f0_wg{i}", [128, 8, MS], BF16) for i in range(2)]
    wuq = [P.sb(f"f0_wu{i}", [128, 8, MS], BF16) for i in range(2)]
    hT = P.sb("f0_hT", [128, 32, QT], BF16)
    hsT = P.sb("f0_hsT", [128, 8, QT], BF16)
    gtmp = P.sb("f0_gt", [128, 2, 512], BF16)
    otmp = P.sb("f0_ot", [128, 2, 512], F32)

    ld, tr, mm, vd, sd, st, gld = (P.ctr("ld"), P.ctr("tr"), P.ctr("mm"), P.ctr("vd"),
                                   P.ctr("sd"), P.ctr("st"), P.ctr("gl"))

    ps_l = P.ps("f0_psl", [128, 16])
    ps_t = P.ps("f0_pst", [16, 128])
    ps_b = P.ps("f0_psb", [128, 512])
    ps_up = [P.ps(f"f0_up{i}", [128, 512]) for i in range(2)]
    ps_dn = [P.ps(f"f0_dn{i}", [128, 512]) for i in range(2)]

    for dt in range(8):
        em.dmaT("sync", xn0T[:, dt, :], xn0b[:, dt * 128:(dt + 1) * 128], tr)
    em.dma("gpsimd", rw[:, :, :], P.aps["L0_ffn_router_w"].rearrange("(a p) e -> p a e", p=128), gld)
    em.dma("gpsimd", idf[:, :], P.aps["idf32"][:, :], gld)
    em.dma("gpsimd", sel[:, :], P.aps["sel16"][:, :], gld)
    em.dma("gpsimd", wds[:, :, :], P.aps["L0_ffn_wd_s"].rearrange("n (a p) d -> p (n a) d", p=128), gld)
    em.dma("gpsimd", wdr[:, :, :], P.aps["L0_ffn_wd_r"].rearrange("n (a p) d -> p (n a) d", p=128), gld)

    em.wait("tensor", tr, tr.n)
    em.wait("tensor", gld, gld.n)
    for tt in range(NTT):
        t0 = tt * 128
        em.wait("tensor", vd, vd.n)
        em.wait("tensor", sd, sd.n)
        def rmm(e, t0=t0):
            for kt in range(8):
                r = e.matmul(ps_l[:, :], xn0T[:, kt, t0:t0 + 128], rw[:, kt, :],
                             start=(kt == 0), stop=(kt == 7))
            return r
        em.t(inc(rmm, mm))
        em.wait("vector", mm, mm.n)
        em.v(inc(lambda e: e.reduce_max(rcols[:, 0:1], ps_l[:, :], axis=AX.X, negate=True), vd))
        em.wait("scalar", vd, vd.n)
        em.s(inc(lambda e: e.activation(out=ecol[:, :], in_=ps_l[:, :], func=AF.Exp,
                                        bias=rcols[:, 0:1], accum_out=rcols[:, 1:2]), sd))
        em.wait("vector", sd, sd.n)
        em.v(lambda e: e.reduce_max(rcols[:, 2:3], ecol[:, :], axis=AX.X))
        em.vbar()
        em.v(lambda e: e.tensor_scalar(emask[:, :], ecol[:, :], rcols[:, 2:3], None, ALU.is_lt))
        em.vbar()
        em.v(lambda e: e.tensor_tensor(emask[:, :], ecol[:, :], emask[:, :], ALU.mult))
        em.vbar()
        em.v(lambda e: e.reduce_max(rcols[:, 3:4], emask[:, :], axis=AX.X))
        em.v(lambda e: e.tensor_scalar(rcols[:, 4:5], rcols[:, 1:2], 1e-9, None, ALU.mult))
        em.vbar()
        em.v(lambda e: e.tensor_tensor(rcols[:, 4:5], rcols[:, 4:5], rcols[:, 2:3], ALU.add))
        em.vbar()
        em.v(lambda e: e.tensor_tensor(rcols[:, 4:5], rcols[:, 4:5], rcols[:, 3:4], ALU.add))
        em.vbar()
        em.v(lambda e: e.reciprocal(rcols[:, 5:6], rcols[:, 4:5]))
        em.v(lambda e: e.tensor_scalar(comb[:, :], ecol[:, :], rcols[:, 3:4], None, ALU.is_ge))
        em.vbar()
        em.v(lambda e: e.tensor_tensor(comb[:, :], ecol[:, :], comb[:, :], ALU.mult))
        em.vbar()
        em.v(inc(lambda e: e.tensor_scalar(comb[:, :], comb[:, :], rcols[:, 5:6], None, ALU.mult), vd))
        em.wait("tensor", vd, vd.n)
        em.t(inc(lambda e: e.transpose(ps_t[:, :], comb[:, :], idf[:, :]), mm))
        em.wait("vector", mm, mm.n)
        em.v(inc(lambda e, t0=t0: e.tensor_copy(combT[:, t0:t0 + 128], ps_t[0:16, :]), vd))

    otmp_store = {0: 0, 1: 0}
    for q in range(NQ):
        q0 = q * QT
        end_mm_expert = {}
        for e_i in range(NE + 2):
            buf = e_i % 2
            is_sh = e_i >= NE
            if e_i >= 2:
                em.wait("gpsimd", mm, end_mm_expert[e_i - 2])
            elif q > 0:
                em.wait("gpsimd", mm, end_mm_expert_prev[e_i - 2 + NE + 2])
            if is_sh:
                n = e_i - NE
                em.dma("gpsimd", wgq[buf][:, :, :],
                       P.aps["L0_ffn_wg_s"][n].rearrange("(a p) m -> p a m", p=128), gld)
                em.dma("gpsimd", wuq[buf][:, :, :],
                       P.aps["L0_ffn_wu_s"][n].rearrange("(a p) m -> p a m", p=128), gld)
            else:
                em.dma("gpsimd", wgq[buf][:, :, 0:MR],
                       P.aps["L0_ffn_wg_r"][e_i].rearrange("(a p) m -> p a m", p=128), gld)
                em.dma("gpsimd", wuq[buf][:, :, 0:MR],
                       P.aps["L0_ffn_wu_r"][e_i].rearrange("(a p) m -> p a m", p=128), gld)
            nm = 4 if is_sh else 2
            em.wait("tensor", gld, gld.n)
            for ms in range(nm):
                pu = ps_up[ms % 2]
                war_v = vd.n
                em.wait("tensor", vd, war_v)
                def upg(e, buf=buf, ms=ms, q0=q0, pu=pu):
                    for kt in range(8):
                        r = e.matmul(pu[:, :], wgq[buf][:, kt, ms * 128:(ms + 1) * 128],
                                     xn0T[:, kt, q0:q0 + 512], start=(kt == 0), stop=(kt == 7))
                    return r
                em.t(inc(upg, mm))
                em.wait("scalar", mm, mm.n)
                em.wait("scalar", vd, war_v)
                em.s(inc(lambda e, pu=pu, ms=ms: e.activation(
                    out=gtmp[:, ms % 2, :], in_=pu[:, :], func=AF.Silu), sd))
                em.wait("tensor", sd, sd.n)
                def upu(e, buf=buf, ms=ms, q0=q0, pu=pu):
                    for kt in range(8):
                        r = e.matmul(pu[:, :], wuq[buf][:, kt, ms * 128:(ms + 1) * 128],
                                     xn0T[:, kt, q0:q0 + 512], start=(kt == 0), stop=(kt == 7))
                    return r
                em.t(inc(upu, mm))
                dst, di = (hsT, (e_i - NE) * 4 + ms) if is_sh else (hT, e_i * 2 + ms)
                em.wait("vector", mm, mm.n)
                em.v(inc(lambda e, pu=pu, ms=ms, dst=dst, di=di: e.tensor_tensor(
                    dst[:, di, :], gtmp[:, ms % 2, :], pu[:, :], ALU.mult), vd))
            if not is_sh:
                war_v = vd.n
                em.wait("tensor", vd, war_v)
                em.t(inc(lambda e, e_i=e_i, q0=q0: e.matmul(
                    ps_b[:, :], sel[:, e_i * 128:(e_i + 1) * 128], combT[:, q0:q0 + 512],
                    start=True, stop=True), mm))
                em.wait("vector", mm, mm.n)
                em.vbar()
                for ms in range(2):
                    em.v(inc(lambda e, e_i=e_i, ms=ms: e.tensor_tensor(
                        hT[:, e_i * 2 + ms, :], hT[:, e_i * 2 + ms, :], ps_b[:, :], ALU.mult), vd))
            end_mm_expert[e_i] = mm.n
        end_mm_expert_prev = dict(end_mm_expert)
        em.wait("tensor", vd, vd.n)
        for t in range(QTT):
            t0 = q0 + t * 128
            for dd in range(2):
                pd = ps_dn[(t * 2 + dd) % 2]
                em.wait("tensor", sd, sd.n)
                def dn(e, t=t, dd=dd, pd=pd):
                    first = True
                    for e_i in range(NE):
                        for ms in range(2):
                            e.matmul(pd[:, :], hT[:, e_i * 2 + ms, t * 128:(t + 1) * 128],
                                     wdr[:, e_i * 2 + ms, dd * 512:dd * 512 + 512],
                                     start=first, stop=False)
                            first = False
                    for n in range(2):
                        for ms in range(4):
                            last = (n == 1 and ms == 3)
                            r = e.matmul(pd[:, :], hsT[:, n * 4 + ms, t * 128:(t + 1) * 128],
                                         wds[:, n * 4 + ms, dd * 512:dd * 512 + 512],
                                         start=False, stop=last)
                    return r
                em.t(inc(dn, mm))
                em.wait("scalar", mm, mm.n)
                em.wait("scalar", st, otmp_store[dd])
                em.s(inc(lambda e, pd=pd, dd=dd: e.activation(
                    out=otmp[:, dd, :], in_=pd[:, :], func=AF.Copy), sd))
                em.wait("sync", sd, sd.n)
                em.dma("sync", P.aps["f0"][t0:t0 + 128, dd * 512:dd * 512 + 512],
                       otmp[:, dd, :], st)
                otmp_store[dd] = st.n
    em.wait("sync", st, st.n)
    em.flush()


ATT0_PANELS = 4
ATT0_CHUNKS = 4
ATT0_SUB = 9
ATT0_SUB2 = 9

def emit_att0(P):
    nc = P.nc
    em = Em(P)
    xn0b = P.aps["xn0b"]
    H = 4

    xn0T = P.sb("a0_xnT", [128, 8, 512], BF16)
    wqk = P.sb("a0_wqk", [128, 8, D], BF16)
    wo = P.sb("a0_wo", [128, 16, D], BF16)
    wvg = P.sb("a0_wvg", [128, 8, 2048], BF16)
    qpan = P.sb("a0_qp", [128, 8, 512], BF16)
    kpan = P.sb("a0_kp", [128, 8, 512], BF16)
    gtpan = P.sb("a0_gp", [128, 16, 512], BF16)
    vtok = P.sb("a0_vt", [128, 4, 2048], BF16)
    ktok = P.sb("a0_kt", [128, 8, 128], BF16)
    qsc = P.sb("a0_qs", [128, 8, 128], BF16)
    atb = P.sb("a0_at", [128, 128], BF16)
    ogT_raw = P.sb("a0_ogr", [128, 4, 128], BF16)
    sqb = P.sb("a0_sqb", [128, 4, 128], BF16)
    ogT = P.sb("a0_og", [128, 16, 128], BF16)
    Sst = P.sb("a0_S", [128, 8, 512], F32)
    Sb = P.sb("a0_Sb", [128, 8, 512], BF16)
    cosb = P.sb("a0_cos", [128, 512], F32)
    sinb = P.sb("a0_sin", [128, 512], F32)
    rtmp = P.sb("a0_rt", [128, 4, 512], F32)
    dtb = P.sb("a0_dt", [128, 4, 128], F32)
    cqb = P.sb("a0_cq", [128, 4, 128], F32)
    ckc = P.sb("a0_ck", [128, 4], F32)
    gnc = P.sb("a0_gn", [128, 4], F32)
    idb = P.sb("a0_id", [128, 128], BF16)
    ones1 = P.sb("a0_o1", [1, 128], F32)
    onescol = P.sb("a0_oc", [128, 1], BF16)
    ssrow = P.sb("a0_ss", [1, 512], F32)
    ssg = P.sb("a0_ssg", [1, 4, 4, 128], F32)
    rstdr = P.sb("a0_rsd", [1, 512], F32)
    s0out = P.sb("a0_s0", [128, 2, 512], F32)

    ld, tr, mm, vd, sd, st, gld, gd = (P.ctr("ld"), P.ctr("tr"), P.ctr("mm"), P.ctr("vd"),
                                       P.ctr("sd"), P.ctr("st"), P.ctr("gl"), P.ctr("gd"))

    ps_a = P.ps("a0_psa", [128, 512])
    ps_b = P.ps("a0_psb", [128, 512])
    ps_o1 = P.ps("a0_pso1", [128, 128])
    ps_o2 = P.ps("a0_pso2", [128, 128])
    ps_kv = P.ps("a0_pskv", [128, 1024])
    ps_tr = P.ps("a0_pstr", [128, 128], BF16)
    ps_s0 = P.ps("a0_pss0", [128, 512])

    gam = 1.0 - np.exp2(-5.0 - np.arange(H))
    gC = (gam ** 128).astype(np.float64)

    em.dma("gpsimd", wo[:, :, :], P.aps["L0_attn_Wo"].rearrange("(a p) f -> p a f", p=128), gld)
    em.dma("gpsimd", dtb[:, :, :], P.aps["l0_DT"].rearrange("h j i -> j h i"), gld)
    em.dma("gpsimd", cqb[:, :, :], P.aps["l0_crossq_bc"][:, :, :], gld)
    em.dma("gpsimd", ckc[:, :], P.aps["l0_crossk"][:, :], gld)
    em.dma("gpsimd", gnc[:, :], P.aps["l0_gn_pa"][:, :], gld)
    em.dma("gpsimd", idb[:, :], P.aps["idf32"][:, :], gld)
    em.dma("gpsimd", ones1[:, :], P.aps["ones1x128"][:, :], gld)
    em.g(lambda e: e.memset(onescol[:, :], 1.0))
    em.g(inc(lambda e: e.engine_nop(), gld, 16))
    em.v(lambda e: e.memset(Sst[:, :, :], 0.0))
    em.v(lambda e: e.memset(Sb[:, :, :], 0.0))
    em.v(inc(lambda e: e.engine_nop(), vd))

    em.wait("tensor", tr, tr.n)
    em.wait("tensor", gld, gld.n)
    em.wait("tensor", vd, vd.n)

    s0A = P.aps["s0"]
    wvA = P.aps["L0_attn_Wv"].rearrange("(a p) f -> p a f", p=128)
    wgA = P.aps["L0_attn_Wg"].rearrange("(a p) f -> p a f", p=128)

    mm_wvg_users = 0
    s0_store = {0: 0, 1: 0}

    for pan in range(ATT0_PANELS):
        p0 = pan * 512
        em.wait("sync", mm, mm.n)
        for dt_ in range(8):
            em.dmaT("sync", xn0T[:, dt_, :], xn0b[p0:p0 + 512, dt_ * 128:(dt_ + 1) * 128], tr)
        em.wait("gpsimd", vd, vd.n)
        em.dma("gpsimd", cosb[:, :], P.aps["l0_cosT"][:, p0:p0 + 512], gld)
        em.dma("gpsimd", sinb[:, :], P.aps["l0_sinT"][:, p0:p0 + 512], gld)
        em.wait("gpsimd", mm, mm_wvg_users)
        em.dma("gpsimd", wvg[:, :, :], wvA, gld)
        for wname, dstpan in (("L0_attn_Wq", qpan), ("L0_attn_Wk", kpan)):
            em.wait("gpsimd", mm, mm.n)
            em.dma("gpsimd", wqk[:, :, :],
                   P.aps[wname].rearrange("(a p) f -> p a f", p=128), gld)
            em.wait("tensor", gld, gld.n)
            em.wait("tensor", tr, tr.n)
            for ft in range(8):
                pu = ps_a if (ft % 2 == 0) else ps_b
                em.wait("tensor", sd, sd.n)
                def prj(e, ft=ft, pu=pu):
                    for kt in range(8):
                        r = e.matmul(pu[:, :], wqk[:, kt, ft * 128:(ft + 1) * 128],
                                     xn0T[:, kt, :], start=(kt == 0), stop=(kt == 7))
                    return r
                em.t(inc(prj, mm))
                em.wait("scalar", mm, mm.n)
                em.s(inc(lambda e, pu=pu, dstpan=dstpan, ft=ft: e.activation(
                    out=dstpan[:, ft, :], in_=pu[:, :], func=AF.Copy), sd))
        em.wait("tensor", gld, gld.n)
        for c in range(4):
            for nn in range(4):
                pu = ps_a if (nn % 2 == 0) else ps_b
                em.wait("tensor", vd, vd.n)
                em.wait("tensor", sd, sd.n)
                def vprj(e, c=c, nn=nn, pu=pu):
                    for kt in range(8):
                        r = e.matmul(pu[:, :], xn0T[:, kt, c * 128:(c + 1) * 128],
                                     wvg[:, kt, nn * 512:(nn + 1) * 512], start=(kt == 0), stop=(kt == 7))
                    return r
                em.t(inc(vprj, mm))
                em.wait("vector", mm, mm.n)
                em.v(inc(lambda e, c=c, nn=nn, pu=pu: e.tensor_copy(
                    vtok[:, c, nn * 512:(nn + 1) * 512], pu[:, :]), vd))
        vmm_done = mm.n
        em.wait("gpsimd", mm, vmm_done)
        em.dma("gpsimd", wvg[:, :, :], wgA, gld)
        em.wait("tensor", gld, gld.n)
        for mt in range(16):
            pu = ps_a if (mt % 2 == 0) else ps_b
            em.wait("tensor", sd, sd.n)
            em.wait("tensor", vd, vd.n)
            def gprj(e, mt=mt, pu=pu):
                for kt in range(8):
                    r = e.matmul(pu[:, :], wvg[:, kt, mt * 128:(mt + 1) * 128],
                                 xn0T[:, kt, :], start=(kt == 0), stop=(kt == 7))
                return r
            em.t(inc(gprj, mm))
            em.wait("scalar", mm, mm.n)
            em.s(inc(lambda e, mt=mt, pu=pu: e.activation(
                out=gtpan[:, mt, :], in_=pu[:, :], func=AF.Silu), sd))
        mm_wvg_users = mm.n
        em.wait("vector", sd, sd.n)
        em.wait("vector", gld, gld.n)
        for dstpan in (qpan, kpan):
            for h in range(H):
                em.v(lambda e, d=dstpan, h=h: e.tensor_tensor(rtmp[:, 0, :], d[:, 2 * h, :], cosb[:, :], ALU.mult))
                em.v(lambda e, d=dstpan, h=h: e.tensor_tensor(rtmp[:, 1, :], d[:, 2 * h + 1, :], sinb[:, :], ALU.mult))
                em.v(lambda e, d=dstpan, h=h: e.tensor_tensor(rtmp[:, 2, :], d[:, 2 * h + 1, :], cosb[:, :], ALU.mult))
                em.v(lambda e, d=dstpan, h=h: e.tensor_tensor(rtmp[:, 3, :], d[:, 2 * h, :], sinb[:, :], ALU.mult))
                em.vbar()
                em.v(lambda e, d=dstpan, h=h: e.tensor_tensor(d[:, 2 * h, :], rtmp[:, 0, :], rtmp[:, 1, :], ALU.subtract))
                em.v(lambda e, d=dstpan, h=h: e.tensor_tensor(d[:, 2 * h + 1, :], rtmp[:, 2, :], rtmp[:, 3, :], ALU.add))
                em.vbar()
        em.v(inc(lambda e: e.engine_nop(), vd))
        em.wait("tensor", vd, vd.n)

        import kernel as _K
        for c in range(ATT0_CHUNKS):
            cg = pan * 4 + c
            cl0 = c * 128
            for kt8 in range(8):
                em.wait("tensor", vd, vd.n)
                em.t(inc(lambda e, kt8=kt8, cl0=cl0: e.transpose(
                    ps_tr[:, :], kpan[:, kt8, cl0:cl0 + 128], idb[:, :]), mm))
                em.wait("vector", mm, mm.n)
                em.v(inc(lambda e, kt8=kt8: e.tensor_scalar(
                    ktok[:, kt8, :], ps_tr[:, :], ckc[:, kt8 // 2:kt8 // 2 + 1], None, ALU.mult), vd))
            for kt8 in range(8):
                em.v(lambda e, kt8=kt8, cl0=cl0: e.tensor_tensor(
                    qsc[:, kt8, :], qpan[:, kt8, cl0:cl0 + 128], cqb[:, kt8 // 2, :], ALU.mult))
            em.v(inc(lambda e: e.engine_nop(), vd))
            em.wait("tensor", vd, vd.n)
            if _K.ATT0_SUB < 2:
                continue
            for h in range(H):
                em.wait("tensor", vd, vd.n)
                em.wait("tensor", sd, sd.n)
                def atm(e, h=h, cl0=cl0):
                    e.matmul(ps_a[:, 0:128], kpan[:, 2 * h, cl0:cl0 + 128],
                             qpan[:, 2 * h, cl0:cl0 + 128], start=True, stop=False)
                    return e.matmul(ps_a[:, 0:128], kpan[:, 2 * h + 1, cl0:cl0 + 128],
                                    qpan[:, 2 * h + 1, cl0:cl0 + 128], start=False, stop=True)
                em.t(inc(atm, mm))
                em.wait("vector", mm, mm.n)
                em.v(inc(lambda e, h=h: e.tensor_tensor(atb[:, :], ps_a[:, 0:128], dtb[:, h, :], ALU.mult), vd))
                em.wait("tensor", vd, vd.n)
                if _K.ATT0_SUB2 < 2:
                    continue
                for es in range(4):
                    po = ps_o1 if es % 2 == 0 else ps_o2
                    if es >= 2:
                        em.wait("tensor", vd, vd.n)
                        em.wait("tensor", sd, sd.n)
                    def otm(e, h=h, c=c, es=es, po=po):
                        e.matmul(po[:, :],
                                 vtok[:, c, 512 * h + 128 * es: 512 * h + 128 * (es + 1)],
                                 atb[:, :], start=True, stop=False)
                        e.matmul(po[:, :], Sb[:, h * 2 + 0, es * 128:(es + 1) * 128],
                                 qsc[:, 2 * h, :], start=False, stop=False)
                        return e.matmul(po[:, :], Sb[:, h * 2 + 1, es * 128:(es + 1) * 128],
                                        qsc[:, 2 * h + 1, :], start=False, stop=True)
                    em.t(inc(otm, mm))
                    em.wait("vector", mm, mm.n)
                    em.v(inc(lambda e, es=es, po=po: e.tensor_copy(ogT_raw[:, es, :], po[:, :]), vd))
                    if _K.ATT0_SUB2 >= 3:
                        em.wait("scalar", vd, vd.n)
                        em.s(inc(lambda e, es=es: e.activation(
                            out=sqb[:, es, :], in_=ogT_raw[:, es, :], func=AF.Square), sd))
                if _K.ATT0_SUB2 >= 4:
                    em.wait("gpsimd", sd, sd.n)
                    em.g(inc(lambda e, h=h: e.tensor_reduce(
                        ssg[0:1, h, :, :], sqb[:, :, :], axis=AX.C, op=ALU.add), gd))
                if _K.ATT0_SUB2 < 5:
                    continue
                em.wait("vector", vd, vd.n)
                for es in range(4):
                    em.v(lambda e, h=h, es=es, cl0=cl0: e.scalar_tensor_tensor(
                        ogT[:, h * 4 + es, :], ogT_raw[:, es, :], gnc[:, es:es + 1],
                        gtpan[:, h * 4 + es, cl0:cl0 + 128], ALU.mult, ALU.mult))
                em.v(inc(lambda e: e.engine_nop(), vd))
            if _K.ATT0_SUB < 3:
                continue
            em.wait("vector", gd, gd.n)
            em.v(lambda e: e.reduce_sum(
                ssrow[0:1, :].rearrange("o (h n) -> o h n", h=4),
                ssg[0:1, :, :, :].rearrange("o h es n -> o h n es"),
                axis=AX.X))
            em.vbar()
            em.v(lambda e: e.tensor_scalar(ssrow[0:1, :], ssrow[0:1, :], 1.0 / 512, 1e-5, ALU.mult, ALU.add))
            em.v(inc(lambda e: e.engine_nop(), vd))
            em.wait("scalar", vd, vd.n)
            em.s(inc(lambda e: e.activation(out=ssrow[0:1, :], in_=ssrow[0:1, :], func=AF.Sqrt), sd))
            em.wait("vector", sd, sd.n)
            em.v(lambda e: e.reciprocal(rstdr[0:1, :], ssrow[0:1, :]))
            em.v(inc(lambda e: e.engine_nop(), vd))
            em.wait("tensor", vd, vd.n)
            if _K.ATT0_SUB < 4:
                continue
            for h in range(H):
                em.t(inc(lambda e, h=h: e.matmul(ps_s0[:, 0:128], ones1[0:1, 0:128],
                                                 rstdr[0:1, h * 128:(h + 1) * 128],
                                                 start=True, stop=True), mm))
                em.wait("vector", mm, mm.n)
                for es in range(4):
                    em.v(lambda e, h=h, es=es: e.tensor_tensor(
                        ogT[:, h * 4 + es, :], ogT[:, h * 4 + es, :], ps_s0[:, 0:128], ALU.mult))
                em.v(inc(lambda e: e.engine_nop(), vd))
                em.wait("tensor", vd, vd.n)
            if _K.ATT0_SUB < 5:
                continue
            for fc in range(2):
                em.wait("tensor", sd, sd.n)
                def s0m(e, fc=fc):
                    for idx in range(16):
                        r = e.matmul(ps_s0[:, 0:512], ogT[:, idx, :], wo[:, idx, fc * 512:(fc + 1) * 512],
                                     start=(idx == 0), stop=(idx == 15))
                    return r
                em.t(inc(s0m, mm))
                em.wait("scalar", mm, mm.n)
                em.wait("scalar", st, s0_store[fc])
                em.s(inc(lambda e, fc=fc: e.activation(out=s0out[:, fc, :], in_=ps_s0[:, 0:512], func=AF.Copy), sd))
                em.wait("sync", sd, sd.n)
                em.dma("sync", s0A[cg * 128:(cg + 1) * 128, fc * 512:(fc + 1) * 512], s0out[:, fc, :], st)
                s0_store[fc] = st.n
            if _K.ATT0_SUB < 6:
                continue
            em.wait("tensor", vd, vd.n)
            for h in range(H):
                def kvm(e, h=h, c=c):
                    for dt_ in range(2):
                        r = e.matmul(ps_kv[:, dt_ * 512:(dt_ + 1) * 512], ktok[:, 2 * h + dt_, :],
                                     vtok[:, c, 512 * h:512 * (h + 1)], start=True, stop=True)
                    return r
                em.t(inc(kvm, mm))
                em.wait("vector", mm, mm.n)
                for dt_ in range(2):
                    em.v(lambda e, h=h, dt_=dt_: e.scalar_tensor_tensor(
                        Sst[:, h * 2 + dt_, :], Sst[:, h * 2 + dt_, :], float(gC[h]),
                        ps_kv[:, dt_ * 512:(dt_ + 1) * 512], ALU.mult, ALU.add))
                em.vbar()
                for dt_ in range(2):
                    em.v(lambda e, h=h, dt_=dt_: e.tensor_copy(Sb[:, h * 2 + dt_, :], Sst[:, h * 2 + dt_, :]))
                em.v(inc(lambda e: e.engine_nop(), vd))
                em.wait("tensor", vd, vd.n)
    em.wait("sync", st, st.n)
    em.flush()


LAMBDA_INIT = 0.8 - 0.6 * math.exp(-0.3 * 12)

def emit_att1(P):
    nc = P.nc
    em = Em(P)
    xn1b = P.aps["xn1b"]

    xn1T = P.sb("a1_xnT", [128, 8, 512], BF16)
    wbuf = P.sb("a1_w", [128, 16, 1024], BF16)
    qtb = P.sb("a1_qt", [128, 1024], BF16)
    costile = P.sb("a1_cos", [128, 1024], F32)
    sintile = P.sb("a1_sin", [128, 1024], F32)
    rt1 = P.sb("a1_rt1", [128, 1024], F32)
    rt2 = P.sb("a1_rt2", [128, 512], F32)
    rtb = P.sb("a1_rtb", [128, 512], F32)
    vv = P.sb("a1_vv", [128, 16, 528], BF16)
    kaT = P.sb("a1_kaT", [128, 8, NT], BF16)
    qa = P.sb("a1_qa", [128, 16, 512], BF16)
    PTb = P.sb("a1_pt", [128, 16, 512], BF16)
    afT = P.sb("a1_af", [64, 16, 512], BF16)
    wo = wbuf
    km2 = P.sb("a1_km2", [128, 16], F32)
    kn2 = P.sb("a1_kn2", [128, 16], F32)
    rowsb = P.sb("a1_rows", [1, 16 * 512], F32)
    rb32 = P.sb("a1_rb32", [32, 512], F32)
    rows16 = P.sb("a1_r16", [16, 512], F32)
    kmsqc = P.sb("a1_kmsq", [32, 1], F32)
    lrow = P.sb("a1_lrow", [1, 4, 32], F32)
    lam = P.sb("a1_lam", [1, 8], F32)
    rmswc = P.sb("a1_rmsw", [128, 1], F32)
    zrows = P.sb("a1_zr", [1, 4, 512], F32)
    onesr = P.sb("a1_ones", [1, 128], F32)
    s1out = P.sb("a1_s1o", [128, 2, 512], F32)

    ld, tr, mm, vd, sd, st, gld, gd = (P.ctr("ld"), P.ctr("tr"), P.ctr("mm"), P.ctr("vd"),
                                       P.ctr("sd"), P.ctr("st"), P.ctr("gl"), P.ctr("gd"))

    ps_a = P.ps("a1_psa", [128, 512])
    ps_b = P.ps("a1_psb", [128, 512])
    ps_pt = P.ps("a1_pspt", [128, 512])
    ps_av1 = P.ps("a1_psav1", [128, 512])
    ps_av2 = P.ps("a1_psav2", [128, 512])
    ps_s1 = P.ps("a1_pss1", [128, 512])
    ps_bc = P.ps("a1_psbc", [128, 512])

    l1qA, l1kA, s1A = P.aps["l1q"], P.aps["l1k"], P.aps["s1"]

    em.dma("gpsimd", onesr[:, :], P.aps["ones1x128"][:, :], gld)
    em.dma("gpsimd", rmswc[:, :], P.aps["l1_rmsw_col"][:, :], gld)
    for i, nm in enumerate(["L1_attn_lambda_q1", "L1_attn_lambda_k1",
                            "L1_attn_lambda_q2", "L1_attn_lambda_k2"]):
        em.dma("gpsimd", lrow[0:1, i, :], P.aps[nm].rearrange("(o d) -> o d", o=1), gld)

    em.v(lambda e: e.memset(km2[:, :], 0.0))
    em.v(lambda e: e.memset(
        vv[:, :, :].rearrange("p a (g u) -> p a g u", u=66)[:, :, :, 64:65], 1.0))
    em.v(inc(lambda e: e.engine_nop(), vd))
    em.wait("vector", gld, gld.n)
    em.v(lambda e: e.tensor_tensor(lrow[0:1, 0, :], lrow[0:1, 0, :], lrow[0:1, 1, :], ALU.mult))
    em.v(lambda e: e.tensor_tensor(lrow[0:1, 2, :], lrow[0:1, 2, :], lrow[0:1, 3, :], ALU.mult))
    em.vbar()
    em.v(lambda e: e.reduce_sum(lam[0:1, 0:1], lrow[0:1, 0, :], axis=AX.X))
    em.v(lambda e: e.reduce_sum(lam[0:1, 1:2], lrow[0:1, 2, :], axis=AX.X))
    em.v(inc(lambda e: e.engine_nop(), vd))
    em.wait("scalar", vd, vd.n)
    em.s(inc(lambda e: e.activation(out=lam[0:1, 2:4], in_=lam[0:1, 0:2], func=AF.Exp), sd))
    em.wait("vector", sd, sd.n)
    em.v(lambda e: e.tensor_tensor(lam[0:1, 4:5], lam[0:1, 3:4], lam[0:1, 2:3], ALU.subtract))
    em.vbar()
    em.v(inc(lambda e: e.tensor_scalar(lam[0:1, 4:5], lam[0:1, 4:5], -LAMBDA_INIT, None, ALU.add), vd))

    for pan in range(4):
        p0 = pan * 512
        em.wait("sync", mm, mm.n)
        for dt_ in range(8):
            em.dmaT("sync", xn1T[:, dt_, :], xn1b[1 + p0:1 + p0 + 512, dt_ * 128:(dt_ + 1) * 128], tr)
        for wi, wname in enumerate(["L1_attn_Wq", "L1_attn_Wk", "L1_attn_Wv"]):
            fw = 1024 if wi == 0 else 512
            em.wait("gpsimd", mm, mm.n)
            em.dma("gpsimd", wbuf[:, 0:8, 0:fw],
                   P.aps[wname].rearrange("(a p) f -> p a f", p=128), gld)
            em.wait("tensor", gld, gld.n)
            em.wait("tensor", tr, tr.n)
            for t in range(4):
                tt = pan * 4 + t
                em.wait("tensor", vd, vd.n)
                em.wait("tensor", sd, sd.n)
                for fc in range(fw // 512):
                    pu = ps_a if fc % 2 == 0 else ps_b
                    def prj(e, t=t, fc=fc, pu=pu):
                        for kt in range(8):
                            r = e.matmul(pu[:, :], xn1T[:, kt, t * 128:(t + 1) * 128],
                                         wbuf[:, kt, fc * 512:fc * 512 + 512],
                                         start=(kt == 0), stop=(kt == 7))
                        return r
                    em.t(inc(prj, mm))
                    em.wait("scalar", mm, mm.n)
                    em.s(inc(lambda e, fc=fc, pu=pu: e.activation(
                        out=qtb[:, fc * 512:fc * 512 + 512], in_=pu[:, :], func=AF.Copy), sd))
                if wi < 2:
                    ctab = "l1_cosq" if wi == 0 else "l1_cosk"
                    stab = "l1_sinq" if wi == 0 else "l1_sink"
                    em.wait("sync", vd, vd.n)
                    em.dma("sync", costile[:, 0:fw], P.aps[ctab][tt * 128:(tt + 1) * 128, 0:fw], ld)
                    em.dma("sync", sintile[:, 0:fw], P.aps[stab][tt * 128:(tt + 1) * 128, 0:fw], ld)
                    em.wait("vector", ld, ld.n)
                    em.wait("vector", sd, sd.n)
                    em.v(lambda e, fw=fw: e.tensor_tensor(rt1[:, 0:fw], qtb[:, 0:fw], costile[:, 0:fw], ALU.mult))
                    em.v(lambda e, fw=fw: e.tensor_tensor(costile[:, 0:fw], qtb[:, 0:fw], sintile[:, 0:fw], ALU.mult))
                    em.vbar()
                    va = lambda a, fw: a[:, 0:fw].rearrange("p (h x) -> p h x", x=32)
                    em.v(lambda e, fw=fw: e.tensor_tensor(
                        va(qtb, fw)[:, :, 0:16], va(rt1, fw)[:, :, 0:16],
                        va(costile, fw)[:, :, 16:32], ALU.add))
                    em.v(lambda e, fw=fw: e.tensor_tensor(
                        va(qtb, fw)[:, :, 16:32], va(rt1, fw)[:, :, 16:32],
                        va(costile, fw)[:, :, 0:16], ALU.subtract))
                    em.vbar()
                    if wi == 1:
                        em.v(lambda e: e.tensor_tensor(rt2[:, :], qtb[:, 0:512], qtb[:, 0:512], ALU.mult))
                        em.vbar()
                        em.v(lambda e: e.reduce_sum(kn2[:, :].rearrange("p (h o) -> p h o", o=1),
                                                    rt2[:, :].rearrange("p (h x) -> p h x", x=32), axis=AX.X))
                        em.vbar()
                        em.v(lambda e: e.tensor_tensor(km2[:, :], km2[:, :], kn2[:, :], ALU.max))
                        em.vbar()
                    em.v(inc(lambda e: e.engine_nop(), vd))
                    em.wait("sync", vd, vd.n)
                    dst = l1qA if wi == 0 else l1kA
                    nh = fw // 32
                    em.dma("sync", dst[tt * 128:(tt + 1) * 128, :].rearrange(
                        "p (h u) -> p h u", u=64)[:, :, 0:32],
                        qtb[:, 0:fw].rearrange("p (h x) -> p h x", x=32), st)
                else:
                    em.wait("vector", sd, sd.n)
                    em.v(inc(lambda e, tt=tt: e.tensor_copy(
                        vv[:, tt, :].rearrange("p (g u) -> p g u", u=66)[:, :, 0:64],
                        qtb[:, 0:512].rearrange("p (g u) -> p g u", u=64)), vd))

    em.wait("gpsimd", mm, mm.n)
    em.dma("gpsimd", wo[:, :, :], P.aps["l1_wo_pad"].rearrange("(a p) f -> p a f", p=128), gld)
    em.wait("sync", st, st.n)
    for b in range(8):
        em.dmaT("sync", kaT[:, b, :], l1kA[:, b * 128:(b + 1) * 128], tr)
    em.wait("vector", tr, tr.n)
    for b in range(8):
        em.v(lambda e, b=b: e.memset(kaT[32:33, b, :], 1.0))
        em.v(lambda e, b=b: e.memset(kaT[96:97, b, :], 1.0))
    em.v(inc(lambda e: e.engine_nop(), vd))
    em.wait("gpsimd", vd, vd.n)
    em.g(inc(lambda e: e.tensor_reduce(zrows[0:1, 0, 0:16], km2[:, :], axis=AX.C, op=ALU.max), gd))
    em.wait("vector", gd, gd.n)
    em.v(lambda e: e.reduce_max(lam[0:1, 5:6], zrows[0:1, 0, 0:16], axis=AX.X))
    em.v(inc(lambda e: e.engine_nop(), vd))
    em.wait("tensor", vd, vd.n)
    em.t(inc(lambda e: e.matmul(ps_bc[0:32, 0:1], onesr[0:1, 0:32], lam[0:1, 5:6],
                                start=True, stop=True), mm))
    em.wait("vector", mm, mm.n)
    em.v(inc(lambda e: e.tensor_copy(kmsqc[:, :], ps_bc[0:32, 0:1]), vd))

    s1_store = {0: 0, 1: 0}
    scale = 32.0 ** -0.5
    for qc in range(4):
        nkt = 4 * (qc + 1)
        em.wait("sync", mm, mm.n)
        em.wait("sync", vd, vd.n)
        em.wait("sync", sd, sd.n)
        for b in range(16):
            em.dmaT("sync", qa[:, b, :], l1qA[qc * 512:(qc + 1) * 512, b * 128:(b + 1) * 128], tr)
        em.wait("scalar", tr, tr.n)
        for ti in range(16):
            em.s(inc(lambda e, ti=ti: e.activation(out=PTb[:, ti, :], in_=qa[:, ti, :],
                                                   func=AF.Square), sd))
        for c in range(2):
            base = 64 * c
            em.wait("gpsimd", sd, sd.n)
            em.wait("gpsimd", ld, ld.n)
            for t_ in range(16):
                em.g(inc(lambda e, t_=t_, base=base: e.tensor_reduce(
                    rowsb[0:1, t_ * 512:(t_ + 1) * 512], PTb[base:base + 32, t_, :],
                    axis=AX.C, op=ALU.add), gd))
            em.wait("sync", gd, gd.n)
            em.dma("sync", rb32[c * 16:(c + 1) * 16, :],
                   rowsb[0:1, :].rearrange("o (t n) -> o t n", n=512), ld)
        em.wait("vector", ld, ld.n)
        em.v(lambda e: e.tensor_scalar(rb32[:, :], rb32[:, :], kmsqc[:, 0:1], None, ALU.mult))
        em.v(inc(lambda e: e.engine_nop(), vd))
        em.wait("scalar", vd, vd.n)
        em.s(inc(lambda e: e.activation(out=rb32[:, :], in_=rb32[:, :], func=AF.Sqrt), sd))
        em.s(inc(lambda e: e.mul(rb32[:, :], rb32[:, :], -1.0), sd))
        em.wait("gpsimd", sd, sd.n)
        em.dma("gpsimd", qa[32:33, :, :], rb32[0:16, :], ld)
        em.dma("gpsimd", qa[96:97, :, :], rb32[16:32, :], ld)
        em.wait("tensor", ld, ld.n)
        em.wait("tensor", vd, vd.n)
        em.wait("tensor", gd, gd.n)
        for h in range(16):
            for c in range(2):
                ha = 2 * h + c
                kk = (h // 2) * 2 + c
                base_q = 64 * (ha % 2)
                base_k = 64 * (kk % 2)
                pav = ps_av1 if c == 0 else ps_av2
                for kt in range(nkt):
                    em.wait("tensor", sd, sd.n)
                    em.t(inc(lambda e, ha=ha, kk=kk, base_q=base_q, base_k=base_k, kt=kt:
                             e.matmul(ps_pt[:, :], kaT[base_k:base_k + 33, kk // 2, kt * 128:(kt + 1) * 128],
                                      qa[base_q:base_q + 33, ha // 2, :], start=True, stop=True), mm))
                    em.wait("scalar", mm, mm.n)
                    em.s(inc(lambda e, kt=kt: e.activation(out=PTb[:, kt, :], in_=ps_pt[:, :],
                                                           func=AF.Exp), sd))
                em.wait("gpsimd", sd, sd.n)
                for kt in range(4 * qc, nkt):
                    em.g(inc(lambda e, kt=kt, qc=qc: e.affine_select(
                        PTb[:, kt, :], PTb[:, kt, :], [[1, 512]], ALU.is_ge, 0.0,
                        base=qc * 512 - kt * 128, channel_multiplier=-1), gd))
                em.wait("tensor", gd, gd.n)
                em.wait("tensor", vd, vd.n)
                def avm(e, h=h, pav=pav, nkt=nkt):
                    g = h // 2
                    for kt in range(nkt):
                        r = e.matmul(pav[0:65, :], vv[:, kt, g * 66:g * 66 + 65],
                                     PTb[:, kt, :], start=(kt == 0), stop=(kt == nkt - 1))
                    return r
                em.t(inc(avm, mm))
            em.wait("vector", mm, mm.n)
            em.v(lambda e: e.reciprocal(zrows[0:1, 0, :], ps_av1[64:65, :]))
            em.v(lambda e: e.reciprocal(zrows[0:1, 1, :], ps_av2[64:65, :]))
            em.vbar()
            em.v(lambda e: e.tensor_scalar(zrows[0:1, 1, :], zrows[0:1, 1, :],
                                           lam[0:1, 4:5], None, ALU.mult))
            em.v(inc(lambda e: e.engine_nop(), vd))
            em.wait("tensor", vd, vd.n)
            em.t(inc(lambda e: e.matmul(ps_bc[0:64, :], onesr[0:1, 0:64], zrows[0:1, 0, :],
                                        start=True, stop=True), mm))
            em.wait("vector", mm, mm.n)
            em.v(lambda e: e.tensor_copy(rt1[0:64, 0:512], ps_bc[0:64, :]))
            em.vbar()
            em.v(inc(lambda e: e.tensor_tensor(rt2[0:64, :], ps_av1[0:64, :], rt1[0:64, 0:512], ALU.mult), vd))
            em.wait("tensor", vd, vd.n)
            em.t(inc(lambda e: e.matmul(ps_bc[0:64, :], onesr[0:1, 0:64], zrows[0:1, 1, :],
                                        start=True, stop=True), mm))
            em.wait("vector", mm, mm.n)
            em.v(lambda e: e.tensor_copy(rt1[0:64, 0:512], ps_bc[0:64, :]))
            em.vbar()
            em.v(inc(lambda e: e.tensor_tensor(rtb[0:64, :], ps_av2[0:64, :], rt1[0:64, 0:512], ALU.mult), vd))
            em.vbar()
            em.v(inc(lambda e, h=h: e.tensor_tensor(afT[0:64, h, :], rt2[0:64, :], rtb[0:64, :], ALU.add), vd))
            em.vbar()
        em.wait("scalar", vd, vd.n)
        for h in range(16):
            em.s(inc(lambda e, h=h: e.activation(out=PTb[0:64, h, :], in_=afT[0:64, h, :],
                                                 func=AF.Square), sd))
        em.wait("gpsimd", sd, sd.n)
        for h in range(16):
            em.g(inc(lambda e, h=h: e.tensor_reduce(
                rowsb[0:1, h * 512:(h + 1) * 512], PTb[0:64, h, :], axis=AX.C, op=ALU.add), gd))
        em.wait("sync", gd, gd.n)
        em.dma("sync", rows16[:, :], rowsb[0:1, 0:8192].rearrange("o (t n) -> o t n", n=512), ld)
        em.wait("vector", ld, ld.n)
        em.v(lambda e: e.tensor_scalar(rows16[:, :], rows16[:, :], 1.0 / 64, 1e-6, ALU.mult, ALU.add))
        em.v(inc(lambda e: e.engine_nop(), vd))
        em.wait("scalar", vd, vd.n)
        em.s(inc(lambda e: e.activation(out=rows16[:, :], in_=rows16[:, :], func=AF.Sqrt), sd))
        em.wait("vector", sd, sd.n)
        em.v(lambda e: e.reciprocal(rows16[:, :], rows16[:, :]))
        em.vbar()
        em.v(inc(lambda e: e.tensor_scalar(rows16[:, :], rows16[:, :], 1.0 - LAMBDA_INIT, None, ALU.mult), vd))
        em.wait("sync", vd, vd.n)
        em.dma("sync", rowsb[0:1, 0:8192].rearrange("o (t n) -> o t n", n=512), rows16[:, :], ld)
        em.wait("tensor", ld, ld.n)
        for h in range(16):
            em.t(inc(lambda e, h=h: e.matmul(ps_bc[0:64, :], onesr[0:1, 0:64],
                                             rowsb[0:1, h * 512:(h + 1) * 512],
                                             start=True, stop=True), mm))
            em.wait("vector", mm, mm.n)
            em.v(inc(lambda e, h=h: e.scalar_tensor_tensor(
                afT[0:64, h, :], afT[0:64, h, :], rmswc[0:64, 0:1], ps_bc[0:64, :],
                ALU.mult, ALU.mult), vd))
            em.wait("tensor", vd, vd.n)
        for qs in range(4):
            for fc in range(2):
                em.wait("tensor", sd, sd.n)
                def s1m(e, qs=qs, fc=fc):
                    for h in range(16):
                        r = e.matmul(ps_s1[:, :], afT[0:64, h, qs * 128:(qs + 1) * 128],
                                     wo[0:64, h, fc * 512:(fc + 1) * 512],
                                     start=(h == 0), stop=(h == 15))
                    return r
                em.t(inc(s1m, mm))
                em.wait("scalar", mm, mm.n)
                em.wait("scalar", st, s1_store[fc])
                em.s(inc(lambda e, fc=fc: e.activation(out=s1out[:, fc, :], in_=ps_s1[:, :],
                                                       func=AF.Copy), sd))
                em.wait("sync", sd, sd.n)
                em.dma("sync", s1A[qc * 512 + qs * 128: qc * 512 + (qs + 1) * 128,
                                   fc * 512:(fc + 1) * 512], s1out[:, fc, :], st)
                s1_store[fc] = st.n
    em.wait("sync", st, st.n)
    em.flush()


FFN1_PRO = 9
FFN1_PH = 9

def emit_ffn1(P, l):
    import kernel as _K
    nc = P.nc
    em = Em(P)
    sfx = str(l)
    yinA = P.aps["xn1f" if l == 0 else "ymid"]
    ysrcA = P.aps["xn1b" if l == 0 else "y1b_0"]
    aqA, akA, h2A = P.aps[f"aq_{l}"], P.aps[f"ak_{l}"], P.aps[f"h2_{l}"]
    yaA, y1fA = P.aps[f"ya_{l}"], P.aps[f"y1f_{l}"]
    outA = P.aps["ymid" if l == 0 else "f1"]
    outbA = P.aps["y1b_0" if l == 0 else None] if l == 0 else None

    ld, tr, mm, vd, sd, st, gld, gd = (P.ctr("ld"), P.ctr("tr"), P.ctr("mm"), P.ctr("vd"),
                                       P.ctr("sd"), P.ctr("st"), P.ctr("gl"), P.ctr("gd"))
    ps_a = P.ps(f"f1{sfx}_psa", [128, 512])
    ps_b = P.ps(f"f1{sfx}_psb", [128, 512])
    ps_c = P.ps(f"f1{sfx}_psc", [128, 512])
    ps_pt = P.ps(f"f1{sfx}_pspt", [128, 512])
    ps_av = P.ps(f"f1{sfx}_psav", [128, 512])
    ps_bc = P.ps(f"f1{sfx}_psbc", [128, 512])
    ps_o = P.ps(f"f1{sfx}_pso", [128, 512])

    from contextlib import ExitStack as _ES
    sub = _ES()
    def asb(name, shape, dt):
        return sub.enter_context(nc.sbuf_tensor(name, shape, dt))

    hTp = asb(f"f1{sfx}_hT", [128, 8, 512], BF16)
    wq3 = asb(f"f1{sfx}_wq3", [128, 8, 1536], BF16)
    qkt = asb(f"f1{sfx}_qkt", [128, 1536], BF16)
    sqb = asb(f"f1{sfx}_sqb", [128, 1024], F32)
    nrm = asb(f"f1{sfx}_nrm", [128, 16], F32)
    qsbc = asb(f"f1{sfx}_qsbc", [128, 2, 512], F32)
    cosx = asb(f"f1{sfx}_cosx", [128, 512], F32)
    sinx = asb(f"f1{sfx}_sinx", [128, 512], F32)
    rxa = asb(f"f1{sfx}_rxa", [128, 512], F32)
    rxb = asb(f"f1{sfx}_rxb", [128, 512], F32)
    vvf = asb(f"f1{sfx}_vvf", [128, 16, 528], BF16)
    kfa = asb(f"f1{sfx}_kfa", [128, 8, NT], BF16)
    kprev = asb(f"f1{sfx}_kprev", [128, 8, 512], BF16)
    qfa = asb(f"f1{sfx}_qfa", [128, 8, 512], BF16)
    PTb = asb(f"f1{sfx}_pt", [128, 8, 512], BF16)
    ofa = asb(f"f1{sfx}_ofa", [64, 8, 512], BF16)
    wob = asb(f"f1{sfx}_wob", [128, 8, 1024], BF16)
    b2col = asb(f"f1{sfx}_b2c", [128, 4], F32)
    negm = asb(f"f1{sfx}_negm", [128, 32], F32)
    rows32 = asb(f"f1{sfx}_r32", [32, 128], F32)
    cmask = asb(f"f1{sfx}_cm", [128, 4, 512], F32)
    acc0 = asb(f"f1{sfx}_ac0", [128, 1], F32)
    acc1 = asb(f"f1{sfx}_ac1", [128, 1], F32)
    b2v = asb(f"f1{sfx}_b2v", [128, 1], F32)
    onesr = asb(f"f1{sfx}_on", [1, 128], F32)
    idf = asb(f"f1{sfx}_id", [128, 128], F32)
    ytile = asb(f"f1{sfx}_yt", [128, 2, 1024], F32)
    ybf = asb(f"f1{sfx}_ybf", [128, 2, 1024], BF16)
    zrow = asb(f"f1{sfx}_zr", [1, 3, 512], F32)
    zb = asb(f"f1{sfx}_zb", [1, D], BF16)

    em.dma("gpsimd", onesr[:, :], P.aps["ones1x128"][:, :], gld)
    if _K.FFN1_PRO >= 2:
        em.dma("gpsimd", idf[:, :], P.aps["idf32"][:, :], gld)
        em.dma("gpsimd", cmask[:, :, :], P.aps["l1f_cmask"].rearrange("p (a n) -> p a n", a=4), gld)
        em.dma("gpsimd", b2v[:, :], P.aps["l1f_base2"][:, :], gld)
    if _K.FFN1_PRO >= 3:
        em.dma("gpsimd", wq3[:, :, :], P.aps["L1_ffn_wqkv"][l].rearrange("(a p) f -> p a f", p=128), gld)
        em.dma("gpsimd", wob[:, :, :], P.aps[f"l1f_wo_pad_{l}"].rearrange("(a p) f -> p a f", p=128), gld)
    em.dma("gpsimd", zrow[0:1, 0, :], P.aps[f"l1f_qs_til_{l}"][:, :], gld)
    em.dma("gpsimd", zrow[0:1, 1, :], P.aps[f"l1f_ks_til_{l}"][:, :], gld)
    if _K.FFN1_PRO >= 4:
        em.v(lambda e: e.memset(vvf[:, :, :].rearrange("p a (g u) -> p a g u", u=66)[:, :, :, 64:65], 1.0))
    em.v(inc(lambda e: e.engine_nop(), vd))
    em.wait("tensor", gld, gld.n)
    em.wait("tensor", vd, vd.n)
    if _K.FFN1_PRO >= 5:
        for i in range(2):
            em.wait("tensor", vd, vd.n)
            em.t(inc(lambda e, i=i: e.matmul(ps_bc[:, :], onesr[0:1, :], zrow[0:1, i, :],
                                             start=True, stop=True), mm))
            em.wait("vector", mm, mm.n)
            em.v(inc(lambda e, i=i: e.tensor_copy(qsbc[:, i, :], ps_bc[:, :]), vd))

    if _K.FFN1_PH < 1:
        em.wait("gpsimd", gld, gld.n)
        em.wait("sync", tr, tr.n)
        em.wait("sync", ld, ld.n)
        em.wait("sync", st, st.n)
        em.flush()
        sub.close()
        return
    for pan in range(4):
        p0 = pan * 512
        em.wait("sync", mm, mm.n)
        for dt_ in range(4):
            em.dmaT("sync", hTp[:, dt_, :], ysrcA[1 + p0:1 + p0 + 512, dt_ * 128:(dt_ + 1) * 128], tr)
        for dt_ in range(4, 8):
            em.dmaT("sync", hTp[:, dt_, :], ysrcA[p0:p0 + 512, dt_ * 128:(dt_ + 1) * 128], tr)
        em.wait("tensor", tr, tr.n)
        for t in range(4):
            tt = pan * 4 + t
            em.wait("tensor", vd, vd.n)
            em.wait("tensor", sd, sd.n)
            for fc in range(3):
                pu = (ps_a, ps_b, ps_c)[fc]
                def prj(e, t=t, fc=fc, pu=pu):
                    for kt in range(8):
                        r = e.matmul(pu[:, :], hTp[:, kt, t * 128:(t + 1) * 128],
                                     wq3[:, kt, fc * 512:fc * 512 + 512],
                                     start=(kt == 0), stop=(kt == 7))
                    return r
                em.t(inc(prj, mm))
                em.wait("scalar", mm, mm.n)
                em.s(inc(lambda e, fc=fc, pu=pu: e.activation(
                    out=qkt[:, fc * 512:fc * 512 + 512], in_=pu[:, :], func=AF.Copy), sd))
            em.wait("vector", sd, sd.n)
            em.v(inc(lambda e, tt=tt: e.tensor_copy(
                vvf[:, tt, :].rearrange("p (g u) -> p g u", u=66)[:, :, 0:64],
                qkt[:, 1024:1536].rearrange("p (g u) -> p g u", u=64)), vd))
            for which in range(2):
                off = which * 512
                tabc = f"l1f_cos{'q' if which == 0 else 'k'}"
                tabs = f"l1f_sin{'q' if which == 0 else 'k'}"
                em.wait("sync", vd, vd.n)
                em.dma("sync", cosx[:, :], P.aps[tabc][tt * 128:(tt + 1) * 128, :], ld)
                em.dma("sync", sinx[:, :], P.aps[tabs][tt * 128:(tt + 1) * 128, :], ld)
                em.v(lambda e, off=off: e.tensor_tensor(sqb[:, 0:512], qkt[:, off:off + 512],
                                                        qkt[:, off:off + 512], ALU.mult))
                em.vbar()
                em.v(lambda e, which=which: e.reduce_sum(
                    nrm[:, which * 8:(which + 1) * 8].rearrange("p (h o) -> p h o", o=1),
                    sqb[:, 0:512].rearrange("p (h x) -> p h x", x=64), axis=AX.X))
                em.v(inc(lambda e: e.engine_nop(), vd))
                em.wait("scalar", vd, vd.n)
                em.s(inc(lambda e, which=which: e.activation(
                    out=nrm[:, which * 8:(which + 1) * 8],
                    in_=nrm[:, which * 8:(which + 1) * 8], func=AF.Sqrt), sd))
                em.wait("vector", sd, sd.n)
                em.v(lambda e, which=which: e.tensor_scalar(
                    nrm[:, which * 8:(which + 1) * 8], nrm[:, which * 8:(which + 1) * 8],
                    1e-12, None, ALU.max))
                em.vbar()
                em.v(lambda e, which=which: e.reciprocal(
                    nrm[:, which * 8:(which + 1) * 8], nrm[:, which * 8:(which + 1) * 8]))
                em.vbar()
                for h in range(8):
                    em.v(lambda e, off=off, h=h, which=which: e.tensor_scalar(
                        qkt[:, off + h * 64: off + (h + 1) * 64],
                        qkt[:, off + h * 64: off + (h + 1) * 64],
                        nrm[:, which * 8 + h:which * 8 + h + 1], None, ALU.mult))
                em.vbar()
                em.v(lambda e, off=off, which=which: e.tensor_tensor(
                    qkt[:, off:off + 512], qkt[:, off:off + 512], qsbc[:, which, :], ALU.mult))
                em.vbar()
                em.wait("vector", ld, ld.n)
                em.v(lambda e, off=off: e.tensor_tensor(rxa[:, :], qkt[:, off:off + 512], cosx[:, :], ALU.mult))
                em.v(lambda e, off=off: e.tensor_tensor(rxb[:, :], qkt[:, off:off + 512], sinx[:, :], ALU.mult))
                em.vbar()
                va = lambda a: a.rearrange("p (h x) -> p h x", x=64)
                vq = lambda off: qkt[:, off:off + 512].rearrange("p (h x) -> p h x", x=64)
                em.v(lambda e, off=off: e.tensor_tensor(
                    vq(off)[:, :, 0:32], va(rxa[:, :])[:, :, 0:32], va(rxb[:, :])[:, :, 32:64], ALU.subtract))
                em.v(lambda e, off=off: e.tensor_tensor(
                    vq(off)[:, :, 32:64], va(rxa[:, :])[:, :, 32:64], va(rxb[:, :])[:, :, 0:32], ALU.add))
                em.v(inc(lambda e: e.engine_nop(), vd))
                em.wait("sync", vd, vd.n)
                dstA = aqA if which == 0 else akA
                em.dma("sync", dstA[tt * 128:(tt + 1) * 128, :].rearrange(
                    "p (h u) -> p h u", u=128)[:, :, 0:64],
                    qkt[:, off:off + 512].rearrange("p (h x) -> p h x", x=64), st)
    if _K.FFN1_PH < 2:
        em.wait("gpsimd", gld, gld.n)
        em.wait("sync", tr, tr.n)
        em.wait("sync", ld, ld.n)
        em.wait("sync", st, st.n)
        em.flush()
        sub.close()
        return
    em.wait("sync", st, st.n)
    for b in range(8):
        em.dmaT("sync", kfa[:, b, :], akA[:, b * 128:(b + 1) * 128], tr)
    em.wait("vector", tr, tr.n)
    for b in range(8):
        em.v(lambda e, b=b: e.memset(kfa[64:65, b, :], 1.0))
    em.v(inc(lambda e: e.engine_nop(), vd))
    em.wait("tensor", vd, vd.n)

    if _K.FFN1_PH < 3:
        em.wait("gpsimd", gld, gld.n)
        em.wait("sync", tr, tr.n)
        em.wait("sync", ld, ld.n)
        em.wait("sync", st, st.n)
        em.flush()
        sub.close()
        return
    ystore = {}
    for w in range(4):
        w0 = w * 512
        em.wait("sync", mm, mm.n)
        em.wait("sync", gd, gd.n)
        for b in range(8):
            em.dmaT("sync", qfa[:, b, :], aqA[w0:w0 + 512, b * 128:(b + 1) * 128], tr)
        if w > 0:
            em.wait("vector", tr, tr.n)
            em.wait("vector", mm, mm.n)
            for b in range(8):
                em.v(lambda e, b=b, w=w: e.tensor_scalar(
                    kprev[:, b, :], kfa[:, b, (w - 1) * 512:w * 512], b2v[:, 0:1], None, ALU.mult))
            em.v(inc(lambda e: e.engine_nop(), vd))
        em.wait("tensor", tr, tr.n)
        em.wait("tensor", vd, vd.n)
        for h in range(8):
            for qs in range(4):
                if w > 0:
                    em.wait("tensor", vd, vd.n)
                    em.t(inc(lambda e, h=h, qs=qs: e.matmul(
                        ps_pt[:, :], qfa[0:64, h, qs * 128:(qs + 1) * 128],
                        kprev[0:64, h, :], start=True, stop=True), mm))
                    em.wait("vector", mm, mm.n)
                    em.v(inc(lambda e: e.reduce_max(acc0[:, :], ps_pt[:, :], axis=AX.X), vd))
                else:
                    em.v(inc(lambda e: e.memset(acc0[:, :], -3.0e38), vd))
                em.wait("tensor", vd, vd.n)
                em.t(inc(lambda e, h=h, qs=qs, w=w: e.matmul(
                    ps_pt[:, :], qfa[0:64, h, qs * 128:(qs + 1) * 128],
                    kfa[0:64, h, w0:w0 + 512], start=True, stop=True), mm))
                em.wait("vector", mm, mm.n)
                em.v(lambda e, qs=qs: e.tensor_tensor(sqb[:, 0:512], ps_pt[:, :],
                                                      cmask[:, qs, :], ALU.add))
                em.vbar()
                em.v(inc(lambda e: e.reduce_max(acc1[:, :], sqb[:, 0:512], axis=AX.X), vd))
                em.vbar()
                em.v(lambda e: e.tensor_tensor(acc1[:, :], acc1[:, :], acc0[:, :], ALU.max))
                em.vbar()
                em.v(inc(lambda e, h=h, qs=qs: e.tensor_scalar(
                    negm[:, h * 4 + qs:h * 4 + qs + 1], acc1[:, :], -1.0, None, ALU.mult), vd))
        em.wait("tensor", vd, vd.n)
        em.t(inc(lambda e: e.transpose(ps_bc[0:32, 0:128], negm[:, :], idf[:, :]), mm))
        em.wait("vector", mm, mm.n)
        em.v(inc(lambda e: e.tensor_copy(rows32[:, :], ps_bc[0:32, 0:128]), vd))
        em.wait("gpsimd", vd, vd.n)
        for h in range(8):
            em.dma("gpsimd", qfa[64:65, h, :], rows32[h * 4:(h + 1) * 4, :], ld)
        em.wait("tensor", ld, ld.n)
        if _K.FFN1_PH < 4:
            continue
        for h in range(8):
            for kt in range(8):
                iw = kt // 4
                ktl = kt % 4
                if w == 0 and iw == 0:
                    em.wait("vector", mm, mm.n)
                    em.v(inc(lambda e, kt=kt: e.memset(PTb[:, kt, :], 0.0), vd))
                    continue
                src = kprev if iw == 0 else kfa
                coff = ktl * 128 if iw == 0 else w0 + ktl * 128
                em.wait("tensor", sd, sd.n)
                em.wait("tensor", gd, gd.n)
                em.t(inc(lambda e, h=h, src=src, coff=coff: e.matmul(
                    ps_pt[:, :], src[0:65, h, coff:coff + 128],
                    qfa[0:65, h, :], start=True, stop=True), mm))
                em.wait("scalar", mm, mm.n)
                em.s(inc(lambda e, kt=kt: e.activation(out=PTb[:, kt, :], in_=ps_pt[:, :],
                                                       func=AF.Exp), sd))
                em.wait("gpsimd", sd, sd.n)
                if iw == 0:
                    em.g(inc(lambda e, kt=kt, ktl=ktl: e.affine_select(
                        PTb[:, kt, :], PTb[:, kt, :], [[-1, 512]], ALU.is_ge, 0.0,
                        base=ktl * 128, channel_multiplier=1), gd))
                else:
                    em.g(inc(lambda e, kt=kt, ktl=ktl: e.affine_select(
                        PTb[:, kt, :], PTb[:, kt, :], [[1, 512]], ALU.is_ge, 0.0,
                        base=-ktl * 128, channel_multiplier=-1), gd))
            em.wait("tensor", gd, gd.n)
            em.wait("tensor", vd, vd.n)
            def avm(e, h=h, w=w):
                pairs = []
                for kt in range(8):
                    ktg = (w - 1) * 4 + kt if kt < 4 else w * 4 + (kt - 4)
                    if ktg >= 0:
                        pairs.append((ktg, kt))
                r = None
                for i, (ktg, kt) in enumerate(pairs):
                    r = e.matmul(ps_av[0:65, :], vvf[:, ktg, h * 66:h * 66 + 65],
                                 PTb[:, kt, :], start=(i == 0), stop=(i == len(pairs) - 1))
                return r
            em.t(inc(avm, mm))
            em.wait("vector", mm, mm.n)
            em.v(lambda e: e.reciprocal(zrow[0:1, 2, :], ps_av[64:65, :]))
            em.v(inc(lambda e: e.engine_nop(), vd))
            em.wait("tensor", vd, vd.n)
            em.t(inc(lambda e: e.matmul(ps_bc[0:64, :], onesr[0:1, 0:64], zrow[0:1, 2, :],
                                        start=True, stop=True), mm))
            em.wait("vector", mm, mm.n)
            em.v(lambda e: e.tensor_copy(rxa[0:64, :], ps_bc[0:64, :]))
            em.vbar()
            em.v(inc(lambda e, h=h: e.tensor_tensor(ofa[0:64, h, :], ps_av[0:64, :],
                                                    rxa[0:64, :], ALU.mult), vd))
            em.vbar()
        if _K.FFN1_PH < 5:
            continue
        for qs in range(4):
            tt = w * 4 + qs
            buf = qs % 2
            em.wait("sync", vd, ystore.get(("v", buf), 0))
            em.wait("sync", st, ystore.get(("s", buf), 0))
            em.dma("sync", ytile[:, buf, :], yinA[tt * 128:(tt + 1) * 128, :], ld)
            for fc in range(2):
                em.wait("tensor", vd, vd.n)
                def wom(e, qs=qs, fc=fc):
                    for h in range(8):
                        r = e.matmul(ps_o[:, :], ofa[0:64, h, qs * 128:(qs + 1) * 128],
                                     wob[0:64, h, fc * 512:(fc + 1) * 512],
                                     start=(h == 0), stop=(h == 7))
                    return r
                em.t(inc(wom, mm))
                em.wait("vector", mm, mm.n)
                em.wait("vector", ld, ld.n)
                em.wait("vector", st, ystore.get(("s", buf), 0))
                em.v(lambda e, buf=buf, fc=fc: e.tensor_tensor(
                    ytile[:, buf, fc * 512:(fc + 1) * 512],
                    ytile[:, buf, fc * 512:(fc + 1) * 512], ps_o[:, :], ALU.add))
                em.v(inc(lambda e: e.engine_nop(), vd))
            em.vbar()
            em.v(inc(lambda e, buf=buf: e.tensor_copy(ybf[:, buf, :], ytile[:, buf, :]), vd))
            ystore[("v", buf)] = vd.n
            em.wait("sync", vd, vd.n)
            em.dma("sync", y1fA[tt * 128:(tt + 1) * 128, :], ytile[:, buf, :], st)
            em.dma("sync", yaA[1 + tt * 128:1 + (tt + 1) * 128, :], ybf[:, buf, :], st)
            ystore[("s", buf)] = st.n
    em.v(lambda e: e.memset(zb[:, :], 0.0))
    em.v(inc(lambda e: e.engine_nop(), vd))
    em.wait("sync", vd, vd.n)
    em.dma("sync", yaA[0:1, :], zb[0:1, :], st)
    em.wait("sync", st, st.n)
    em.flush()
    embar = Em(P)
    barrier(P, embar)
    sub.close()

    em = Em(P)
    sub2 = _ES()
    def bsb(name, shape, dt):
        return sub2.enter_context(nc.sbuf_tensor(name, shape, dt))
    h2T = bsb(f"f1{sfx}_h2T", [128, 8, 512], BF16)
    w1b = [bsb(f"f1{sfx}_w1b{i}", [128, 8, 256], BF16) for i in range(2)]
    w2b = bsb(f"f1{sfx}_w2b", [128, 22, 1024], BF16)
    h3T = bsb(f"f1{sfx}_h3T", [128, 22, 512], BF16)
    gl1 = bsb(f"f1{sfx}_gl1", [128, 1024], F32)
    lgbc = bsb(f"f1{sfx}_lgbc", [128, 1024], F32)
    lbbc = bsb(f"f1{sfx}_lbbc", [128, 1024], F32)
    b2bc = bsb(f"f1{sfx}_b2bc", [128, 1024], F32)
    b1c = bsb(f"f1{sfx}_b1c", [128, 44], F32)
    yt2 = bsb(f"f1{sfx}_yt2", [128, 2, 1024], F32)
    ht = bsb(f"f1{sfx}_ht", [128, 2, 1024], F32)
    hnb = bsb(f"f1{sfx}_hnb", [128, 2, 1024], BF16)
    cols = bsb(f"f1{sfx}_cols", [128, 2, 8], F32)
    sq2 = bsb(f"f1{sfx}_sq2", [128, 1024], F32)
    onesr2 = bsb(f"f1{sfx}_on2", [1, 1024], F32)
    otile = bsb(f"f1{sfx}_ot", [128, 2, 512], F32)

    em.dma("gpsimd", b1c[:, :], P.aps[f"b1c_{l}"][:, :], gld)
    em.dma("gpsimd", w2b[:, :, :], P.aps[f"w2p_{l}"].rearrange("(a p) f -> p a f", p=128), gld)
    em.dma("gpsimd", onesr2[0:1, 0:128], P.aps["ones1x128"][:, :], gld)
    for name, dst in (("L1_ffn_ln_g", lgbc), ("L1_ffn_ln_b", lbbc), ("L1_ffn_b2", b2bc)):
        em.wait("gpsimd", vd, vd.n)
        em.dma("gpsimd", sq2[0:1, 0:1024],
               P.aps[name][l].rearrange("(o d) -> o d", o=1), gld)
        em.wait("tensor", gld, gld.n)
        for seg in range(2):
            em.wait("tensor", vd, vd.n)
            em.t(inc(lambda e, seg=seg: e.matmul(ps_bc[:, :], onesr2[0:1, 0:128],
                                                 sq2[0:1, seg * 512:(seg + 1) * 512],
                                                 start=True, stop=True), mm))
            em.wait("vector", mm, mm.n)
            em.v(inc(lambda e, dst=dst, seg=seg: e.tensor_copy(
                dst[:, seg * 512:(seg + 1) * 512], ps_bc[:, :]), vd))
        em.wait("gpsimd", vd, vd.n)
    if _K.FFN1_PH < 6:
        em.flush()
        sub2.close()
        return
    snaps = []
    for tt in range(NTT):
        b = tt % 2
        if tt >= 2:
            sn = snaps[tt - 2]
            em.wait("gpsimd", vd, sn[0])
            em.wait("vector", st, sn[1])
        em.dma("gpsimd", ht[:, b, 0:512], yaA[1 + tt * 128:1 + (tt + 1) * 128, 0:512], ld)
        em.dma("gpsimd", ht[:, b, 512:1024], yaA[tt * 128:(tt + 1) * 128, 512:1024], ld)
        em.wait("vector", ld, ld.n)
        em.v(inc(lambda e, b=b: e.reduce_sum(cols[:, b, 0:1], ht[:, b, :], axis=AX.X), vd))
        em.v(lambda e, b=b: e.tensor_tensor(sq2[:, :], ht[:, b, :], ht[:, b, :], ALU.mult))
        em.vbar()
        em.v(inc(lambda e, b=b: e.reduce_sum(cols[:, b, 1:2], sq2[:, :], axis=AX.X), vd))
        em.vbar()
        em.v(lambda e, b=b: e.tensor_scalar(cols[:, b, 2:3], cols[:, b, 0:1], 1.0 / D, None, ALU.mult))
        em.v(lambda e, b=b: e.tensor_scalar(cols[:, b, 3:4], cols[:, b, 1:2], 1.0 / D, None, ALU.mult))
        em.vbar()
        em.v(lambda e, b=b: e.tensor_tensor(cols[:, b, 4:5], cols[:, b, 2:3], cols[:, b, 2:3], ALU.mult))
        em.vbar()
        em.v(inc(lambda e, b=b: e.tensor_scalar(cols[:, b, 5:6], cols[:, b, 3:4], cols[:, b, 4:5],
                                                1e-5, ALU.subtract, ALU.add), vd))
        em.wait("scalar", vd, vd.n)
        em.s(inc(lambda e, b=b: e.activation(out=cols[:, b, 6:7], in_=cols[:, b, 5:6], func=AF.Sqrt), sd))
        em.wait("vector", sd, sd.n)
        em.v(lambda e, b=b: e.reciprocal(cols[:, b, 7:8], cols[:, b, 6:7]))
        em.vbar()
        em.v(lambda e, b=b: e.tensor_scalar(ht[:, b, :], ht[:, b, :], cols[:, b, 2:3],
                                            cols[:, b, 7:8], ALU.subtract, ALU.mult))
        em.vbar()
        em.v(lambda e, b=b: e.tensor_tensor(ht[:, b, :], ht[:, b, :], lgbc[:, :], ALU.mult))
        em.vbar()
        em.v(inc(lambda e, b=b: e.tensor_tensor(hnb[:, b, :], ht[:, b, :], lbbc[:, :], ALU.add), vd))
        em.wait("sync", vd, vd.n)
        em.dma("sync", h2A[tt * 128:(tt + 1) * 128, :], hnb[:, b, :], st)
        snaps.append((vd.n, st.n))
    em.wait("sync", st, st.n)
    WJ = [128] * 21 + [42]
    if _K.FFN1_PH < 7:
        em.wait("sync", st, st.n)
        em.flush()
        sub2.close()
        return
    ot_store = {0: 0, 1: 0}
    w1snap = {}
    for pan in range(4):
        p0 = pan * 512
        em.wait("sync", mm, mm.n)
        for dt_ in range(8):
            em.dmaT("sync", h2T[:, dt_, :], h2A[p0:p0 + 512, dt_ * 128:(dt_ + 1) * 128], tr)
        em.wait("tensor", tr, tr.n)
        for j in range(22):
            wj = WJ[j]
            buf = j % 2
            key = (pan, j - 2) if j >= 2 else (pan - 1, 20 + j)
            if key in w1snap:
                em.wait("gpsimd", mm, w1snap[key])
            em.dma("gpsimd", w1b[buf][:, :, 0:2 * wj],
                   P.aps[f"w1p_{l}"].rearrange("(a p) f -> p a f", p=128)[:, :, 256 * j:256 * j + 2 * wj], gld)
            em.wait("tensor", gld, gld.n)
            for ag in range(2):
                pu = ps_a if ag == 0 else ps_b
                em.wait("tensor", sd, sd.n)
                em.wait("tensor", vd, vd.n)
                def upm(e, j=j, wj=wj, buf=buf, ag=ag, pu=pu):
                    for kt in range(8):
                        r = e.matmul(pu[0:wj, :], w1b[buf][:, kt, ag * wj:(ag + 1) * wj],
                                     h2T[:, kt, :], start=(kt == 0), stop=(kt == 7))
                    return r
                em.t(inc(upm, mm))
                em.wait("scalar", mm, mm.n)
                if ag == 1:
                    em.s(inc(lambda e, j=j, wj=wj, pu=pu: e.activation(
                        out=h3T[0:wj, j, :], in_=pu[0:wj, :], func=AF.Gelu,
                        bias=b1c[0:wj, 2 * j + 1:2 * j + 2]), sd))
                else:
                    em.wait("vector", mm, mm.n)
                    em.v(inc(lambda e, j=j, wj=wj, pu=pu: e.tensor_scalar(
                        gl1[0:wj, 0:512], pu[0:wj, :], b1c[0:wj, 2 * j:2 * j + 1],
                        None, ALU.add), vd))
            w1snap[(pan, j)] = mm.n
            em.wait("vector", sd, sd.n)
            em.v(inc(lambda e, j=j, wj=wj: e.tensor_tensor(
                h3T[0:wj, j, :], gl1[0:wj, 0:512], h3T[0:wj, j, :], ALU.mult), vd))
            em.vbar()
        for t in range(4):
            tt = pan * 4 + t
            buf = t % 2
            em.wait("sync", vd, ot_store.get(("v", buf), 0))
            em.wait("sync", st, ot_store.get(("s", buf), 0))
            em.dma("sync", yt2[:, buf, :], y1fA[tt * 128:(tt + 1) * 128, :], ld)
            for fc in range(2):
                em.wait("tensor", sd, sd.n)
                em.wait("tensor", vd, vd.n)
                def dnm(e, t=t, fc=fc):
                    first = True
                    for j in range(22):
                        wj = WJ[j]
                        r = e.matmul(ps_o[:, :], h3T[0:wj, j, t * 128:(t + 1) * 128],
                                     w2b[0:wj, j, fc * 512:(fc + 1) * 512],
                                     start=first, stop=(j == 21))
                        first = False
                    return r
                em.t(inc(dnm, mm))
                em.wait("vector", mm, mm.n)
                em.wait("vector", ld, ld.n)
                em.wait("vector", st, ot_store.get(("s", buf), 0))
                em.v(lambda e, buf=buf, fc=fc: e.tensor_tensor(
                    sq2[:, 0:512], ps_o[:, :], b2bc[:, fc * 512:(fc + 1) * 512], ALU.add))
                em.vbar()
                em.v(inc(lambda e, buf=buf, fc=fc: e.tensor_tensor(
                    yt2[:, buf, fc * 512:(fc + 1) * 512], yt2[:, buf, fc * 512:(fc + 1) * 512],
                    sq2[:, 0:512], ALU.add), vd))
            ot_store[("v", buf)] = vd.n
            em.wait("sync", vd, vd.n)
            em.dma("sync", outA[tt * 128:(tt + 1) * 128, :], yt2[:, buf, :], st)
            if l == 0:
                em.vbar()
                em.v(inc(lambda e, buf=buf: e.tensor_copy(hnb[:, buf, :], yt2[:, buf, :]), vd))
                em.wait("sync", vd, vd.n)
                em.dma("sync", P.aps["y1b_0"][1 + tt * 128:1 + (tt + 1) * 128, :], hnb[:, buf, :], st)
            ot_store[("s", buf)] = st.n
    if l == 0:
        em.v(lambda e: e.memset(hnb[0:1, 0, :], 0.0))
        em.v(inc(lambda e: e.engine_nop(), vd))
        em.wait("sync", vd, vd.n)
        em.dma("sync", P.aps["y1b_0"][0:1, :], hnb[0:1, 0, :], st)
    em.wait("sync", st, st.n)
    em.flush()
    sub2.close()


def emit_ffn1a(P):
    emit_ffn1(P, 0)


def emit_ffn1b(P):
    emit_ffn1(P, 1)


STAGE_EMIT = {
    "ln0": emit_ln0,
    "att0": emit_att0,
    "att1": emit_att1,
    "ffn1a": emit_ffn1a,
    "ffn1b": emit_ffn1b,
    "ffn0": emit_ffn0,
    "comb0": emit_comb0,
    "comb1": emit_comb1,
}


def build_program(stages, weight_specs):
    nc = bass.Bass("TRN2", target_bir_lowering=False)
    produced = set()
    for s in stages:
        produced.update(STAGE_OUTS[s])
        produced.update(STAGE_INTERNAL.get(s, []))
    consumed = set()
    for s in stages:
        consumed.update(STAGE_INS[s])

    aps = {}
    ext_ins, ext_outs = [], []
    if "x" in consumed:
        aps["x"] = nc.dram_tensor("x", [NT, D], F32, kind="ExternalInput").ap()
        ext_ins.append("x")
    final_outs = set(FORCE_OUTS)
    for s in stages:
        for o in STAGE_OUTS[s] + STAGE_INTERNAL.get(s, []):
            consumers_outside = any(o in STAGE_INS[s2] for s2 in STAGE_INS if s2 not in stages)
            if o == "yout" or consumers_outside:
                final_outs.add(o)
    for name, (shape, dt) in DRAM_ACTS.items():
        if name in aps:
            continue
        used = (name in consumed) or (name in produced)
        if not used:
            continue
        if name in produced:
            kind = "ExternalOutput" if name in final_outs else "Internal"
        else:
            kind = "ExternalInput"
        aps[name] = nc.dram_tensor(name, list(shape), dt, kind=kind).ap()
        if kind == "ExternalInput":
            ext_ins.append(name)
        elif kind == "ExternalOutput":
            ext_outs.append(name)
    wnames = []
    for s in stages:
        for w in STAGE_WEIGHTS[s]:
            if w not in wnames:
                wnames.append(w)
    for w in wnames:
        shape, npdt = weight_specs[w]
        aps[w] = nc.dram_tensor(w, list(shape), F32, kind="ExternalInput").ap()
        ext_ins.append(w)

    with ExitStack() as stack:
        block = stack.enter_context(nc.Block())
        P = Prog(nc, block, stack, aps)
        for i, s in enumerate(stages):
            P.stage_stack = ExitStack()
            STAGE_EMIT[s](P)
            if i < len(stages) - 1:
                em = Em(P)
                barrier(P, em)
            P.stage_stack.close()
            P.stage_stack = None
    return nc, ext_ins, ext_outs


FORCE_OUTS = set()
LAUNCH_PLAN = [["ln0"], ["att0"], ["ffn0"], ["comb0"], ["att1"], ["ffn1a"],
               ["ffn1b"], ["comb1"]]


def _derived_weights(inputs):
    d = dict(inputs)
    d.update(_host_consts())
    d["l0_gn_pa"] = np.ascontiguousarray(
        np.asarray(inputs["L0_attn_gn_weight"], np.float32).reshape(4, 128).T)
    rw_ = np.asarray(inputs["L1_attn_rms_w"], np.float32)
    d["l1_rmsw_col"] = np.ascontiguousarray(np.concatenate([rw_, rw_])[:, None])
    wo_ = np.asarray(inputs["L1_attn_Wo"], np.float32).reshape(16, 64, 1024)
    d["l1_wo_pad"] = np.ascontiguousarray(
        np.concatenate([np.pad(wo_[h], ((0, 64), (0, 0)))[None] for h in range(16)], 0)
    ).reshape(2048, 1024)
    HD = 32
    inv = 1.0 / (10000.0 ** (np.arange(0, HD, 2, dtype=np.float64) / HD))
    t_ = np.arange(NT, dtype=np.float64)
    fr = np.outer(t_, inv)
    cos1 = np.cos(fr); sin1 = np.sin(fr)
    cosd = np.concatenate([cos1, cos1], 1)
    sind = np.concatenate([sin1, sin1], 1)
    sc = 32.0 ** -0.5
    d["l1_cosq"] = np.ascontiguousarray((np.tile(cosd, (1, 32)) * sc).astype(np.float32))
    d["l1_sinq"] = np.ascontiguousarray((np.tile(sind, (1, 32)) * sc).astype(np.float32))
    d["l1_cosk"] = np.ascontiguousarray(np.tile(cosd, (1, 16)).astype(np.float32))
    d["l1_sink"] = np.ascontiguousarray(np.tile(sind, (1, 16)).astype(np.float32))
    DH, WIN, QKS = 64, 512, 8.0
    half_idx = np.arange(0, DH, 2, dtype=np.float64)
    invf = 1.0 / (10000.0 ** (half_idx / DH))
    fr2 = np.outer(t_, invf)
    fr2 = np.concatenate([fr2, fr2], 1)
    based = (half_idx + 0.4 * DH) / (1.4 * DH)
    based2 = np.concatenate([based, based])
    powq = (t_ % 512) / 256.0
    scq = based2[None, :] ** powq[:, None]
    cq = np.cos(fr2) * scq * QKS
    sq_ = np.sin(fr2) * scq * QKS
    ck = np.cos(fr2) / scq
    sk = np.sin(fr2) / scq
    d["l1f_cosq"] = np.ascontiguousarray(np.tile(cq, (1, 8)).astype(np.float32))
    d["l1f_sinq"] = np.ascontiguousarray(np.tile(sq_, (1, 8)).astype(np.float32))
    d["l1f_cosk"] = np.ascontiguousarray(np.tile(ck, (1, 8)).astype(np.float32))
    d["l1f_sink"] = np.ascontiguousarray(np.tile(sk, (1, 8)).astype(np.float32))
    b2c_ = np.zeros((128, 1), np.float32)
    b2c_[0:64, 0] = (based2 ** 2.0)
    b2c_[64, 0] = 1.0
    d["l1f_base2"] = b2c_
    mc = np.zeros((128, 12), np.float32)
    pcol = np.arange(128, dtype=np.float32)
    for qs in range(4):
        mc[:, qs] = pcol + qs * 128
        mc[:, 4 + qs] = pcol + qs * 128 + 1
    mc[:, 8] = 512.0
    mc[:, 9] = 0.0
    d["l1f_mcols"] = mc
    cm = np.zeros((128, 4, 512), np.float32)
    for qs in range(4):
        ii = qs * 128 + np.arange(128)[:, None]
        jj = np.arange(512)[None, :]
        cm[:, qs, :] = np.where(jj <= ii, 0.0, -1.0e30)
    d["l1f_cmask"] = np.ascontiguousarray(cm.reshape(128, 2048))
    for l in range(2):
        d[f"l1f_qs_til_{l}"] = np.ascontiguousarray(
            np.tile(np.asarray(inputs["L1_ffn_q_scale"][l], np.float32), 8)[None, :])
        d[f"l1f_ks_til_{l}"] = np.ascontiguousarray(
            np.tile(np.asarray(inputs["L1_ffn_k_scale"][l], np.float32), 8)[None, :])
    HID = 2730
    for l in range(2):
        wo2 = np.asarray(inputs["L1_ffn_wo"][l], np.float32).reshape(8, 64, 1024)
        d[f"l1f_wo_pad_{l}"] = np.ascontiguousarray(
            np.concatenate([np.pad(wo2[h], ((0, 64), (0, 0)))[None] for h in range(8)], 0)
        ).reshape(1024, 1024)
        w2_ = np.asarray(inputs["L1_ffn_w2"][l], np.float32)
        w2p = np.zeros((22 * 128, 1024), np.float32)
        w2p[0:2730] = 0
        for j in range(22):
            wj = 128 if j < 21 else 42
            w2p[j * 128:j * 128 + wj] = w2_[j * 128:j * 128 + wj]
        d[f"w2p_{l}"] = np.ascontiguousarray(w2p)
        b1_ = np.asarray(inputs["L1_ffn_b1"][l], np.float32)
        b1c_ = np.zeros((128, 44), np.float32)
        for j in range(22):
            wj = 128 if j < 21 else 42
            b1c_[0:wj, 2 * j] = b1_[j * 128:j * 128 + wj]
            b1c_[0:wj, 2 * j + 1] = b1_[HID + j * 128:HID + j * 128 + wj]
        d[f"b1c_{l}"] = b1c_
    HID = 2730
    for l in range(2):
        w1 = np.asarray(inputs["L1_ffn_w1"][l])
        b1 = np.asarray(inputs["L1_ffn_b1"][l])
        cols = []
        for j in range(22):
            a0, a1 = j * 128, min((j + 1) * 128, HID)
            cols.append(np.arange(a0, a1))
            cols.append(np.arange(HID + a0, HID + a1))
        perm = np.concatenate(cols)
        d[f"w1p_{l}"] = np.ascontiguousarray(w1[:, perm])
        d[f"b1p_{l}"] = np.ascontiguousarray(b1[perm][None, :])
    return d


def kernel(**inputs):
    full = _derived_weights(inputs)
    x_full = np.asarray(inputs["L0_attn_x"], dtype=np.float32)

    weight_specs = {}
    for k, v in full.items():
        v = np.asarray(v)
        weight_specs[k] = (v.shape, v.dtype)

    outs = [dict() for _ in range(8)]
    for stages in LAUNCH_PLAN:
        nc, ext_ins, ext_outs = build_program(stages, weight_specs)
        in_maps = []
        for c in range(8):
            m = {}
            for name in ext_ins:
                if name == "x":
                    m["x"] = np.ascontiguousarray(x_full[c])
                elif name in DRAM_ACTS:
                    m[name] = outs[c][name]
                else:
                    m[name] = np.ascontiguousarray(np.asarray(full[name], dtype=np.float32))
            in_maps.append(m)
        res = run_bass_kernel_spmd(nc, in_maps, core_ids=list(range(8)))
        for c in range(8):
            for name in ext_outs:
                outs[c][name] = res.results[c][name]
    return np.stack([outs[c]["yout"] for c in range(8)], axis=0)


# version 03 (final)
# speedup vs baseline: 1.0029x; 1.0029x over previous
import math
from contextlib import ExitStack

import numpy as np
import ml_dtypes

import concourse.bass as bass
import concourse.mybir as mybir
from concourse.bass_utils import run_bass_kernel_spmd

F32 = mybir.dt.float32
BF16 = mybir.dt.bfloat16
AF = mybir.ActivationFunctionType
ALU = mybir.AluOpType
AX = mybir.AxisListType
BF = ml_dtypes.bfloat16

NT = 2048
D = 1024
NTT = NT // 128


def _host_consts():
    c = {}
    c["idf32"] = np.eye(128, dtype=np.float32)
    c["ones1x128"] = np.ones((1, 128), dtype=np.float32)
    c["sel16"] = np.kron(np.eye(16, dtype=np.float32), np.ones((1, 128), np.float32))

    H0, CH = 4, 128
    gam = 1.0 - np.exp2(-5.0 - np.arange(H0, dtype=np.float64))
    i = np.arange(CH, dtype=np.float64)
    rel = i[:, None] - i[None, :]
    Dm = np.where(rel[None] >= 0, gam[:, None, None] ** rel[None], 0.0) / 16.0
    c["l0_DT"] = np.ascontiguousarray(np.transpose(Dm, (0, 2, 1))).astype(np.float32)
    cq = (gam[:, None] ** (i[None, :] + 1) / 16.0).astype(np.float32)
    c["l0_crossq_bc"] = np.ascontiguousarray(np.broadcast_to(cq[None], (128, H0, CH)))
    c["l0_crossk"] = np.ascontiguousarray(
        (gam[:, None] ** (CH - 1 - i[None, :])).astype(np.float32).T)
    d0 = 256
    inv = 1.0 / (10000.0 ** (np.arange(0, d0, 2, dtype=np.float64) / d0))
    t = np.arange(NT, dtype=np.float64)
    fr = np.outer(inv, t)
    c["l0_cosT"] = np.cos(fr).astype(np.float32)
    c["l0_sinT"] = np.sin(fr).astype(np.float32)
    return c


DRAM_ACTS = {
    "xn0b": ((NT, D), BF16),
    "s0":   ((NT, D), F32),
    "f0":   ((NT, D), F32),
    "x1":   ((NT, D), F32),
    "xn1f": ((NT, D), F32),
    "xn1b": ((NT + 1, D), BF16),
    "s1":   ((NT, D), F32),
    "l1q":  ((NT, 2048), BF16),
    "l1k":  ((NT, 1024), BF16),
    "ymid": ((NT, D), F32),
    "y1b_0": ((NT + 1, D), BF16),
    "h2_0": ((NT, D), BF16),
    "aq_0": ((NT, 1024), BF16),
    "ak_0": ((NT, 1024), BF16),
    "ya_0": ((NT + 1, D), BF16),
    "y1f_0": ((NT, D), F32),
    "h2_1": ((NT, D), BF16),
    "aq_1": ((NT, 1024), BF16),
    "ak_1": ((NT, 1024), BF16),
    "ya_1": ((NT + 1, D), BF16),
    "y1f_1": ((NT, D), F32),
    "f1":   ((NT, D), F32),
    "yout": ((NT, D), F32),
}

STAGE_WEIGHTS = {
    "ln0": [],
    "att0": ["L0_attn_Wq", "L0_attn_Wk", "L0_attn_Wv", "L0_attn_Wg", "L0_attn_Wo",
             "l0_gn_pa", "l0_DT", "l0_crossq_bc", "l0_crossk",
             "l0_cosT", "l0_sinT", "idf32", "ones1x128"],
    "ffn0": ["L0_ffn_router_w", "L0_ffn_wg_r", "L0_ffn_wu_r", "L0_ffn_wd_r",
             "L0_ffn_wg_s", "L0_ffn_wu_s", "L0_ffn_wd_s", "idf32", "sel16"],
    "comb0": [],
    "att1": ["L1_attn_Wq", "L1_attn_Wk", "L1_attn_Wv", "l1_wo_pad",
             "L1_attn_lambda_q1", "L1_attn_lambda_k1", "L1_attn_lambda_q2",
             "L1_attn_lambda_k2", "l1_rmsw_col", "l1_cosq", "l1_sinq",
             "l1_cosk", "l1_sink", "ones1x128"],
    "ffn1a": ["L1_ffn_wqkv", "L1_ffn_q_scale", "L1_ffn_k_scale", "l1f_wo_pad_0",
              "L1_ffn_ln_g", "L1_ffn_ln_b", "w1p_0", "b1c_0", "w2p_0",
              "L1_ffn_b2", "l1f_mcols", "l1f_base2", "l1f_cosq", "l1f_sinq",
              "l1f_cosk", "l1f_sink", "idf32", "ones1x128", "l1f_cmask",
              "l1f_qs_til_0", "l1f_ks_til_0"],
    "ffn1b": ["L1_ffn_wqkv", "L1_ffn_q_scale", "L1_ffn_k_scale", "l1f_wo_pad_1",
              "L1_ffn_ln_g", "L1_ffn_ln_b", "w1p_1", "b1c_1", "w2p_1",
              "L1_ffn_b2", "l1f_mcols", "l1f_base2", "l1f_cosq", "l1f_sinq",
              "l1f_cosk", "l1f_sink", "idf32", "ones1x128", "l1f_cmask",
              "l1f_qs_til_1", "l1f_ks_til_1"],
    "comb1": [],
}
STAGE_INS = {
    "ln0": ["x"],
    "att0": ["xn0b"],
    "ffn0": ["xn0b"],
    "comb0": ["x", "s0", "f0"],
    "att1": ["xn1b"],
    "ffn1a": ["xn1f", "xn1b"],
    "ffn1b": ["ymid", "y1b_0"],
    "comb1": ["x1", "s1", "f1"],
}
STAGE_OUTS = {
    "ln0": ["xn0b"],
    "att0": ["s0"],
    "ffn0": ["f0"],
    "comb0": ["x1", "xn1f", "xn1b"],
    "att1": ["s1"],
    "ffn1a": ["ymid", "y1b_0"],
    "ffn1b": ["f1"],
    "comb1": ["yout"],
}
STAGE_INTERNAL = {
    "att1": ["l1q", "l1k"],
    "ffn1a": ["h2_0", "aq_0", "ak_0", "ya_0", "y1f_0"],
    "ffn1b": ["h2_1", "aq_1", "ak_1", "ya_1", "y1f_1"],
}


class Ctr:
    __slots__ = ("sem", "n")
    def __init__(self, sem):
        self.sem = sem
        self.n = 0
    def bump(self, k=1):
        self.n += k
        return self.n


ENGS = ("tensor", "vector", "scalar", "gpsimd", "sync")


class Em:
    def __init__(self, P):
        self.P = P
        self.qs = {e: [] for e in ENGS}

    def _push(self, eng, fn):
        self.qs[eng].append(fn)

    def t(self, fn): self._push("tensor", fn)
    def v(self, fn): self._push("vector", fn)
    def s(self, fn): self._push("scalar", fn)
    def g(self, fn): self._push("gpsimd", fn)
    def y(self, fn): self._push("sync", fn)

    def wait(self, eng, ctr, val=None):
        v = ctr.n if val is None else val
        if v > 0:
            self._push(eng, lambda e, s=ctr.sem, v=v: e.wait_ge(s, v))

    def dma(self, eng, out, in_, ctr, **kw):
        ctr.bump(16)
        self._push(eng, lambda e, o=out, i=in_, kw=kw, s=ctr.sem:
                   e.dma_start(out=o, in_=i, **kw).then_inc(s, 16))

    def dmaT(self, eng, out, in_, ctr):
        ctr.bump(16)
        self._push(eng, lambda e, o=out, i=in_, s=ctr.sem:
                   e.dma_start_transpose(o, i).then_inc(s, 16))

    def flush(self):
        for name, q in self.qs.items():
            if q:
                getattr(self.P.block, name)(lambda eng, q=tuple(q): [f(eng) for f in q])
        self.qs = {e: [] for e in ENGS}

    def _selfsync(self, eng, ctr):
        n = ctr.bump(1)
        self._push(eng, lambda e, s=ctr.sem: e.engine_nop().then_inc(s, 1))
        self._push(eng, lambda e, s=ctr.sem, n=n: e.wait_ge(s, n))

    def vbar(self): self._selfsync("vector", self.P.vself)
    def sbar(self): self._selfsync("scalar", self.P.sself)
    def gbar(self): self._selfsync("gpsimd", self.P.gself)


def inc(fn, ctr, k=1):
    ctr.bump(k)
    return lambda e, fn=fn, s=ctr.sem, n=k: fn(e).then_inc(s, n)


def barrier(P, em):
    P.bar_k += 1
    tgt = 5 * P.bar_k
    for eng in ENGS:
        em._push(eng, lambda e, s=P.bar.sem: e.sem_inc(s, 1))
        em._push(eng, lambda e, s=P.bar.sem, t=tgt: e.wait_ge(s, t))
    em.flush()


class Prog:
    def __init__(self, nc, block, stack, aps):
        self.nc = nc
        self.block = block
        self.stack = stack
        self.aps = aps
        self.bar_k = 0
        self.bar = self.ctr("bar")
        self.vself = self.ctr("vself")
        self.sself = self.ctr("sself")
        self.gself = self.ctr("gself")

    def ctr(self, name):
        sem = self.stack.enter_context(self.nc.semaphore(f"c{getattr(self, '_ctr_i', 0)}_{name}"))
        self._ctr_i = getattr(self, "_ctr_i", 0) + 1
        return Ctr(sem)

    def sb(self, name, shape, dtype):
        st = getattr(self, "stage_stack", None) or self.stack
        return st.enter_context(self.nc.sbuf_tensor(name, shape, dtype))

    def ps(self, name, shape, dtype=F32):
        st = getattr(self, "stage_stack", None) or self.stack
        return st.enter_context(self.nc.psum_tensor(name, shape, dtype))


def emit_ln(P, src_name, outb_name, outf_name=None, extra_row0=False, src_sbuf=None,
            tag="ln"):
    nc = P.nc
    em = Em(P)
    x = P.aps[src_name] if src_name else None
    ob = P.aps[outb_name]
    of = P.aps[outf_name] if outf_name else None

    xt = P.sb(f"{tag}_x", [128, 3, D], F32)
    trash = P.sb(f"{tag}_tr", [128, D], BF16)
    cols = P.sb(f"{tag}_c", [128, 3, 8], F32)
    sqc = P.sb(f"{tag}_sq", [128, 3, 4], F32)
    sqbuf = P.sb(f"{tag}_sqb", [128, D], F32)
    onb = P.sb(f"{tag}_ob", [128, 3, D], BF16)
    onf = P.sb(f"{tag}_of", [128, 3, D], F32) if of is not None else None
    zrow = P.sb(f"{tag}_z", [1, D], BF16)

    ld, vd, sd, st = P.ctr("ld"), P.ctr("vd"), P.ctr("sd"), P.ctr("st")
    row0 = 1 if extra_row0 else 0
    if extra_row0:
        em.v(lambda e: e.memset(zrow[:, :], 0.0))
        em.v(inc(lambda e: e.engine_nop(), vd))
        em.wait("sync", vd, vd.n)
        em.dma("sync", ob[0:1, :], zrow[0:1, :], st)

    snaps = []
    for tt in range(NTT):
        b = tt % 3
        if tt >= 3:
            sn = snaps[tt - 3]
            em.wait("sync", vd, sn["vd"])
            em.wait("sync", sd, sn["sd"])
            em.wait("vector", st, sn["st"])
            em.wait("scalar", vd, sn["vd"])
        em.dma("sync", xt[:, b, :], x[tt * 128:(tt + 1) * 128, :], ld)
        em.wait("vector", ld, 16 * (tt + 1))
        em.v(inc(lambda e, b=b: e.reduce_sum(cols[:, b, 0:1], xt[:, b, :], axis=AX.X), vd))
        em.v(lambda e, b=b: e.tensor_tensor(sqbuf[:, :], xt[:, b, :], xt[:, b, :], ALU.mult))
        em.vbar()
        em.v(inc(lambda e, b=b: e.reduce_sum(sqc[:, b, 0:1], sqbuf[:, :], axis=AX.X), vd))
        em.vbar()
        em.v(lambda e, b=b: e.tensor_scalar(cols[:, b, 3:4], sqc[:, b, 0:1], 1.0 / D, None, ALU.mult))
        em.v(lambda e, b=b: e.tensor_scalar(cols[:, b, 2:3], cols[:, b, 0:1], 1.0 / D, None, ALU.mult))
        em.vbar()
        em.v(lambda e, b=b: e.tensor_tensor(cols[:, b, 4:5], cols[:, b, 2:3], cols[:, b, 2:3], ALU.mult))
        em.vbar()
        em.v(inc(lambda e, b=b: e.tensor_scalar(cols[:, b, 5:6], cols[:, b, 3:4], cols[:, b, 4:5],
                                                1e-5, ALU.subtract, ALU.add), vd))
        em.wait("scalar", vd, vd.n)
        em.s(inc(lambda e, b=b: e.activation(out=cols[:, b, 6:7], in_=cols[:, b, 5:6], func=AF.Sqrt), sd))
        em.wait("vector", sd, sd.n)
        em.v(lambda e, b=b: e.reciprocal(cols[:, b, 7:8], cols[:, b, 6:7]))
        em.vbar()
        em.v(lambda e, b=b: e.tensor_scalar(onb[:, b, :], xt[:, b, :], cols[:, b, 2:3],
                                            cols[:, b, 7:8], ALU.subtract, ALU.mult))
        if onf is not None:
            em.v(lambda e, b=b: e.tensor_scalar(onf[:, b, :], xt[:, b, :], cols[:, b, 2:3],
                                                cols[:, b, 7:8], ALU.subtract, ALU.mult))
        em.v(inc(lambda e: e.engine_nop(), vd))
        em.wait("sync", vd, vd.n)
        em.dma("sync", ob[row0 + tt * 128: row0 + (tt + 1) * 128, :], onb[:, b, :], st)
        if of is not None:
            em.dma("sync", of[tt * 128:(tt + 1) * 128, :], onf[:, b, :], st)
        snaps.append({"vd": vd.n, "sd": sd.n, "st": st.n})
    em.wait("sync", st, st.n)
    em.flush()


def emit_ln0(P):
    emit_ln(P, "x", "xn0b", tag="ln0")


def emit_combine(P, x_name, s_name, f_name, out_name, lnb_name=None, lnf_name=None, tag="cmb"):
    nc = P.nc
    em = Em(P)
    xA, sA, fA = P.aps[x_name], P.aps[s_name], P.aps[f_name]
    oA = P.aps[out_name]
    lnbA = P.aps[lnb_name] if lnb_name else None
    lnfA = P.aps[lnf_name] if lnf_name else None

    NB = 3
    xt = P.sb(f"{tag}_x", [128, NB, D], F32)
    st_ = P.sb(f"{tag}_s", [128, NB, D], F32)
    ft = P.sb(f"{tag}_f", [128, NB, D], F32)
    gt = P.sb(f"{tag}_g", [128, NB, D], F32)
    x1 = P.sb(f"{tag}_o", [128, NB, D], F32)
    cols = P.sb(f"{tag}_c", [128, NB, 8], F32)
    sqc = P.sb(f"{tag}_sq", [128, NB, 4], F32)
    sqbuf = P.sb(f"{tag}_sqb", [128, D], F32)
    onb = P.sb(f"{tag}_ob", [128, NB, D], BF16) if lnbA is not None else None
    onf = P.sb(f"{tag}_of", [128, NB, D], F32) if lnfA is not None else None
    trash = P.sb(f"{tag}_tr", [128, D], BF16)
    zrow = P.sb(f"{tag}_z", [1, D], BF16)

    ld, vd, sd, st = P.ctr("ld"), P.ctr("vd"), P.ctr("sd"), P.ctr("st")
    if lnbA is not None:
        em.v(lambda e: e.memset(zrow[:, :], 0.0))
        em.v(inc(lambda e: e.engine_nop(), vd))
        em.wait("sync", vd, vd.n)
        em.dma("sync", lnbA[0:1, :], zrow[0:1, :], st)

    snaps = []
    for tt in range(NTT):
        b = tt % NB
        t0 = tt * 128
        if tt >= NB:
            sn = snaps[tt - NB]
            em.wait("sync", vd, sn["vd"])
            em.wait("sync", sd, sn["sd"])
            em.wait("vector", st, sn["st"])
            em.wait("scalar", vd, sn["vd"])
        em.dma("sync", xt[:, b, :], xA[t0:t0 + 128, :], ld)
        em.dma("sync", st_[:, b, :], sA[t0:t0 + 128, :], ld)
        em.dma("sync", ft[:, b, :], fA[t0:t0 + 128, :], ld)
        em.wait("scalar", ld, ld.n)
        em.s(inc(lambda e, b=b: e.activation(out=gt[:, b, :], in_=ft[:, b, :], func=AF.Silu), sd))
        em.wait("vector", sd, sd.n)
        em.wait("vector", ld, ld.n)
        em.v(lambda e, b=b: e.tensor_tensor(gt[:, b, :], gt[:, b, :], st_[:, b, :], ALU.mult))
        em.vbar()
        em.v(inc(lambda e, b=b: e.tensor_tensor(x1[:, b, :], xt[:, b, :], gt[:, b, :], ALU.add), vd))
        em.vbar()
        em.wait("sync", vd, vd.n)
        em.dma("sync", oA[t0:t0 + 128, :], x1[:, b, :], st)
        if lnbA is not None:
            em.v(inc(lambda e, b=b: e.reduce_sum(cols[:, b, 0:1], x1[:, b, :], axis=AX.X), vd))
            em.v(lambda e, b=b: e.tensor_tensor(sqbuf[:, :], x1[:, b, :], x1[:, b, :], ALU.mult))
            em.vbar()
            em.v(inc(lambda e, b=b: e.reduce_sum(sqc[:, b, 0:1], sqbuf[:, :], axis=AX.X), vd))
            em.vbar()
            em.v(lambda e, b=b: e.tensor_scalar(cols[:, b, 3:4], sqc[:, b, 0:1], 1.0 / D, None, ALU.mult))
            em.v(lambda e, b=b: e.tensor_scalar(cols[:, b, 2:3], cols[:, b, 0:1], 1.0 / D, None, ALU.mult))
            em.vbar()
            em.v(lambda e, b=b: e.tensor_tensor(cols[:, b, 4:5], cols[:, b, 2:3], cols[:, b, 2:3], ALU.mult))
            em.vbar()
            em.v(inc(lambda e, b=b: e.tensor_scalar(cols[:, b, 5:6], cols[:, b, 3:4], cols[:, b, 4:5],
                                                    1e-5, ALU.subtract, ALU.add), vd))
            em.wait("scalar", vd, vd.n)
            em.s(inc(lambda e, b=b: e.activation(out=cols[:, b, 6:7], in_=cols[:, b, 5:6], func=AF.Sqrt), sd))
            em.wait("vector", sd, sd.n)
            em.v(lambda e, b=b: e.reciprocal(cols[:, b, 7:8], cols[:, b, 6:7]))
            em.vbar()
            em.v(lambda e, b=b: e.tensor_scalar(onb[:, b, :], x1[:, b, :], cols[:, b, 2:3],
                                                cols[:, b, 7:8], ALU.subtract, ALU.mult))
            if onf is not None:
                em.v(lambda e, b=b: e.tensor_scalar(onf[:, b, :], x1[:, b, :], cols[:, b, 2:3],
                                                    cols[:, b, 7:8], ALU.subtract, ALU.mult))
            em.v(inc(lambda e: e.engine_nop(), vd))
            em.wait("sync", vd, vd.n)
            em.dma("sync", lnbA[1 + t0: 1 + t0 + 128, :], onb[:, b, :], st)
            if onf is not None:
                em.dma("sync", lnfA[t0:t0 + 128, :], onf[:, b, :], st)
        snaps.append({"vd": vd.n, "sd": sd.n, "st": st.n})
    em.wait("sync", st, st.n)
    em.flush()


def emit_comb0(P):
    emit_combine(P, "x", "s0", "f0", "x1", lnb_name="xn1b", lnf_name="xn1f", tag="c0")


def emit_comb1(P):
    emit_combine(P, "x1", "s1", "f1", "yout", tag="c1")


def emit_ffn0(P):
    nc = P.nc
    em = Em(P)
    xn0b = P.aps["xn0b"]
    NE, MR, MS = 16, 256, 512
    NQ, QT = 4, 512
    QTT = QT // 128

    xn0T = P.sb("f0_xnT", [128, 8, NT], BF16)
    rw = P.sb("f0_rw", [128, 8, 16], BF16)
    idf = P.sb("f0_id", [128, 128], F32)
    sel = P.sb("f0_sel", [16, NE * 128], BF16)
    combT = P.sb("f0_combT", [16, NT], BF16)
    ecol = P.sb("f0_e", [128, 16], F32)
    emask = P.sb("f0_em", [128, 16], F32)
    rcols = P.sb("f0_rc", [128, 8], F32)
    comb = P.sb("f0_comb", [128, 16], F32)

    wds = P.sb("f0_wds", [128, 8, D], BF16)
    wdr = P.sb("f0_wdr", [128, 32, D], BF16)
    wgq = [P.sb(f"f0_wg{i}", [128, 8, MS], BF16) for i in range(2)]
    wuq = [P.sb(f"f0_wu{i}", [128, 8, MS], BF16) for i in range(2)]
    hT = P.sb("f0_hT", [128, 32, QT], BF16)
    hsT = P.sb("f0_hsT", [128, 8, QT], BF16)
    gtmp = P.sb("f0_gt", [128, 2, 512], BF16)
    otmp = P.sb("f0_ot", [128, 2, 512], F32)

    ld, tr, mm, vd, sd, st, gld = (P.ctr("ld"), P.ctr("tr"), P.ctr("mm"), P.ctr("vd"),
                                   P.ctr("sd"), P.ctr("st"), P.ctr("gl"))

    ps_l = P.ps("f0_psl", [128, 16])
    ps_t = P.ps("f0_pst", [16, 128])
    ps_b = P.ps("f0_psb", [128, 512])
    ps_up = [P.ps(f"f0_up{i}", [128, 512]) for i in range(2)]
    ps_dn = [P.ps(f"f0_dn{i}", [128, 512]) for i in range(2)]

    for dt in range(8):
        em.dmaT("sync", xn0T[:, dt, :], xn0b[:, dt * 128:(dt + 1) * 128], tr)
    em.dma("gpsimd", rw[:, :, :], P.aps["L0_ffn_router_w"].rearrange("(a p) e -> p a e", p=128), gld)
    em.dma("gpsimd", idf[:, :], P.aps["idf32"][:, :], gld)
    em.dma("gpsimd", sel[:, :], P.aps["sel16"][:, :], gld)
    em.dma("gpsimd", wds[:, :, :], P.aps["L0_ffn_wd_s"].rearrange("n (a p) d -> p (n a) d", p=128), gld)
    em.dma("gpsimd", wdr[:, :, :], P.aps["L0_ffn_wd_r"].rearrange("n (a p) d -> p (n a) d", p=128), gld)

    em.wait("tensor", tr, tr.n)
    em.wait("tensor", gld, gld.n)
    for tt in range(NTT):
        t0 = tt * 128
        em.wait("tensor", vd, vd.n)
        em.wait("tensor", sd, sd.n)
        def rmm(e, t0=t0):
            for kt in range(8):
                r = e.matmul(ps_l[:, :], xn0T[:, kt, t0:t0 + 128], rw[:, kt, :],
                             start=(kt == 0), stop=(kt == 7))
            return r
        em.t(inc(rmm, mm))
        em.wait("vector", mm, mm.n)
        em.v(inc(lambda e: e.reduce_max(rcols[:, 0:1], ps_l[:, :], axis=AX.X, negate=True), vd))
        em.wait("scalar", vd, vd.n)
        em.s(inc(lambda e: e.activation(out=ecol[:, :], in_=ps_l[:, :], func=AF.Exp,
                                        bias=rcols[:, 0:1], accum_out=rcols[:, 1:2]), sd))
        em.wait("vector", sd, sd.n)
        em.v(lambda e: e.reduce_max(rcols[:, 2:3], ecol[:, :], axis=AX.X))
        em.vbar()
        em.v(lambda e: e.tensor_scalar(emask[:, :], ecol[:, :], rcols[:, 2:3], None, ALU.is_lt))
        em.vbar()
        em.v(lambda e: e.tensor_tensor(emask[:, :], ecol[:, :], emask[:, :], ALU.mult))
        em.vbar()
        em.v(lambda e: e.reduce_max(rcols[:, 3:4], emask[:, :], axis=AX.X))
        em.v(lambda e: e.tensor_scalar(rcols[:, 4:5], rcols[:, 1:2], 1e-9, None, ALU.mult))
        em.vbar()
        em.v(lambda e: e.tensor_tensor(rcols[:, 4:5], rcols[:, 4:5], rcols[:, 2:3], ALU.add))
        em.vbar()
        em.v(lambda e: e.tensor_tensor(rcols[:, 4:5], rcols[:, 4:5], rcols[:, 3:4], ALU.add))
        em.vbar()
        em.v(lambda e: e.reciprocal(rcols[:, 5:6], rcols[:, 4:5]))
        em.v(lambda e: e.tensor_scalar(comb[:, :], ecol[:, :], rcols[:, 3:4], None, ALU.is_ge))
        em.vbar()
        em.v(lambda e: e.tensor_tensor(comb[:, :], ecol[:, :], comb[:, :], ALU.mult))
        em.vbar()
        em.v(inc(lambda e: e.tensor_scalar(comb[:, :], comb[:, :], rcols[:, 5:6], None, ALU.mult), vd))
        em.wait("tensor", vd, vd.n)
        em.t(inc(lambda e: e.transpose(ps_t[:, :], comb[:, :], idf[:, :]), mm))
        em.wait("vector", mm, mm.n)
        em.v(inc(lambda e, t0=t0: e.tensor_copy(combT[:, t0:t0 + 128], ps_t[0:16, :]), vd))

    otmp_store = {0: 0, 1: 0}
    for q in range(NQ):
        q0 = q * QT
        end_mm_expert = {}
        for e_i in range(NE + 2):
            buf = e_i % 2
            is_sh = e_i >= NE
            if e_i >= 2:
                em.wait("gpsimd", mm, end_mm_expert[e_i - 2])
            elif q > 0:
                em.wait("gpsimd", mm, end_mm_expert_prev[e_i - 2 + NE + 2])
            if is_sh:
                n = e_i - NE
                em.dma("gpsimd", wgq[buf][:, :, :],
                       P.aps["L0_ffn_wg_s"][n].rearrange("(a p) m -> p a m", p=128), gld)
                em.dma("gpsimd", wuq[buf][:, :, :],
                       P.aps["L0_ffn_wu_s"][n].rearrange("(a p) m -> p a m", p=128), gld)
            else:
                em.dma("gpsimd", wgq[buf][:, :, 0:MR],
                       P.aps["L0_ffn_wg_r"][e_i].rearrange("(a p) m -> p a m", p=128), gld)
                em.dma("gpsimd", wuq[buf][:, :, 0:MR],
                       P.aps["L0_ffn_wu_r"][e_i].rearrange("(a p) m -> p a m", p=128), gld)
            nm = 4 if is_sh else 2
            em.wait("tensor", gld, gld.n)
            for ms in range(nm):
                pu = ps_up[ms % 2]
                war_v = vd.n
                em.wait("tensor", vd, war_v)
                def upg(e, buf=buf, ms=ms, q0=q0, pu=pu):
                    for kt in range(8):
                        r = e.matmul(pu[:, :], wgq[buf][:, kt, ms * 128:(ms + 1) * 128],
                                     xn0T[:, kt, q0:q0 + 512], start=(kt == 0), stop=(kt == 7))
                    return r
                em.t(inc(upg, mm))
                em.wait("scalar", mm, mm.n)
                em.wait("scalar", vd, war_v)
                em.s(inc(lambda e, pu=pu, ms=ms: e.activation(
                    out=gtmp[:, ms % 2, :], in_=pu[:, :], func=AF.Silu), sd))
                em.wait("tensor", sd, sd.n)
                def upu(e, buf=buf, ms=ms, q0=q0, pu=pu):
                    for kt in range(8):
                        r = e.matmul(pu[:, :], wuq[buf][:, kt, ms * 128:(ms + 1) * 128],
                                     xn0T[:, kt, q0:q0 + 512], start=(kt == 0), stop=(kt == 7))
                    return r
                em.t(inc(upu, mm))
                dst, di = (hsT, (e_i - NE) * 4 + ms) if is_sh else (hT, e_i * 2 + ms)
                em.wait("vector", mm, mm.n)
                em.v(inc(lambda e, pu=pu, ms=ms, dst=dst, di=di: e.tensor_tensor(
                    dst[:, di, :], gtmp[:, ms % 2, :], pu[:, :], ALU.mult), vd))
            if not is_sh:
                war_v = vd.n
                em.wait("tensor", vd, war_v)
                em.t(inc(lambda e, e_i=e_i, q0=q0: e.matmul(
                    ps_b[:, :], sel[:, e_i * 128:(e_i + 1) * 128], combT[:, q0:q0 + 512],
                    start=True, stop=True), mm))
                em.wait("vector", mm, mm.n)
                em.vbar()
                for ms in range(2):
                    em.v(inc(lambda e, e_i=e_i, ms=ms: e.tensor_tensor(
                        hT[:, e_i * 2 + ms, :], hT[:, e_i * 2 + ms, :], ps_b[:, :], ALU.mult), vd))
            end_mm_expert[e_i] = mm.n
        end_mm_expert_prev = dict(end_mm_expert)
        em.wait("tensor", vd, vd.n)
        for t in range(QTT):
            t0 = q0 + t * 128
            for dd in range(2):
                pd = ps_dn[(t * 2 + dd) % 2]
                em.wait("tensor", sd, sd.n)
                def dn(e, t=t, dd=dd, pd=pd):
                    first = True
                    for e_i in range(NE):
                        for ms in range(2):
                            e.matmul(pd[:, :], hT[:, e_i * 2 + ms, t * 128:(t + 1) * 128],
                                     wdr[:, e_i * 2 + ms, dd * 512:dd * 512 + 512],
                                     start=first, stop=False)
                            first = False
                    for n in range(2):
                        for ms in range(4):
                            last = (n == 1 and ms == 3)
                            r = e.matmul(pd[:, :], hsT[:, n * 4 + ms, t * 128:(t + 1) * 128],
                                         wds[:, n * 4 + ms, dd * 512:dd * 512 + 512],
                                         start=False, stop=last)
                    return r
                em.t(inc(dn, mm))
                em.wait("scalar", mm, mm.n)
                em.wait("scalar", st, otmp_store[dd])
                em.s(inc(lambda e, pd=pd, dd=dd: e.activation(
                    out=otmp[:, dd, :], in_=pd[:, :], func=AF.Copy), sd))
                em.wait("sync", sd, sd.n)
                em.dma("sync", P.aps["f0"][t0:t0 + 128, dd * 512:dd * 512 + 512],
                       otmp[:, dd, :], st)
                otmp_store[dd] = st.n
    em.wait("sync", st, st.n)
    em.flush()


ATT0_PANELS = 4
ATT0_CHUNKS = 4
ATT0_SUB = 9
ATT0_SUB2 = 9

def emit_att0(P):
    nc = P.nc
    em = Em(P)
    xn0b = P.aps["xn0b"]
    H = 4

    xn0T = P.sb("a0_xnT", [128, 8, 512], BF16)
    wqk = P.sb("a0_wqk", [128, 8, D], BF16)
    wo = P.sb("a0_wo", [128, 16, D], BF16)
    wvg = P.sb("a0_wvg", [128, 8, 2048], BF16)
    qpan = P.sb("a0_qp", [128, 8, 512], BF16)
    kpan = P.sb("a0_kp", [128, 8, 512], BF16)
    gtpan = P.sb("a0_gp", [128, 16, 512], BF16)
    vtok = P.sb("a0_vt", [128, 4, 2048], BF16)
    ktok = P.sb("a0_kt", [128, 8, 128], BF16)
    qsc = P.sb("a0_qs", [128, 8, 128], BF16)
    atb = P.sb("a0_at", [128, 128], BF16)
    ogT_raw = P.sb("a0_ogr", [128, 4, 128], BF16)
    sqb = P.sb("a0_sqb", [128, 4, 128], BF16)
    ogT = P.sb("a0_og", [128, 16, 128], BF16)
    Sst = P.sb("a0_S", [128, 8, 512], F32)
    Sb = P.sb("a0_Sb", [128, 8, 512], BF16)
    cosb = P.sb("a0_cos", [128, 512], F32)
    sinb = P.sb("a0_sin", [128, 512], F32)
    rtmp = P.sb("a0_rt", [128, 4, 512], F32)
    dtb = P.sb("a0_dt", [128, 4, 128], F32)
    cqb = P.sb("a0_cq", [128, 4, 128], F32)
    ckc = P.sb("a0_ck", [128, 4], F32)
    gnc = P.sb("a0_gn", [128, 4], F32)
    idb = P.sb("a0_id", [128, 128], BF16)
    ones1 = P.sb("a0_o1", [1, 128], F32)
    onescol = P.sb("a0_oc", [128, 1], BF16)
    ssrow = P.sb("a0_ss", [1, 512], F32)
    ssg = P.sb("a0_ssg", [1, 4, 4, 128], F32)
    rstdr = P.sb("a0_rsd", [1, 512], F32)
    s0out = P.sb("a0_s0", [128, 2, 512], F32)

    ld, tr, mm, vd, sd, st, gld, gd = (P.ctr("ld"), P.ctr("tr"), P.ctr("mm"), P.ctr("vd"),
                                       P.ctr("sd"), P.ctr("st"), P.ctr("gl"), P.ctr("gd"))

    ps_a = P.ps("a0_psa", [128, 512])
    ps_b = P.ps("a0_psb", [128, 512])
    ps_o1 = P.ps("a0_pso1", [128, 128])
    ps_o2 = P.ps("a0_pso2", [128, 128])
    ps_kv = P.ps("a0_pskv", [128, 1024])
    ps_tr = P.ps("a0_pstr", [128, 128], BF16)
    ps_s0 = P.ps("a0_pss0", [128, 512])

    gam = 1.0 - np.exp2(-5.0 - np.arange(H))
    gC = (gam ** 128).astype(np.float64)

    em.dma("gpsimd", wo[:, :, :], P.aps["L0_attn_Wo"].rearrange("(a p) f -> p a f", p=128), gld)
    em.dma("gpsimd", dtb[:, :, :], P.aps["l0_DT"].rearrange("h j i -> j h i"), gld)
    em.dma("gpsimd", cqb[:, :, :], P.aps["l0_crossq_bc"][:, :, :], gld)
    em.dma("gpsimd", ckc[:, :], P.aps["l0_crossk"][:, :], gld)
    em.dma("gpsimd", gnc[:, :], P.aps["l0_gn_pa"][:, :], gld)
    em.dma("gpsimd", idb[:, :], P.aps["idf32"][:, :], gld)
    em.dma("gpsimd", ones1[:, :], P.aps["ones1x128"][:, :], gld)
    em.g(lambda e: e.memset(onescol[:, :], 1.0))
    em.g(inc(lambda e: e.engine_nop(), gld, 16))
    em.v(lambda e: e.memset(Sst[:, :, :], 0.0))
    em.v(lambda e: e.memset(Sb[:, :, :], 0.0))
    em.v(inc(lambda e: e.engine_nop(), vd))

    em.wait("tensor", tr, tr.n)
    em.wait("tensor", gld, gld.n)
    em.wait("tensor", vd, vd.n)

    s0A = P.aps["s0"]
    wvA = P.aps["L0_attn_Wv"].rearrange("(a p) f -> p a f", p=128)
    wgA = P.aps["L0_attn_Wg"].rearrange("(a p) f -> p a f", p=128)

    mm_wvg_users = 0
    s0_store = {0: 0, 1: 0}

    for pan in range(ATT0_PANELS):
        p0 = pan * 512
        em.wait("sync", mm, mm.n)
        for dt_ in range(8):
            em.dmaT("sync", xn0T[:, dt_, :], xn0b[p0:p0 + 512, dt_ * 128:(dt_ + 1) * 128], tr)
        em.wait("gpsimd", vd, vd.n)
        em.dma("gpsimd", cosb[:, :], P.aps["l0_cosT"][:, p0:p0 + 512], gld)
        em.dma("gpsimd", sinb[:, :], P.aps["l0_sinT"][:, p0:p0 + 512], gld)
        em.wait("gpsimd", mm, mm_wvg_users)
        em.dma("gpsimd", wvg[:, :, :], wvA, gld)
        for wname, dstpan in (("L0_attn_Wq", qpan), ("L0_attn_Wk", kpan)):
            em.wait("gpsimd", mm, mm.n)
            em.dma("gpsimd", wqk[:, :, :],
                   P.aps[wname].rearrange("(a p) f -> p a f", p=128), gld)
            em.wait("tensor", gld, gld.n)
            em.wait("tensor", tr, tr.n)
            for ft in range(8):
                pu = ps_a if (ft % 2 == 0) else ps_b
                em.wait("tensor", sd, sd.n)
                def prj(e, ft=ft, pu=pu):
                    for kt in range(8):
                        r = e.matmul(pu[:, :], wqk[:, kt, ft * 128:(ft + 1) * 128],
                                     xn0T[:, kt, :], start=(kt == 0), stop=(kt == 7))
                    return r
                em.t(inc(prj, mm))
                em.wait("scalar", mm, mm.n)
                em.s(inc(lambda e, pu=pu, dstpan=dstpan, ft=ft: e.activation(
                    out=dstpan[:, ft, :], in_=pu[:, :], func=AF.Copy), sd))
        em.wait("tensor", gld, gld.n)
        for c in range(4):
            for nn in range(4):
                pu = ps_a if (nn % 2 == 0) else ps_b
                em.wait("tensor", vd, vd.n)
                em.wait("tensor", sd, sd.n)
                def vprj(e, c=c, nn=nn, pu=pu):
                    for kt in range(8):
                        r = e.matmul(pu[:, :], xn0T[:, kt, c * 128:(c + 1) * 128],
                                     wvg[:, kt, nn * 512:(nn + 1) * 512], start=(kt == 0), stop=(kt == 7))
                    return r
                em.t(inc(vprj, mm))
                em.wait("vector", mm, mm.n)
                em.v(inc(lambda e, c=c, nn=nn, pu=pu: e.tensor_copy(
                    vtok[:, c, nn * 512:(nn + 1) * 512], pu[:, :]), vd))
        vmm_done = mm.n
        em.wait("gpsimd", mm, vmm_done)
        em.dma("gpsimd", wvg[:, :, :], wgA, gld)
        em.wait("tensor", gld, gld.n)
        for mt in range(16):
            pu = ps_a if (mt % 2 == 0) else ps_b
            em.wait("tensor", sd, sd.n)
            em.wait("tensor", vd, vd.n)
            def gprj(e, mt=mt, pu=pu):
                for kt in range(8):
                    r = e.matmul(pu[:, :], wvg[:, kt, mt * 128:(mt + 1) * 128],
                                 xn0T[:, kt, :], start=(kt == 0), stop=(kt == 7))
                return r
            em.t(inc(gprj, mm))
            em.wait("scalar", mm, mm.n)
            em.s(inc(lambda e, mt=mt, pu=pu: e.activation(
                out=gtpan[:, mt, :], in_=pu[:, :], func=AF.Silu), sd))
        mm_wvg_users = mm.n
        em.wait("vector", sd, sd.n)
        em.wait("vector", gld, gld.n)
        for dstpan in (qpan, kpan):
            for h in range(H):
                em.v(lambda e, d=dstpan, h=h: e.tensor_tensor(rtmp[:, 0, :], d[:, 2 * h, :], cosb[:, :], ALU.mult))
                em.v(lambda e, d=dstpan, h=h: e.tensor_tensor(rtmp[:, 1, :], d[:, 2 * h + 1, :], sinb[:, :], ALU.mult))
                em.v(lambda e, d=dstpan, h=h: e.tensor_tensor(rtmp[:, 2, :], d[:, 2 * h + 1, :], cosb[:, :], ALU.mult))
                em.v(lambda e, d=dstpan, h=h: e.tensor_tensor(rtmp[:, 3, :], d[:, 2 * h, :], sinb[:, :], ALU.mult))
                em.vbar()
                em.v(lambda e, d=dstpan, h=h: e.tensor_tensor(d[:, 2 * h, :], rtmp[:, 0, :], rtmp[:, 1, :], ALU.subtract))
                em.v(lambda e, d=dstpan, h=h: e.tensor_tensor(d[:, 2 * h + 1, :], rtmp[:, 2, :], rtmp[:, 3, :], ALU.add))
                em.vbar()
        em.v(inc(lambda e: e.engine_nop(), vd))
        em.wait("tensor", vd, vd.n)

        import kernel as _K
        for c in range(ATT0_CHUNKS):
            cg = pan * 4 + c
            cl0 = c * 128
            for kt8 in range(8):
                em.wait("tensor", vd, vd.n)
                em.t(inc(lambda e, kt8=kt8, cl0=cl0: e.transpose(
                    ps_tr[:, :], kpan[:, kt8, cl0:cl0 + 128], idb[:, :]), mm))
                em.wait("vector", mm, mm.n)
                em.v(inc(lambda e, kt8=kt8: e.tensor_scalar(
                    ktok[:, kt8, :], ps_tr[:, :], ckc[:, kt8 // 2:kt8 // 2 + 1], None, ALU.mult), vd))
            for kt8 in range(8):
                em.v(lambda e, kt8=kt8, cl0=cl0: e.tensor_tensor(
                    qsc[:, kt8, :], qpan[:, kt8, cl0:cl0 + 128], cqb[:, kt8 // 2, :], ALU.mult))
            em.v(inc(lambda e: e.engine_nop(), vd))
            em.wait("tensor", vd, vd.n)
            if _K.ATT0_SUB < 2:
                continue
            for h in range(H):
                em.wait("tensor", vd, vd.n)
                em.wait("tensor", sd, sd.n)
                def atm(e, h=h, cl0=cl0):
                    e.matmul(ps_a[:, 0:128], kpan[:, 2 * h, cl0:cl0 + 128],
                             qpan[:, 2 * h, cl0:cl0 + 128], start=True, stop=False)
                    return e.matmul(ps_a[:, 0:128], kpan[:, 2 * h + 1, cl0:cl0 + 128],
                                    qpan[:, 2 * h + 1, cl0:cl0 + 128], start=False, stop=True)
                em.t(inc(atm, mm))
                em.wait("vector", mm, mm.n)
                em.v(inc(lambda e, h=h: e.tensor_tensor(atb[:, :], ps_a[:, 0:128], dtb[:, h, :], ALU.mult), vd))
                em.wait("tensor", vd, vd.n)
                if _K.ATT0_SUB2 < 2:
                    continue
                for es in range(4):
                    po = ps_o1 if es % 2 == 0 else ps_o2
                    if es >= 2:
                        em.wait("tensor", vd, vd.n)
                        em.wait("tensor", sd, sd.n)
                    def otm(e, h=h, c=c, es=es, po=po):
                        e.matmul(po[:, :],
                                 vtok[:, c, 512 * h + 128 * es: 512 * h + 128 * (es + 1)],
                                 atb[:, :], start=True, stop=False)
                        e.matmul(po[:, :], Sb[:, h * 2 + 0, es * 128:(es + 1) * 128],
                                 qsc[:, 2 * h, :], start=False, stop=False)
                        return e.matmul(po[:, :], Sb[:, h * 2 + 1, es * 128:(es + 1) * 128],
                                        qsc[:, 2 * h + 1, :], start=False, stop=True)
                    em.t(inc(otm, mm))
                    em.wait("vector", mm, mm.n)
                    em.v(inc(lambda e, es=es, po=po: e.tensor_copy(ogT_raw[:, es, :], po[:, :]), vd))
                    if _K.ATT0_SUB2 >= 3:
                        em.wait("scalar", vd, vd.n)
                        em.s(inc(lambda e, es=es: e.activation(
                            out=sqb[:, es, :], in_=ogT_raw[:, es, :], func=AF.Square), sd))
                if _K.ATT0_SUB2 >= 4:
                    em.wait("gpsimd", sd, sd.n)
                    em.g(inc(lambda e, h=h: e.tensor_reduce(
                        ssg[0:1, h, :, :], sqb[:, :, :], axis=AX.C, op=ALU.add), gd))
                if _K.ATT0_SUB2 < 5:
                    continue
                em.wait("vector", vd, vd.n)
                for es in range(4):
                    em.v(lambda e, h=h, es=es, cl0=cl0: e.scalar_tensor_tensor(
                        ogT[:, h * 4 + es, :], ogT_raw[:, es, :], gnc[:, es:es + 1],
                        gtpan[:, h * 4 + es, cl0:cl0 + 128], ALU.mult, ALU.mult))
                em.v(inc(lambda e: e.engine_nop(), vd))
            if _K.ATT0_SUB < 3:
                continue
            em.wait("vector", gd, gd.n)
            em.v(lambda e: e.reduce_sum(
                ssrow[0:1, :].rearrange("o (h n) -> o h n", h=4),
                ssg[0:1, :, :, :].rearrange("o h es n -> o h n es"),
                axis=AX.X))
            em.vbar()
            em.v(lambda e: e.tensor_scalar(ssrow[0:1, :], ssrow[0:1, :], 1.0 / 512, 1e-5, ALU.mult, ALU.add))
            em.v(inc(lambda e: e.engine_nop(), vd))
            em.wait("scalar", vd, vd.n)
            em.s(inc(lambda e: e.activation(out=ssrow[0:1, :], in_=ssrow[0:1, :], func=AF.Sqrt), sd))
            em.wait("vector", sd, sd.n)
            em.v(lambda e: e.reciprocal(rstdr[0:1, :], ssrow[0:1, :]))
            em.v(inc(lambda e: e.engine_nop(), vd))
            em.wait("tensor", vd, vd.n)
            if _K.ATT0_SUB < 4:
                continue
            for h in range(H):
                em.t(inc(lambda e, h=h: e.matmul(ps_s0[:, 0:128], ones1[0:1, 0:128],
                                                 rstdr[0:1, h * 128:(h + 1) * 128],
                                                 start=True, stop=True), mm))
                em.wait("vector", mm, mm.n)
                for es in range(4):
                    em.v(lambda e, h=h, es=es: e.tensor_tensor(
                        ogT[:, h * 4 + es, :], ogT[:, h * 4 + es, :], ps_s0[:, 0:128], ALU.mult))
                em.v(inc(lambda e: e.engine_nop(), vd))
                em.wait("tensor", vd, vd.n)
            if _K.ATT0_SUB < 5:
                continue
            for fc in range(2):
                em.wait("tensor", sd, sd.n)
                def s0m(e, fc=fc):
                    for idx in range(16):
                        r = e.matmul(ps_s0[:, 0:512], ogT[:, idx, :], wo[:, idx, fc * 512:(fc + 1) * 512],
                                     start=(idx == 0), stop=(idx == 15))
                    return r
                em.t(inc(s0m, mm))
                em.wait("scalar", mm, mm.n)
                em.wait("scalar", st, s0_store[fc])
                em.s(inc(lambda e, fc=fc: e.activation(out=s0out[:, fc, :], in_=ps_s0[:, 0:512], func=AF.Copy), sd))
                em.wait("sync", sd, sd.n)
                em.dma("sync", s0A[cg * 128:(cg + 1) * 128, fc * 512:(fc + 1) * 512], s0out[:, fc, :], st)
                s0_store[fc] = st.n
            if _K.ATT0_SUB < 6:
                continue
            em.wait("tensor", vd, vd.n)
            for h in range(H):
                def kvm(e, h=h, c=c):
                    for dt_ in range(2):
                        r = e.matmul(ps_kv[:, dt_ * 512:(dt_ + 1) * 512], ktok[:, 2 * h + dt_, :],
                                     vtok[:, c, 512 * h:512 * (h + 1)], start=True, stop=True)
                    return r
                em.t(inc(kvm, mm))
                em.wait("vector", mm, mm.n)
                for dt_ in range(2):
                    em.v(lambda e, h=h, dt_=dt_: e.scalar_tensor_tensor(
                        Sst[:, h * 2 + dt_, :], Sst[:, h * 2 + dt_, :], float(gC[h]),
                        ps_kv[:, dt_ * 512:(dt_ + 1) * 512], ALU.mult, ALU.add))
                em.vbar()
                for dt_ in range(2):
                    em.v(lambda e, h=h, dt_=dt_: e.tensor_copy(Sb[:, h * 2 + dt_, :], Sst[:, h * 2 + dt_, :]))
                em.v(inc(lambda e: e.engine_nop(), vd))
                em.wait("tensor", vd, vd.n)
    em.wait("sync", st, st.n)
    em.flush()


LAMBDA_INIT = 0.8 - 0.6 * math.exp(-0.3 * 12)

def emit_att1(P):
    nc = P.nc
    em = Em(P)
    xn1b = P.aps["xn1b"]

    xn1T = P.sb("a1_xnT", [128, 8, 512], BF16)
    wbuf = P.sb("a1_w", [128, 16, 1024], BF16)
    qtb = P.sb("a1_qt", [128, 1024], BF16)
    costile = P.sb("a1_cos", [128, 1024], F32)
    sintile = P.sb("a1_sin", [128, 1024], F32)
    rt1 = P.sb("a1_rt1", [128, 1024], F32)
    rt2 = P.sb("a1_rt2", [128, 512], F32)
    rtb = P.sb("a1_rtb", [128, 512], F32)
    vv = P.sb("a1_vv", [128, 16, 528], BF16)
    kaT = P.sb("a1_kaT", [128, 8, NT], BF16)
    qa = P.sb("a1_qa", [128, 16, 512], BF16)
    PTb = P.sb("a1_pt", [128, 16, 512], BF16)
    afT = P.sb("a1_af", [64, 16, 512], BF16)
    wo = wbuf
    km2 = P.sb("a1_km2", [128, 16], F32)
    kn2 = P.sb("a1_kn2", [128, 16], F32)
    rowsb = P.sb("a1_rows", [1, 16 * 512], F32)
    rb32 = P.sb("a1_rb32", [32, 512], F32)
    rows16 = P.sb("a1_r16", [16, 512], F32)
    kmsqc = P.sb("a1_kmsq", [32, 1], F32)
    lrow = P.sb("a1_lrow", [1, 4, 32], F32)
    lam = P.sb("a1_lam", [1, 8], F32)
    rmswc = P.sb("a1_rmsw", [128, 1], F32)
    zrows = P.sb("a1_zr", [1, 4, 512], F32)
    onesr = P.sb("a1_ones", [1, 128], F32)
    s1out = P.sb("a1_s1o", [128, 2, 512], F32)

    ld, tr, mm, vd, sd, st, gld, gd = (P.ctr("ld"), P.ctr("tr"), P.ctr("mm"), P.ctr("vd"),
                                       P.ctr("sd"), P.ctr("st"), P.ctr("gl"), P.ctr("gd"))

    ps_a = P.ps("a1_psa", [128, 512])
    ps_b = P.ps("a1_psb", [128, 512])
    ps_pt = P.ps("a1_pspt", [128, 512])
    ps_av1 = P.ps("a1_psav1", [128, 512])
    ps_av2 = P.ps("a1_psav2", [128, 512])
    ps_s1 = P.ps("a1_pss1", [128, 512])
    ps_bc = P.ps("a1_psbc", [128, 512])

    l1qA, l1kA, s1A = P.aps["l1q"], P.aps["l1k"], P.aps["s1"]

    em.dma("gpsimd", onesr[:, :], P.aps["ones1x128"][:, :], gld)
    em.dma("gpsimd", rmswc[:, :], P.aps["l1_rmsw_col"][:, :], gld)
    for i, nm in enumerate(["L1_attn_lambda_q1", "L1_attn_lambda_k1",
                            "L1_attn_lambda_q2", "L1_attn_lambda_k2"]):
        em.dma("gpsimd", lrow[0:1, i, :], P.aps[nm].rearrange("(o d) -> o d", o=1), gld)

    em.v(lambda e: e.memset(km2[:, :], 0.0))
    em.v(lambda e: e.memset(
        vv[:, :, :].rearrange("p a (g u) -> p a g u", u=66)[:, :, :, 64:65], 1.0))
    em.v(inc(lambda e: e.engine_nop(), vd))
    em.wait("vector", gld, gld.n)
    em.v(lambda e: e.tensor_tensor(lrow[0:1, 0, :], lrow[0:1, 0, :], lrow[0:1, 1, :], ALU.mult))
    em.v(lambda e: e.tensor_tensor(lrow[0:1, 2, :], lrow[0:1, 2, :], lrow[0:1, 3, :], ALU.mult))
    em.vbar()
    em.v(lambda e: e.reduce_sum(lam[0:1, 0:1], lrow[0:1, 0, :], axis=AX.X))
    em.v(lambda e: e.reduce_sum(lam[0:1, 1:2], lrow[0:1, 2, :], axis=AX.X))
    em.v(inc(lambda e: e.engine_nop(), vd))
    em.wait("scalar", vd, vd.n)
    em.s(inc(lambda e: e.activation(out=lam[0:1, 2:4], in_=lam[0:1, 0:2], func=AF.Exp), sd))
    em.wait("vector", sd, sd.n)
    em.v(lambda e: e.tensor_tensor(lam[0:1, 4:5], lam[0:1, 3:4], lam[0:1, 2:3], ALU.subtract))
    em.vbar()
    em.v(inc(lambda e: e.tensor_scalar(lam[0:1, 4:5], lam[0:1, 4:5], -LAMBDA_INIT, None, ALU.add), vd))

    for pan in range(4):
        p0 = pan * 512
        em.wait("sync", mm, mm.n)
        for dt_ in range(8):
            em.dmaT("sync", xn1T[:, dt_, :], xn1b[1 + p0:1 + p0 + 512, dt_ * 128:(dt_ + 1) * 128], tr)
        for wi, wname in enumerate(["L1_attn_Wq", "L1_attn_Wk", "L1_attn_Wv"]):
            fw = 1024 if wi == 0 else 512
            em.wait("gpsimd", mm, mm.n)
            em.dma("gpsimd", wbuf[:, 0:8, 0:fw],
                   P.aps[wname].rearrange("(a p) f -> p a f", p=128), gld)
            em.wait("tensor", gld, gld.n)
            em.wait("tensor", tr, tr.n)
            for t in range(4):
                tt = pan * 4 + t
                em.wait("tensor", vd, vd.n)
                em.wait("tensor", sd, sd.n)
                for fc in range(fw // 512):
                    pu = ps_a if fc % 2 == 0 else ps_b
                    def prj(e, t=t, fc=fc, pu=pu):
                        for kt in range(8):
                            r = e.matmul(pu[:, :], xn1T[:, kt, t * 128:(t + 1) * 128],
                                         wbuf[:, kt, fc * 512:fc * 512 + 512],
                                         start=(kt == 0), stop=(kt == 7))
                        return r
                    em.t(inc(prj, mm))
                    em.wait("scalar", mm, mm.n)
                    em.s(inc(lambda e, fc=fc, pu=pu: e.activation(
                        out=qtb[:, fc * 512:fc * 512 + 512], in_=pu[:, :], func=AF.Copy), sd))
                if wi < 2:
                    ctab = "l1_cosq" if wi == 0 else "l1_cosk"
                    stab = "l1_sinq" if wi == 0 else "l1_sink"
                    em.wait("sync", vd, vd.n)
                    em.dma("sync", costile[:, 0:fw], P.aps[ctab][tt * 128:(tt + 1) * 128, 0:fw], ld)
                    em.dma("sync", sintile[:, 0:fw], P.aps[stab][tt * 128:(tt + 1) * 128, 0:fw], ld)
                    em.wait("vector", ld, ld.n)
                    em.wait("vector", sd, sd.n)
                    em.v(lambda e, fw=fw: e.tensor_tensor(rt1[:, 0:fw], qtb[:, 0:fw], costile[:, 0:fw], ALU.mult))
                    em.v(lambda e, fw=fw: e.tensor_tensor(costile[:, 0:fw], qtb[:, 0:fw], sintile[:, 0:fw], ALU.mult))
                    em.vbar()
                    va = lambda a, fw: a[:, 0:fw].rearrange("p (h x) -> p h x", x=32)
                    em.v(lambda e, fw=fw: e.tensor_tensor(
                        va(qtb, fw)[:, :, 0:16], va(rt1, fw)[:, :, 0:16],
                        va(costile, fw)[:, :, 16:32], ALU.add))
                    em.v(lambda e, fw=fw: e.tensor_tensor(
                        va(qtb, fw)[:, :, 16:32], va(rt1, fw)[:, :, 16:32],
                        va(costile, fw)[:, :, 0:16], ALU.subtract))
                    em.vbar()
                    if wi == 1:
                        em.v(lambda e: e.tensor_tensor(rt2[:, :], qtb[:, 0:512], qtb[:, 0:512], ALU.mult))
                        em.vbar()
                        em.v(lambda e: e.reduce_sum(kn2[:, :].rearrange("p (h o) -> p h o", o=1),
                                                    rt2[:, :].rearrange("p (h x) -> p h x", x=32), axis=AX.X))
                        em.vbar()
                        em.v(lambda e: e.tensor_tensor(km2[:, :], km2[:, :], kn2[:, :], ALU.max))
                        em.vbar()
                    em.v(inc(lambda e: e.engine_nop(), vd))
                    em.wait("sync", vd, vd.n)
                    dst = l1qA if wi == 0 else l1kA
                    nh = fw // 32
                    em.dma("sync", dst[tt * 128:(tt + 1) * 128, :].rearrange(
                        "p (h u) -> p h u", u=64)[:, :, 0:32],
                        qtb[:, 0:fw].rearrange("p (h x) -> p h x", x=32), st)
                else:
                    em.wait("vector", sd, sd.n)
                    em.v(inc(lambda e, tt=tt: e.tensor_copy(
                        vv[:, tt, :].rearrange("p (g u) -> p g u", u=66)[:, :, 0:64],
                        qtb[:, 0:512].rearrange("p (g u) -> p g u", u=64)), vd))

    em.wait("gpsimd", mm, mm.n)
    em.dma("gpsimd", wo[:, :, :], P.aps["l1_wo_pad"].rearrange("(a p) f -> p a f", p=128), gld)
    em.wait("sync", st, st.n)
    for b in range(8):
        em.dmaT("sync", kaT[:, b, :], l1kA[:, b * 128:(b + 1) * 128], tr)
    em.wait("vector", tr, tr.n)
    for b in range(8):
        em.v(lambda e, b=b: e.memset(kaT[32:33, b, :], 1.0))
        em.v(lambda e, b=b: e.memset(kaT[96:97, b, :], 1.0))
    em.v(inc(lambda e: e.engine_nop(), vd))
    em.wait("gpsimd", vd, vd.n)
    em.g(inc(lambda e: e.tensor_reduce(zrows[0:1, 0, 0:16], km2[:, :], axis=AX.C, op=ALU.max), gd))
    em.wait("vector", gd, gd.n)
    em.v(lambda e: e.reduce_max(lam[0:1, 5:6], zrows[0:1, 0, 0:16], axis=AX.X))
    em.v(inc(lambda e: e.engine_nop(), vd))
    em.wait("tensor", vd, vd.n)
    em.t(inc(lambda e: e.matmul(ps_bc[0:32, 0:1], onesr[0:1, 0:32], lam[0:1, 5:6],
                                start=True, stop=True), mm))
    em.wait("vector", mm, mm.n)
    em.v(inc(lambda e: e.tensor_copy(kmsqc[:, :], ps_bc[0:32, 0:1]), vd))

    s1_store = {0: 0, 1: 0}
    scale = 32.0 ** -0.5
    for qc in range(4):
        nkt = 4 * (qc + 1)
        em.wait("sync", mm, mm.n)
        em.wait("sync", vd, vd.n)
        em.wait("sync", sd, sd.n)
        for b in range(16):
            em.dmaT("sync", qa[:, b, :], l1qA[qc * 512:(qc + 1) * 512, b * 128:(b + 1) * 128], tr)
        em.wait("scalar", tr, tr.n)
        for ti in range(16):
            em.s(inc(lambda e, ti=ti: e.activation(out=PTb[:, ti, :], in_=qa[:, ti, :],
                                                   func=AF.Square), sd))
        for c in range(2):
            base = 64 * c
            em.wait("gpsimd", sd, sd.n)
            em.wait("gpsimd", ld, ld.n)
            for t_ in range(16):
                em.g(inc(lambda e, t_=t_, base=base: e.tensor_reduce(
                    rowsb[0:1, t_ * 512:(t_ + 1) * 512], PTb[base:base + 32, t_, :],
                    axis=AX.C, op=ALU.add), gd))
            em.wait("sync", gd, gd.n)
            em.dma("sync", rb32[c * 16:(c + 1) * 16, :],
                   rowsb[0:1, :].rearrange("o (t n) -> o t n", n=512), ld)
        em.wait("vector", ld, ld.n)
        em.v(lambda e: e.tensor_scalar(rb32[:, :], rb32[:, :], kmsqc[:, 0:1], None, ALU.mult))
        em.v(inc(lambda e: e.engine_nop(), vd))
        em.wait("scalar", vd, vd.n)
        em.s(inc(lambda e: e.activation(out=rb32[:, :], in_=rb32[:, :], func=AF.Sqrt), sd))
        em.s(inc(lambda e: e.mul(rb32[:, :], rb32[:, :], -1.0), sd))
        em.wait("gpsimd", sd, sd.n)
        em.dma("gpsimd", qa[32:33, :, :], rb32[0:16, :], ld)
        em.dma("gpsimd", qa[96:97, :, :], rb32[16:32, :], ld)
        em.wait("tensor", ld, ld.n)
        em.wait("tensor", vd, vd.n)
        em.wait("tensor", gd, gd.n)
        for h in range(16):
            for c in range(2):
                ha = 2 * h + c
                kk = (h // 2) * 2 + c
                base_q = 64 * (ha % 2)
                base_k = 64 * (kk % 2)
                pav = ps_av1 if c == 0 else ps_av2
                for kt in range(nkt):
                    em.wait("tensor", sd, sd.n)
                    em.t(inc(lambda e, ha=ha, kk=kk, base_q=base_q, base_k=base_k, kt=kt:
                             e.matmul(ps_pt[:, :], kaT[base_k:base_k + 33, kk // 2, kt * 128:(kt + 1) * 128],
                                      qa[base_q:base_q + 33, ha // 2, :], start=True, stop=True), mm))
                    em.wait("scalar", mm, mm.n)
                    em.s(inc(lambda e, kt=kt: e.activation(out=PTb[:, kt, :], in_=ps_pt[:, :],
                                                           func=AF.Exp), sd))
                em.wait("gpsimd", sd, sd.n)
                for kt in range(4 * qc, nkt):
                    em.g(inc(lambda e, kt=kt, qc=qc: e.affine_select(
                        PTb[:, kt, :], PTb[:, kt, :], [[1, 512]], ALU.is_ge, 0.0,
                        base=qc * 512 - kt * 128, channel_multiplier=-1), gd))
                em.wait("tensor", gd, gd.n)
                em.wait("tensor", vd, vd.n)
                def avm(e, h=h, pav=pav, nkt=nkt):
                    g = h // 2
                    for kt in range(nkt):
                        r = e.matmul(pav[0:65, :], vv[:, kt, g * 66:g * 66 + 65],
                                     PTb[:, kt, :], start=(kt == 0), stop=(kt == nkt - 1))
                    return r
                em.t(inc(avm, mm))
            em.wait("vector", mm, mm.n)
            em.v(lambda e: e.reciprocal(zrows[0:1, 0, :], ps_av1[64:65, :]))
            em.v(lambda e: e.reciprocal(zrows[0:1, 1, :], ps_av2[64:65, :]))
            em.vbar()
            em.v(lambda e: e.tensor_scalar(zrows[0:1, 1, :], zrows[0:1, 1, :],
                                           lam[0:1, 4:5], None, ALU.mult))
            em.v(inc(lambda e: e.engine_nop(), vd))
            em.wait("tensor", vd, vd.n)
            em.t(inc(lambda e: e.matmul(ps_bc[0:64, :], onesr[0:1, 0:64], zrows[0:1, 0, :],
                                        start=True, stop=True), mm))
            em.wait("vector", mm, mm.n)
            em.v(lambda e: e.tensor_copy(rt1[0:64, 0:512], ps_bc[0:64, :]))
            em.vbar()
            em.v(inc(lambda e: e.tensor_tensor(rt2[0:64, :], ps_av1[0:64, :], rt1[0:64, 0:512], ALU.mult), vd))
            em.wait("tensor", vd, vd.n)
            em.t(inc(lambda e: e.matmul(ps_bc[0:64, :], onesr[0:1, 0:64], zrows[0:1, 1, :],
                                        start=True, stop=True), mm))
            em.wait("vector", mm, mm.n)
            em.v(lambda e: e.tensor_copy(rt1[0:64, 0:512], ps_bc[0:64, :]))
            em.vbar()
            em.v(inc(lambda e: e.tensor_tensor(rtb[0:64, :], ps_av2[0:64, :], rt1[0:64, 0:512], ALU.mult), vd))
            em.vbar()
            em.v(inc(lambda e, h=h: e.tensor_tensor(afT[0:64, h, :], rt2[0:64, :], rtb[0:64, :], ALU.add), vd))
            em.vbar()
        em.wait("scalar", vd, vd.n)
        for h in range(16):
            em.s(inc(lambda e, h=h: e.activation(out=PTb[0:64, h, :], in_=afT[0:64, h, :],
                                                 func=AF.Square), sd))
        em.wait("gpsimd", sd, sd.n)
        for h in range(16):
            em.g(inc(lambda e, h=h: e.tensor_reduce(
                rowsb[0:1, h * 512:(h + 1) * 512], PTb[0:64, h, :], axis=AX.C, op=ALU.add), gd))
        em.wait("sync", gd, gd.n)
        em.dma("sync", rows16[:, :], rowsb[0:1, 0:8192].rearrange("o (t n) -> o t n", n=512), ld)
        em.wait("vector", ld, ld.n)
        em.v(lambda e: e.tensor_scalar(rows16[:, :], rows16[:, :], 1.0 / 64, 1e-6, ALU.mult, ALU.add))
        em.v(inc(lambda e: e.engine_nop(), vd))
        em.wait("scalar", vd, vd.n)
        em.s(inc(lambda e: e.activation(out=rows16[:, :], in_=rows16[:, :], func=AF.Sqrt), sd))
        em.wait("vector", sd, sd.n)
        em.v(lambda e: e.reciprocal(rows16[:, :], rows16[:, :]))
        em.vbar()
        em.v(inc(lambda e: e.tensor_scalar(rows16[:, :], rows16[:, :], 1.0 - LAMBDA_INIT, None, ALU.mult), vd))
        em.wait("sync", vd, vd.n)
        em.dma("sync", rowsb[0:1, 0:8192].rearrange("o (t n) -> o t n", n=512), rows16[:, :], ld)
        em.wait("tensor", ld, ld.n)
        for h in range(16):
            em.t(inc(lambda e, h=h: e.matmul(ps_bc[0:64, :], onesr[0:1, 0:64],
                                             rowsb[0:1, h * 512:(h + 1) * 512],
                                             start=True, stop=True), mm))
            em.wait("vector", mm, mm.n)
            em.v(inc(lambda e, h=h: e.scalar_tensor_tensor(
                afT[0:64, h, :], afT[0:64, h, :], rmswc[0:64, 0:1], ps_bc[0:64, :],
                ALU.mult, ALU.mult), vd))
            em.wait("tensor", vd, vd.n)
        for qs in range(4):
            for fc in range(2):
                em.wait("tensor", sd, sd.n)
                def s1m(e, qs=qs, fc=fc):
                    for h in range(16):
                        r = e.matmul(ps_s1[:, :], afT[0:64, h, qs * 128:(qs + 1) * 128],
                                     wo[0:64, h, fc * 512:(fc + 1) * 512],
                                     start=(h == 0), stop=(h == 15))
                    return r
                em.t(inc(s1m, mm))
                em.wait("scalar", mm, mm.n)
                em.wait("scalar", st, s1_store[fc])
                em.s(inc(lambda e, fc=fc: e.activation(out=s1out[:, fc, :], in_=ps_s1[:, :],
                                                       func=AF.Copy), sd))
                em.wait("sync", sd, sd.n)
                em.dma("sync", s1A[qc * 512 + qs * 128: qc * 512 + (qs + 1) * 128,
                                   fc * 512:(fc + 1) * 512], s1out[:, fc, :], st)
                s1_store[fc] = st.n
    em.wait("sync", st, st.n)
    em.flush()


FFN1_PRO = 9
FFN1_PH = 9

def emit_ffn1(P, l):
    import kernel as _K
    nc = P.nc
    em = Em(P)
    sfx = str(l)
    yinA = P.aps["xn1f" if l == 0 else "ymid"]
    ysrcA = P.aps["xn1b" if l == 0 else "y1b_0"]
    aqA, akA, h2A = P.aps[f"aq_{l}"], P.aps[f"ak_{l}"], P.aps[f"h2_{l}"]
    yaA, y1fA = P.aps[f"ya_{l}"], P.aps[f"y1f_{l}"]
    outA = P.aps["ymid" if l == 0 else "f1"]
    outbA = P.aps["y1b_0" if l == 0 else None] if l == 0 else None

    ld, tr, mm, vd, sd, st, gld, gd = (P.ctr("ld"), P.ctr("tr"), P.ctr("mm"), P.ctr("vd"),
                                       P.ctr("sd"), P.ctr("st"), P.ctr("gl"), P.ctr("gd"))
    ps_a = P.ps(f"f1{sfx}_psa", [128, 512])
    ps_b = P.ps(f"f1{sfx}_psb", [128, 512])
    ps_c = P.ps(f"f1{sfx}_psc", [128, 512])
    ps_pt = P.ps(f"f1{sfx}_pspt", [128, 512])
    ps_av = P.ps(f"f1{sfx}_psav", [128, 512])
    ps_bc = P.ps(f"f1{sfx}_psbc", [128, 512])
    ps_o = P.ps(f"f1{sfx}_pso", [128, 512])

    from contextlib import ExitStack as _ES
    sub = _ES()
    def asb(name, shape, dt):
        return sub.enter_context(nc.sbuf_tensor(name, shape, dt))

    hTp = asb(f"f1{sfx}_hT", [128, 8, 512], BF16)
    wq3 = asb(f"f1{sfx}_wq3", [128, 8, 1536], BF16)
    qkt = asb(f"f1{sfx}_qkt", [128, 1536], BF16)
    sqb = asb(f"f1{sfx}_sqb", [128, 1024], F32)
    nrm = asb(f"f1{sfx}_nrm", [128, 16], F32)
    qsbc = asb(f"f1{sfx}_qsbc", [128, 2, 512], F32)
    cosx = asb(f"f1{sfx}_cosx", [128, 512], F32)
    sinx = asb(f"f1{sfx}_sinx", [128, 512], F32)
    rxa = asb(f"f1{sfx}_rxa", [128, 512], F32)
    rxb = asb(f"f1{sfx}_rxb", [128, 512], F32)
    vvf = asb(f"f1{sfx}_vvf", [128, 16, 528], BF16)
    kfa = asb(f"f1{sfx}_kfa", [128, 8, NT], BF16)
    kprev = asb(f"f1{sfx}_kprev", [128, 8, 512], BF16)
    qfa = asb(f"f1{sfx}_qfa", [128, 8, 512], BF16)
    PTb = asb(f"f1{sfx}_pt", [128, 8, 512], BF16)
    ofa = asb(f"f1{sfx}_ofa", [64, 8, 512], BF16)
    wob = asb(f"f1{sfx}_wob", [128, 8, 1024], BF16)
    b2col = asb(f"f1{sfx}_b2c", [128, 4], F32)
    negm = asb(f"f1{sfx}_negm", [128, 32], F32)
    rows32 = asb(f"f1{sfx}_r32", [32, 128], F32)
    cmask = asb(f"f1{sfx}_cm", [128, 4, 512], F32)
    acc0 = asb(f"f1{sfx}_ac0", [128, 1], F32)
    acc1 = asb(f"f1{sfx}_ac1", [128, 1], F32)
    b2v = asb(f"f1{sfx}_b2v", [128, 1], F32)
    onesr = asb(f"f1{sfx}_on", [1, 128], F32)
    idf = asb(f"f1{sfx}_id", [128, 128], F32)
    ytile = asb(f"f1{sfx}_yt", [128, 2, 1024], F32)
    ybf = asb(f"f1{sfx}_ybf", [128, 2, 1024], BF16)
    zrow = asb(f"f1{sfx}_zr", [1, 3, 512], F32)
    zb = asb(f"f1{sfx}_zb", [1, D], BF16)

    em.dma("gpsimd", onesr[:, :], P.aps["ones1x128"][:, :], gld)
    if _K.FFN1_PRO >= 2:
        em.dma("gpsimd", idf[:, :], P.aps["idf32"][:, :], gld)
        em.dma("gpsimd", cmask[:, :, :], P.aps["l1f_cmask"].rearrange("p (a n) -> p a n", a=4), gld)
        em.dma("gpsimd", b2v[:, :], P.aps["l1f_base2"][:, :], gld)
    if _K.FFN1_PRO >= 3:
        em.dma("gpsimd", wq3[:, :, :], P.aps["L1_ffn_wqkv"][l].rearrange("(a p) f -> p a f", p=128), gld)
        em.dma("gpsimd", wob[:, :, :], P.aps[f"l1f_wo_pad_{l}"].rearrange("(a p) f -> p a f", p=128), gld)
    em.dma("gpsimd", zrow[0:1, 0, :], P.aps[f"l1f_qs_til_{l}"][:, :], gld)
    em.dma("gpsimd", zrow[0:1, 1, :], P.aps[f"l1f_ks_til_{l}"][:, :], gld)
    if _K.FFN1_PRO >= 4:
        em.v(lambda e: e.memset(vvf[:, :, :].rearrange("p a (g u) -> p a g u", u=66)[:, :, :, 64:65], 1.0))
    em.v(inc(lambda e: e.engine_nop(), vd))
    em.wait("tensor", gld, gld.n)
    em.wait("tensor", vd, vd.n)
    if _K.FFN1_PRO >= 5:
        for i in range(2):
            em.wait("tensor", vd, vd.n)
            em.t(inc(lambda e, i=i: e.matmul(ps_bc[:, :], onesr[0:1, :], zrow[0:1, i, :],
                                             start=True, stop=True), mm))
            em.wait("vector", mm, mm.n)
            em.v(inc(lambda e, i=i: e.tensor_copy(qsbc[:, i, :], ps_bc[:, :]), vd))

    if _K.FFN1_PH < 1:
        em.wait("gpsimd", gld, gld.n)
        em.wait("sync", tr, tr.n)
        em.wait("sync", ld, ld.n)
        em.wait("sync", st, st.n)
        em.flush()
        sub.close()
        return
    for pan in range(4):
        p0 = pan * 512
        em.wait("sync", mm, mm.n)
        for dt_ in range(4):
            em.dmaT("sync", hTp[:, dt_, :], ysrcA[1 + p0:1 + p0 + 512, dt_ * 128:(dt_ + 1) * 128], tr)
        for dt_ in range(4, 8):
            em.dmaT("sync", hTp[:, dt_, :], ysrcA[p0:p0 + 512, dt_ * 128:(dt_ + 1) * 128], tr)
        em.wait("tensor", tr, tr.n)
        for t in range(4):
            tt = pan * 4 + t
            em.wait("tensor", vd, vd.n)
            em.wait("tensor", sd, sd.n)
            for fc in range(3):
                pu = (ps_a, ps_b, ps_c)[fc]
                def prj(e, t=t, fc=fc, pu=pu):
                    for kt in range(8):
                        r = e.matmul(pu[:, :], hTp[:, kt, t * 128:(t + 1) * 128],
                                     wq3[:, kt, fc * 512:fc * 512 + 512],
                                     start=(kt == 0), stop=(kt == 7))
                    return r
                em.t(inc(prj, mm))
                em.wait("scalar", mm, mm.n)
                em.s(inc(lambda e, fc=fc, pu=pu: e.activation(
                    out=qkt[:, fc * 512:fc * 512 + 512], in_=pu[:, :], func=AF.Copy), sd))
            em.wait("vector", sd, sd.n)
            em.v(inc(lambda e, tt=tt: e.tensor_copy(
                vvf[:, tt, :].rearrange("p (g u) -> p g u", u=66)[:, :, 0:64],
                qkt[:, 1024:1536].rearrange("p (g u) -> p g u", u=64)), vd))
            for which in range(2):
                off = which * 512
                tabc = f"l1f_cos{'q' if which == 0 else 'k'}"
                tabs = f"l1f_sin{'q' if which == 0 else 'k'}"
                em.wait("sync", vd, vd.n)
                em.dma("sync", cosx[:, :], P.aps[tabc][tt * 128:(tt + 1) * 128, :], ld)
                em.dma("sync", sinx[:, :], P.aps[tabs][tt * 128:(tt + 1) * 128, :], ld)
                em.v(lambda e, off=off: e.tensor_tensor(sqb[:, 0:512], qkt[:, off:off + 512],
                                                        qkt[:, off:off + 512], ALU.mult))
                em.vbar()
                em.v(lambda e, which=which: e.reduce_sum(
                    nrm[:, which * 8:(which + 1) * 8].rearrange("p (h o) -> p h o", o=1),
                    sqb[:, 0:512].rearrange("p (h x) -> p h x", x=64), axis=AX.X))
                em.v(inc(lambda e: e.engine_nop(), vd))
                em.wait("scalar", vd, vd.n)
                em.s(inc(lambda e, which=which: e.activation(
                    out=nrm[:, which * 8:(which + 1) * 8],
                    in_=nrm[:, which * 8:(which + 1) * 8], func=AF.Sqrt), sd))
                em.wait("vector", sd, sd.n)
                em.v(lambda e, which=which: e.tensor_scalar(
                    nrm[:, which * 8:(which + 1) * 8], nrm[:, which * 8:(which + 1) * 8],
                    1e-12, None, ALU.max))
                em.vbar()
                em.v(lambda e, which=which: e.reciprocal(
                    nrm[:, which * 8:(which + 1) * 8], nrm[:, which * 8:(which + 1) * 8]))
                em.vbar()
                for h in range(8):
                    em.v(lambda e, off=off, h=h, which=which: e.tensor_scalar(
                        qkt[:, off + h * 64: off + (h + 1) * 64],
                        qkt[:, off + h * 64: off + (h + 1) * 64],
                        nrm[:, which * 8 + h:which * 8 + h + 1], None, ALU.mult))
                em.vbar()
                em.v(lambda e, off=off, which=which: e.tensor_tensor(
                    qkt[:, off:off + 512], qkt[:, off:off + 512], qsbc[:, which, :], ALU.mult))
                em.vbar()
                em.wait("vector", ld, ld.n)
                em.v(lambda e, off=off: e.tensor_tensor(rxa[:, :], qkt[:, off:off + 512], cosx[:, :], ALU.mult))
                em.v(lambda e, off=off: e.tensor_tensor(rxb[:, :], qkt[:, off:off + 512], sinx[:, :], ALU.mult))
                em.vbar()
                va = lambda a: a.rearrange("p (h x) -> p h x", x=64)
                vq = lambda off: qkt[:, off:off + 512].rearrange("p (h x) -> p h x", x=64)
                em.v(lambda e, off=off: e.tensor_tensor(
                    vq(off)[:, :, 0:32], va(rxa[:, :])[:, :, 0:32], va(rxb[:, :])[:, :, 32:64], ALU.subtract))
                em.v(lambda e, off=off: e.tensor_tensor(
                    vq(off)[:, :, 32:64], va(rxa[:, :])[:, :, 32:64], va(rxb[:, :])[:, :, 0:32], ALU.add))
                em.v(inc(lambda e: e.engine_nop(), vd))
                em.wait("sync", vd, vd.n)
                dstA = aqA if which == 0 else akA
                em.dma("sync", dstA[tt * 128:(tt + 1) * 128, :].rearrange(
                    "p (h u) -> p h u", u=128)[:, :, 0:64],
                    qkt[:, off:off + 512].rearrange("p (h x) -> p h x", x=64), st)
    if _K.FFN1_PH < 2:
        em.wait("gpsimd", gld, gld.n)
        em.wait("sync", tr, tr.n)
        em.wait("sync", ld, ld.n)
        em.wait("sync", st, st.n)
        em.flush()
        sub.close()
        return
    em.wait("sync", st, st.n)
    for b in range(8):
        em.dmaT("sync", kfa[:, b, :], akA[:, b * 128:(b + 1) * 128], tr)
    em.wait("vector", tr, tr.n)
    for b in range(8):
        em.v(lambda e, b=b: e.memset(kfa[64:65, b, :], 1.0))
    em.v(inc(lambda e: e.engine_nop(), vd))
    em.wait("tensor", vd, vd.n)

    if _K.FFN1_PH < 3:
        em.wait("gpsimd", gld, gld.n)
        em.wait("sync", tr, tr.n)
        em.wait("sync", ld, ld.n)
        em.wait("sync", st, st.n)
        em.flush()
        sub.close()
        return
    ystore = {}
    for w in range(4):
        w0 = w * 512
        em.wait("sync", mm, mm.n)
        em.wait("sync", gd, gd.n)
        for b in range(8):
            em.dmaT("sync", qfa[:, b, :], aqA[w0:w0 + 512, b * 128:(b + 1) * 128], tr)
        if w > 0:
            em.wait("vector", tr, tr.n)
            em.wait("vector", mm, mm.n)
            for b in range(8):
                em.v(lambda e, b=b, w=w: e.tensor_scalar(
                    kprev[:, b, :], kfa[:, b, (w - 1) * 512:w * 512], b2v[:, 0:1], None, ALU.mult))
            em.v(inc(lambda e: e.engine_nop(), vd))
        em.wait("tensor", tr, tr.n)
        em.wait("tensor", vd, vd.n)
        for h in range(8):
            for qs in range(4):
                if w > 0:
                    em.wait("tensor", vd, vd.n)
                    em.t(inc(lambda e, h=h, qs=qs: e.matmul(
                        ps_pt[:, :], qfa[0:64, h, qs * 128:(qs + 1) * 128],
                        kprev[0:64, h, :], start=True, stop=True), mm))
                    em.wait("vector", mm, mm.n)
                    em.v(inc(lambda e: e.reduce_max(acc0[:, :], ps_pt[:, :], axis=AX.X), vd))
                else:
                    em.v(inc(lambda e: e.memset(acc0[:, :], -3.0e38), vd))
                em.wait("tensor", vd, vd.n)
                em.t(inc(lambda e, h=h, qs=qs, w=w: e.matmul(
                    ps_pt[:, :], qfa[0:64, h, qs * 128:(qs + 1) * 128],
                    kfa[0:64, h, w0:w0 + 512], start=True, stop=True), mm))
                em.wait("vector", mm, mm.n)
                em.v(lambda e, qs=qs: e.tensor_tensor(sqb[:, 0:512], ps_pt[:, :],
                                                      cmask[:, qs, :], ALU.add))
                em.vbar()
                em.v(inc(lambda e: e.reduce_max(acc1[:, :], sqb[:, 0:512], axis=AX.X), vd))
                em.vbar()
                em.v(lambda e: e.tensor_tensor(acc1[:, :], acc1[:, :], acc0[:, :], ALU.max))
                em.vbar()
                em.v(inc(lambda e, h=h, qs=qs: e.tensor_scalar(
                    negm[:, h * 4 + qs:h * 4 + qs + 1], acc1[:, :], -1.0, None, ALU.mult), vd))
        em.wait("tensor", vd, vd.n)
        em.t(inc(lambda e: e.transpose(ps_bc[0:32, 0:128], negm[:, :], idf[:, :]), mm))
        em.wait("vector", mm, mm.n)
        em.v(inc(lambda e: e.tensor_copy(rows32[:, :], ps_bc[0:32, 0:128]), vd))
        em.wait("gpsimd", vd, vd.n)
        for h in range(8):
            em.dma("gpsimd", qfa[64:65, h, :], rows32[h * 4:(h + 1) * 4, :], ld)
        em.wait("tensor", ld, ld.n)
        if _K.FFN1_PH < 4:
            continue
        for h in range(8):
            for kt in range(8):
                iw = kt // 4
                ktl = kt % 4
                if w == 0 and iw == 0:
                    em.wait("vector", mm, mm.n)
                    em.v(inc(lambda e, kt=kt: e.memset(PTb[:, kt, :], 0.0), vd))
                    continue
                src = kprev if iw == 0 else kfa
                coff = ktl * 128 if iw == 0 else w0 + ktl * 128
                em.wait("tensor", sd, sd.n)
                em.wait("tensor", gd, gd.n)
                em.t(inc(lambda e, h=h, src=src, coff=coff: e.matmul(
                    ps_pt[:, :], src[0:65, h, coff:coff + 128],
                    qfa[0:65, h, :], start=True, stop=True), mm))
                em.wait("scalar", mm, mm.n)
                em.s(inc(lambda e, kt=kt: e.activation(out=PTb[:, kt, :], in_=ps_pt[:, :],
                                                       func=AF.Exp), sd))
                em.wait("gpsimd", sd, sd.n)
                if iw == 0:
                    em.g(inc(lambda e, kt=kt, ktl=ktl: e.affine_select(
                        PTb[:, kt, :], PTb[:, kt, :], [[-1, 512]], ALU.is_ge, 0.0,
                        base=ktl * 128, channel_multiplier=1), gd))
                else:
                    em.g(inc(lambda e, kt=kt, ktl=ktl: e.affine_select(
                        PTb[:, kt, :], PTb[:, kt, :], [[1, 512]], ALU.is_ge, 0.0,
                        base=-ktl * 128, channel_multiplier=-1), gd))
            em.wait("tensor", gd, gd.n)
            em.wait("tensor", vd, vd.n)
            def avm(e, h=h, w=w):
                pairs = []
                for kt in range(8):
                    ktg = (w - 1) * 4 + kt if kt < 4 else w * 4 + (kt - 4)
                    if ktg >= 0:
                        pairs.append((ktg, kt))
                r = None
                for i, (ktg, kt) in enumerate(pairs):
                    r = e.matmul(ps_av[0:65, :], vvf[:, ktg, h * 66:h * 66 + 65],
                                 PTb[:, kt, :], start=(i == 0), stop=(i == len(pairs) - 1))
                return r
            em.t(inc(avm, mm))
            em.wait("vector", mm, mm.n)
            em.v(lambda e: e.reciprocal(zrow[0:1, 2, :], ps_av[64:65, :]))
            em.v(inc(lambda e: e.engine_nop(), vd))
            em.wait("tensor", vd, vd.n)
            em.t(inc(lambda e: e.matmul(ps_bc[0:64, :], onesr[0:1, 0:64], zrow[0:1, 2, :],
                                        start=True, stop=True), mm))
            em.wait("vector", mm, mm.n)
            em.v(lambda e: e.tensor_copy(rxa[0:64, :], ps_bc[0:64, :]))
            em.vbar()
            em.v(inc(lambda e, h=h: e.tensor_tensor(ofa[0:64, h, :], ps_av[0:64, :],
                                                    rxa[0:64, :], ALU.mult), vd))
            em.vbar()
        if _K.FFN1_PH < 5:
            continue
        for qs in range(4):
            tt = w * 4 + qs
            buf = qs % 2
            em.wait("sync", vd, ystore.get(("v", buf), 0))
            em.wait("sync", st, ystore.get(("s", buf), 0))
            em.dma("sync", ytile[:, buf, :], yinA[tt * 128:(tt + 1) * 128, :], ld)
            for fc in range(2):
                em.wait("tensor", vd, vd.n)
                def wom(e, qs=qs, fc=fc):
                    for h in range(8):
                        r = e.matmul(ps_o[:, :], ofa[0:64, h, qs * 128:(qs + 1) * 128],
                                     wob[0:64, h, fc * 512:(fc + 1) * 512],
                                     start=(h == 0), stop=(h == 7))
                    return r
                em.t(inc(wom, mm))
                em.wait("vector", mm, mm.n)
                em.wait("vector", ld, ld.n)
                em.wait("vector", st, ystore.get(("s", buf), 0))
                em.v(lambda e, buf=buf, fc=fc: e.tensor_tensor(
                    ytile[:, buf, fc * 512:(fc + 1) * 512],
                    ytile[:, buf, fc * 512:(fc + 1) * 512], ps_o[:, :], ALU.add))
                em.v(inc(lambda e: e.engine_nop(), vd))
            em.vbar()
            em.v(inc(lambda e, buf=buf: e.tensor_copy(ybf[:, buf, :], ytile[:, buf, :]), vd))
            ystore[("v", buf)] = vd.n
            em.wait("sync", vd, vd.n)
            em.dma("sync", y1fA[tt * 128:(tt + 1) * 128, :], ytile[:, buf, :], st)
            em.dma("sync", yaA[1 + tt * 128:1 + (tt + 1) * 128, :], ybf[:, buf, :], st)
            ystore[("s", buf)] = st.n
    em.v(lambda e: e.memset(zb[:, :], 0.0))
    em.v(inc(lambda e: e.engine_nop(), vd))
    em.wait("sync", vd, vd.n)
    em.dma("sync", yaA[0:1, :], zb[0:1, :], st)
    em.wait("sync", st, st.n)
    em.flush()
    embar = Em(P)
    barrier(P, embar)
    sub.close()

    em = Em(P)
    sub2 = _ES()
    def bsb(name, shape, dt):
        return sub2.enter_context(nc.sbuf_tensor(name, shape, dt))
    h2T = bsb(f"f1{sfx}_h2T", [128, 8, 512], BF16)
    w1b = [bsb(f"f1{sfx}_w1b{i}", [128, 8, 256], BF16) for i in range(2)]
    w2b = bsb(f"f1{sfx}_w2b", [128, 22, 1024], BF16)
    h3T = bsb(f"f1{sfx}_h3T", [128, 22, 512], BF16)
    gl1 = bsb(f"f1{sfx}_gl1", [128, 1024], F32)
    lgbc = bsb(f"f1{sfx}_lgbc", [128, 1024], F32)
    lbbc = bsb(f"f1{sfx}_lbbc", [128, 1024], F32)
    b2bc = bsb(f"f1{sfx}_b2bc", [128, 1024], F32)
    b1c = bsb(f"f1{sfx}_b1c", [128, 44], F32)
    yt2 = bsb(f"f1{sfx}_yt2", [128, 2, 1024], F32)
    ht = bsb(f"f1{sfx}_ht", [128, 2, 1024], F32)
    hnb = bsb(f"f1{sfx}_hnb", [128, 2, 1024], BF16)
    cols = bsb(f"f1{sfx}_cols", [128, 2, 8], F32)
    sq2 = bsb(f"f1{sfx}_sq2", [128, 1024], F32)
    onesr2 = bsb(f"f1{sfx}_on2", [1, 1024], F32)
    otile = bsb(f"f1{sfx}_ot", [128, 2, 512], F32)

    em.dma("gpsimd", b1c[:, :], P.aps[f"b1c_{l}"][:, :], gld)
    em.dma("gpsimd", w2b[:, :, :], P.aps[f"w2p_{l}"].rearrange("(a p) f -> p a f", p=128), gld)
    em.dma("gpsimd", onesr2[0:1, 0:128], P.aps["ones1x128"][:, :], gld)
    for name, dst in (("L1_ffn_ln_g", lgbc), ("L1_ffn_ln_b", lbbc), ("L1_ffn_b2", b2bc)):
        em.wait("gpsimd", vd, vd.n)
        em.dma("gpsimd", sq2[0:1, 0:1024],
               P.aps[name][l].rearrange("(o d) -> o d", o=1), gld)
        em.wait("tensor", gld, gld.n)
        for seg in range(2):
            em.wait("tensor", vd, vd.n)
            em.t(inc(lambda e, seg=seg: e.matmul(ps_bc[:, :], onesr2[0:1, 0:128],
                                                 sq2[0:1, seg * 512:(seg + 1) * 512],
                                                 start=True, stop=True), mm))
            em.wait("vector", mm, mm.n)
            em.v(inc(lambda e, dst=dst, seg=seg: e.tensor_copy(
                dst[:, seg * 512:(seg + 1) * 512], ps_bc[:, :]), vd))
        em.wait("gpsimd", vd, vd.n)
    if _K.FFN1_PH < 6:
        em.flush()
        sub2.close()
        return
    snaps = []
    for tt in range(NTT):
        b = tt % 2
        if tt >= 2:
            sn = snaps[tt - 2]
            em.wait("gpsimd", vd, sn[0])
            em.wait("vector", st, sn[1])
        em.dma("gpsimd", ht[:, b, 0:512], yaA[1 + tt * 128:1 + (tt + 1) * 128, 0:512], ld)
        em.dma("gpsimd", ht[:, b, 512:1024], yaA[tt * 128:(tt + 1) * 128, 512:1024], ld)
        em.wait("vector", ld, ld.n)
        em.v(inc(lambda e, b=b: e.reduce_sum(cols[:, b, 0:1], ht[:, b, :], axis=AX.X), vd))
        em.v(lambda e, b=b: e.tensor_tensor(sq2[:, :], ht[:, b, :], ht[:, b, :], ALU.mult))
        em.vbar()
        em.v(inc(lambda e, b=b: e.reduce_sum(cols[:, b, 1:2], sq2[:, :], axis=AX.X), vd))
        em.vbar()
        em.v(lambda e, b=b: e.tensor_scalar(cols[:, b, 2:3], cols[:, b, 0:1], 1.0 / D, None, ALU.mult))
        em.v(lambda e, b=b: e.tensor_scalar(cols[:, b, 3:4], cols[:, b, 1:2], 1.0 / D, None, ALU.mult))
        em.vbar()
        em.v(lambda e, b=b: e.tensor_tensor(cols[:, b, 4:5], cols[:, b, 2:3], cols[:, b, 2:3], ALU.mult))
        em.vbar()
        em.v(inc(lambda e, b=b: e.tensor_scalar(cols[:, b, 5:6], cols[:, b, 3:4], cols[:, b, 4:5],
                                                1e-5, ALU.subtract, ALU.add), vd))
        em.wait("scalar", vd, vd.n)
        em.s(inc(lambda e, b=b: e.activation(out=cols[:, b, 6:7], in_=cols[:, b, 5:6], func=AF.Sqrt), sd))
        em.wait("vector", sd, sd.n)
        em.v(lambda e, b=b: e.reciprocal(cols[:, b, 7:8], cols[:, b, 6:7]))
        em.vbar()
        em.v(lambda e, b=b: e.tensor_scalar(ht[:, b, :], ht[:, b, :], cols[:, b, 2:3],
                                            cols[:, b, 7:8], ALU.subtract, ALU.mult))
        em.vbar()
        em.v(lambda e, b=b: e.tensor_tensor(ht[:, b, :], ht[:, b, :], lgbc[:, :], ALU.mult))
        em.vbar()
        em.v(inc(lambda e, b=b: e.tensor_tensor(hnb[:, b, :], ht[:, b, :], lbbc[:, :], ALU.add), vd))
        em.wait("sync", vd, vd.n)
        em.dma("sync", h2A[tt * 128:(tt + 1) * 128, :], hnb[:, b, :], st)
        snaps.append((vd.n, st.n))
    em.wait("sync", st, st.n)
    WJ = [128] * 21 + [42]
    if _K.FFN1_PH < 7:
        em.wait("sync", st, st.n)
        em.flush()
        sub2.close()
        return
    ot_store = {0: 0, 1: 0}
    w1snap = {}
    for pan in range(4):
        p0 = pan * 512
        em.wait("sync", mm, mm.n)
        for dt_ in range(8):
            em.dmaT("sync", h2T[:, dt_, :], h2A[p0:p0 + 512, dt_ * 128:(dt_ + 1) * 128], tr)
        em.wait("tensor", tr, tr.n)
        for j in range(22):
            wj = WJ[j]
            buf = j % 2
            key = (pan, j - 2) if j >= 2 else (pan - 1, 20 + j)
            if key in w1snap:
                em.wait("gpsimd", mm, w1snap[key])
            em.dma("gpsimd", w1b[buf][:, :, 0:2 * wj],
                   P.aps[f"w1p_{l}"].rearrange("(a p) f -> p a f", p=128)[:, :, 256 * j:256 * j + 2 * wj], gld)
            em.wait("tensor", gld, gld.n)
            for ag in range(2):
                pu = ps_a if ag == 0 else ps_b
                em.wait("tensor", sd, sd.n)
                em.wait("tensor", vd, vd.n)
                def upm(e, j=j, wj=wj, buf=buf, ag=ag, pu=pu):
                    for kt in range(8):
                        r = e.matmul(pu[0:wj, :], w1b[buf][:, kt, ag * wj:(ag + 1) * wj],
                                     h2T[:, kt, :], start=(kt == 0), stop=(kt == 7))
                    return r
                em.t(inc(upm, mm))
                em.wait("scalar", mm, mm.n)
                if ag == 1:
                    em.s(inc(lambda e, j=j, wj=wj, pu=pu: e.activation(
                        out=h3T[0:wj, j, :], in_=pu[0:wj, :], func=AF.Gelu,
                        bias=b1c[0:wj, 2 * j + 1:2 * j + 2]), sd))
                else:
                    em.wait("vector", mm, mm.n)
                    em.v(inc(lambda e, j=j, wj=wj, pu=pu: e.tensor_scalar(
                        gl1[0:wj, 0:512], pu[0:wj, :], b1c[0:wj, 2 * j:2 * j + 1],
                        None, ALU.add), vd))
            w1snap[(pan, j)] = mm.n
            em.wait("vector", sd, sd.n)
            em.v(inc(lambda e, j=j, wj=wj: e.tensor_tensor(
                h3T[0:wj, j, :], gl1[0:wj, 0:512], h3T[0:wj, j, :], ALU.mult), vd))
            em.vbar()
        for t in range(4):
            tt = pan * 4 + t
            buf = t % 2
            em.wait("sync", vd, ot_store.get(("v", buf), 0))
            em.wait("sync", st, ot_store.get(("s", buf), 0))
            em.dma("sync", yt2[:, buf, :], y1fA[tt * 128:(tt + 1) * 128, :], ld)
            for fc in range(2):
                em.wait("tensor", sd, sd.n)
                em.wait("tensor", vd, vd.n)
                def dnm(e, t=t, fc=fc):
                    first = True
                    for j in range(22):
                        wj = WJ[j]
                        r = e.matmul(ps_o[:, :], h3T[0:wj, j, t * 128:(t + 1) * 128],
                                     w2b[0:wj, j, fc * 512:(fc + 1) * 512],
                                     start=first, stop=(j == 21))
                        first = False
                    return r
                em.t(inc(dnm, mm))
                em.wait("vector", mm, mm.n)
                em.wait("vector", ld, ld.n)
                em.wait("vector", st, ot_store.get(("s", buf), 0))
                em.v(lambda e, buf=buf, fc=fc: e.tensor_tensor(
                    sq2[:, 0:512], ps_o[:, :], b2bc[:, fc * 512:(fc + 1) * 512], ALU.add))
                em.vbar()
                em.v(inc(lambda e, buf=buf, fc=fc: e.tensor_tensor(
                    yt2[:, buf, fc * 512:(fc + 1) * 512], yt2[:, buf, fc * 512:(fc + 1) * 512],
                    sq2[:, 0:512], ALU.add), vd))
            ot_store[("v", buf)] = vd.n
            em.wait("sync", vd, vd.n)
            em.dma("sync", outA[tt * 128:(tt + 1) * 128, :], yt2[:, buf, :], st)
            if l == 0:
                em.vbar()
                em.v(inc(lambda e, buf=buf: e.tensor_copy(hnb[:, buf, :], yt2[:, buf, :]), vd))
                em.wait("sync", vd, vd.n)
                em.dma("sync", P.aps["y1b_0"][1 + tt * 128:1 + (tt + 1) * 128, :], hnb[:, buf, :], st)
            ot_store[("s", buf)] = st.n
    if l == 0:
        em.v(lambda e: e.memset(hnb[0:1, 0, :], 0.0))
        em.v(inc(lambda e: e.engine_nop(), vd))
        em.wait("sync", vd, vd.n)
        em.dma("sync", P.aps["y1b_0"][0:1, :], hnb[0:1, 0, :], st)
    em.wait("sync", st, st.n)
    em.flush()
    sub2.close()


def emit_ffn1a(P):
    emit_ffn1(P, 0)


def emit_ffn1b(P):
    emit_ffn1(P, 1)


STAGE_EMIT = {
    "ln0": emit_ln0,
    "att0": emit_att0,
    "att1": emit_att1,
    "ffn1a": emit_ffn1a,
    "ffn1b": emit_ffn1b,
    "ffn0": emit_ffn0,
    "comb0": emit_comb0,
    "comb1": emit_comb1,
}


def build_program(stages, weight_specs):
    nc = bass.Bass("TRN2", target_bir_lowering=False)
    produced = set()
    for s in stages:
        produced.update(STAGE_OUTS[s])
        produced.update(STAGE_INTERNAL.get(s, []))
    consumed = set()
    for s in stages:
        consumed.update(STAGE_INS[s])

    aps = {}
    ext_ins, ext_outs = [], []
    if "x" in consumed:
        aps["x"] = nc.dram_tensor("x", [NT, D], F32, kind="ExternalInput").ap()
        ext_ins.append("x")
    final_outs = set(FORCE_OUTS)
    for s in stages:
        for o in STAGE_OUTS[s] + STAGE_INTERNAL.get(s, []):
            consumers_outside = any(o in STAGE_INS[s2] for s2 in STAGE_INS if s2 not in stages)
            if o == "yout" or consumers_outside:
                final_outs.add(o)
    for name, (shape, dt) in DRAM_ACTS.items():
        if name in aps:
            continue
        used = (name in consumed) or (name in produced)
        if not used:
            continue
        if name in produced:
            kind = "ExternalOutput" if name in final_outs else "Internal"
        else:
            kind = "ExternalInput"
        aps[name] = nc.dram_tensor(name, list(shape), dt, kind=kind).ap()
        if kind == "ExternalInput":
            ext_ins.append(name)
        elif kind == "ExternalOutput":
            ext_outs.append(name)
    wnames = []
    for s in stages:
        for w in STAGE_WEIGHTS[s]:
            if w not in wnames:
                wnames.append(w)
    for w in wnames:
        shape, npdt = weight_specs[w]
        aps[w] = nc.dram_tensor(w, list(shape), F32, kind="ExternalInput").ap()
        ext_ins.append(w)

    with ExitStack() as stack:
        block = stack.enter_context(nc.Block())
        P = Prog(nc, block, stack, aps)
        for i, s in enumerate(stages):
            P.stage_stack = ExitStack()
            STAGE_EMIT[s](P)
            if i < len(stages) - 1:
                em = Em(P)
                barrier(P, em)
            P.stage_stack.close()
            P.stage_stack = None
    return nc, ext_ins, ext_outs


FORCE_OUTS = set()
LAUNCH_PLAN = [["ln0", "att0", "ffn0", "comb0", "att1", "ffn1a", "ffn1b", "comb1"]]


def _derived_weights(inputs):
    d = dict(inputs)
    d.update(_host_consts())
    d["l0_gn_pa"] = np.ascontiguousarray(
        np.asarray(inputs["L0_attn_gn_weight"], np.float32).reshape(4, 128).T)
    rw_ = np.asarray(inputs["L1_attn_rms_w"], np.float32)
    d["l1_rmsw_col"] = np.ascontiguousarray(np.concatenate([rw_, rw_])[:, None])
    wo_ = np.asarray(inputs["L1_attn_Wo"], np.float32).reshape(16, 64, 1024)
    d["l1_wo_pad"] = np.ascontiguousarray(
        np.concatenate([np.pad(wo_[h], ((0, 64), (0, 0)))[None] for h in range(16)], 0)
    ).reshape(2048, 1024)
    HD = 32
    inv = 1.0 / (10000.0 ** (np.arange(0, HD, 2, dtype=np.float64) / HD))
    t_ = np.arange(NT, dtype=np.float64)
    fr = np.outer(t_, inv)
    cos1 = np.cos(fr); sin1 = np.sin(fr)
    cosd = np.concatenate([cos1, cos1], 1)
    sind = np.concatenate([sin1, sin1], 1)
    sc = 32.0 ** -0.5
    d["l1_cosq"] = np.ascontiguousarray((np.tile(cosd, (1, 32)) * sc).astype(np.float32))
    d["l1_sinq"] = np.ascontiguousarray((np.tile(sind, (1, 32)) * sc).astype(np.float32))
    d["l1_cosk"] = np.ascontiguousarray(np.tile(cosd, (1, 16)).astype(np.float32))
    d["l1_sink"] = np.ascontiguousarray(np.tile(sind, (1, 16)).astype(np.float32))
    DH, WIN, QKS = 64, 512, 8.0
    half_idx = np.arange(0, DH, 2, dtype=np.float64)
    invf = 1.0 / (10000.0 ** (half_idx / DH))
    fr2 = np.outer(t_, invf)
    fr2 = np.concatenate([fr2, fr2], 1)
    based = (half_idx + 0.4 * DH) / (1.4 * DH)
    based2 = np.concatenate([based, based])
    powq = (t_ % 512) / 256.0
    scq = based2[None, :] ** powq[:, None]
    cq = np.cos(fr2) * scq * QKS
    sq_ = np.sin(fr2) * scq * QKS
    ck = np.cos(fr2) / scq
    sk = np.sin(fr2) / scq
    d["l1f_cosq"] = np.ascontiguousarray(np.tile(cq, (1, 8)).astype(np.float32))
    d["l1f_sinq"] = np.ascontiguousarray(np.tile(sq_, (1, 8)).astype(np.float32))
    d["l1f_cosk"] = np.ascontiguousarray(np.tile(ck, (1, 8)).astype(np.float32))
    d["l1f_sink"] = np.ascontiguousarray(np.tile(sk, (1, 8)).astype(np.float32))
    b2c_ = np.zeros((128, 1), np.float32)
    b2c_[0:64, 0] = (based2 ** 2.0)
    b2c_[64, 0] = 1.0
    d["l1f_base2"] = b2c_
    mc = np.zeros((128, 12), np.float32)
    pcol = np.arange(128, dtype=np.float32)
    for qs in range(4):
        mc[:, qs] = pcol + qs * 128
        mc[:, 4 + qs] = pcol + qs * 128 + 1
    mc[:, 8] = 512.0
    mc[:, 9] = 0.0
    d["l1f_mcols"] = mc
    cm = np.zeros((128, 4, 512), np.float32)
    for qs in range(4):
        ii = qs * 128 + np.arange(128)[:, None]
        jj = np.arange(512)[None, :]
        cm[:, qs, :] = np.where(jj <= ii, 0.0, -1.0e30)
    d["l1f_cmask"] = np.ascontiguousarray(cm.reshape(128, 2048))
    for l in range(2):
        d[f"l1f_qs_til_{l}"] = np.ascontiguousarray(
            np.tile(np.asarray(inputs["L1_ffn_q_scale"][l], np.float32), 8)[None, :])
        d[f"l1f_ks_til_{l}"] = np.ascontiguousarray(
            np.tile(np.asarray(inputs["L1_ffn_k_scale"][l], np.float32), 8)[None, :])
    HID = 2730
    for l in range(2):
        wo2 = np.asarray(inputs["L1_ffn_wo"][l], np.float32).reshape(8, 64, 1024)
        d[f"l1f_wo_pad_{l}"] = np.ascontiguousarray(
            np.concatenate([np.pad(wo2[h], ((0, 64), (0, 0)))[None] for h in range(8)], 0)
        ).reshape(1024, 1024)
        w2_ = np.asarray(inputs["L1_ffn_w2"][l], np.float32)
        w2p = np.zeros((22 * 128, 1024), np.float32)
        w2p[0:2730] = 0
        for j in range(22):
            wj = 128 if j < 21 else 42
            w2p[j * 128:j * 128 + wj] = w2_[j * 128:j * 128 + wj]
        d[f"w2p_{l}"] = np.ascontiguousarray(w2p)
        b1_ = np.asarray(inputs["L1_ffn_b1"][l], np.float32)
        b1c_ = np.zeros((128, 44), np.float32)
        for j in range(22):
            wj = 128 if j < 21 else 42
            b1c_[0:wj, 2 * j] = b1_[j * 128:j * 128 + wj]
            b1c_[0:wj, 2 * j + 1] = b1_[HID + j * 128:HID + j * 128 + wj]
        d[f"b1c_{l}"] = b1c_
    HID = 2730
    for l in range(2):
        w1 = np.asarray(inputs["L1_ffn_w1"][l])
        b1 = np.asarray(inputs["L1_ffn_b1"][l])
        cols = []
        for j in range(22):
            a0, a1 = j * 128, min((j + 1) * 128, HID)
            cols.append(np.arange(a0, a1))
            cols.append(np.arange(HID + a0, HID + a1))
        perm = np.concatenate(cols)
        d[f"w1p_{l}"] = np.ascontiguousarray(w1[:, perm])
        d[f"b1p_{l}"] = np.ascontiguousarray(b1[perm][None, :])
    return d


def kernel(**inputs):
    full = _derived_weights(inputs)
    x_full = np.asarray(inputs["L0_attn_x"], dtype=np.float32)

    weight_specs = {}
    for k, v in full.items():
        v = np.asarray(v)
        weight_specs[k] = (v.shape, v.dtype)

    outs = [dict() for _ in range(8)]
    for stages in LAUNCH_PLAN:
        nc, ext_ins, ext_outs = build_program(stages, weight_specs)
        in_maps = []
        for c in range(8):
            m = {}
            for name in ext_ins:
                if name == "x":
                    m["x"] = np.ascontiguousarray(x_full[c])
                elif name in DRAM_ACTS:
                    m[name] = outs[c][name]
                else:
                    m[name] = np.ascontiguousarray(np.asarray(full[name], dtype=np.float32))
            in_maps.append(m)
        res = run_bass_kernel_spmd(nc, in_maps, core_ids=list(range(8)))
        for c in range(8):
            for name in ext_outs:
                outs[c][name] = res.results[c][name]
    return np.stack([outs[c]["yout"] for c in range(8)], axis=0)
